# version 12
# baseline (speedup 1.0000x reference)
import os
import numpy as np
from contextlib import ExitStack

import concourse.bass as bass
import concourse.mybir as mybir
from concourse.bass_utils import run_bass_kernel_spmd

F32 = mybir.dt.float32
BF16 = mybir.dt.bfloat16
AF = mybir.ActivationFunctionType
ALU = mybir.AluOpType
AX = mybir.AxisListType

NCORES = 8
T = 4096
D = 1024
CH = 8
TT = 512
NTT = T // TT
NH = 16
DH = 64

ENGS = ("pe", "act", "dve", "pop", "sp")


class BB:

    def __init__(self, nc, stack):
        self.nc = nc
        self.stack = stack
        self.q = {e: [] for e in ENGS}
        self.sems = {}
        self.cnt = {}
        self.hi = {e: {} for e in ENGS}
        self.pend = {e: [] for e in ENGS}
        self.same_sync = bool(int(os.environ.get("BB_SAME_ENGINE_SYNC", "1")))
        for e in ENGS:
            self._mksem("g_" + e)

    def _mksem(self, name):
        if name not in self.sems:
            self.sems[name] = self.stack.enter_context(self.nc.semaphore(name))
            self.cnt[name] = 0
        return self.sems[name]

    def _waits(self, eng, deps):
        out = []
        for d in deps:
            if d is None:
                continue
            s, v = d
            if s == "g_" + eng and not self.same_sync:
                continue
            if self.hi[eng].get(s, 0) >= v:
                continue
            self.hi[eng][s] = v
            out.append((s, v))
        return out

    def op(self, eng, fn, deps=()):
        deps = list(deps) + self.pend[eng]
        self.pend[eng] = []
        waits = self._waits(eng, deps)
        s = "g_" + eng
        self.cnt[s] += 1
        tok = (s, self.cnt[s])
        self.q[eng].append((fn, waits, [(s, 1)]))
        return tok

    def dma(self, eng, out_ap, in_ap, sem, deps=(), **kw):
        deps = list(deps) + self.pend[eng]
        self.pend[eng] = []
        waits = self._waits(eng, deps)
        self._mksem(sem)
        self.cnt[sem] += 16
        tok = (sem, self.cnt[sem])

        def fn(E, out_ap=out_ap, in_ap=in_ap, kw=kw):
            return E.dma_start(out_ap, in_ap, **kw)

        self.q[eng].append((fn, waits, [(sem, 16)]))
        return tok

    def fence(self, toks=None):
        if toks is None:
            toks = [(s, v) for s, v in self.cnt.items() if v > 0]
        for e in ENGS:
            self.pend[e] = list({*self.pend[e], *toks})

    def finalize(self, final_toks):
        nc = self.nc
        waits = self._waits("sp", list(final_toks))
        if waits:
            self.q["sp"].append((None, waits, []))
        with nc.Block() as block:
            emap = dict(pe=block.tensor, act=block.scalar, dve=block.vector,
                        pop=block.gpsimd, sp=block.sync)
            for e in ENGS:
                items = self.q[e]
                if not items:
                    continue

                def body(E, items=items):
                    for fn, ws, incs in items:
                        for s, v in ws:
                            E.wait_ge(self.sems[s], v)
                        if fn is None:
                            continue
                        ins = fn(E)
                        for s, amt in incs:
                            ins = ins.then_inc(self.sems[s], amt)

                emap[e](body)



_NAME_N = [0]


def _u(n):
    _NAME_N[0] += 1
    return f"{n}_{_NAME_N[0]}"


def ts(i, n):
    return slice(i * n, (i + 1) * n)



def emit_consts(bb, ar):
    nc = bb.nc
    import ml_dtypes
    ident_np = np.eye(128, dtype=np.float32).astype(ml_dtypes.bfloat16)
    onesc_np = np.ones((128, 1), dtype=np.float32).astype(ml_dtypes.bfloat16)
    onesr_np = np.ones((1, 128), dtype=np.float32)
    cvals_np = np.array([[1e-5, 1e-30, 0.0, 0.0]], dtype=np.float32)
    t_cv = nc.inline_tensor(cvals_np, "c_cvals").ap()
    t_id = nc.inline_tensor(ident_np, "c_ident").ap()
    t_oc = nc.inline_tensor(onesc_np, "c_onesc").ap()
    t_or = nc.inline_tensor(onesr_np, "c_onesr").ap()
    E = ar["E"]
    ar["ident"] = E(nc.sbuf_tensor(_u("ident"), [128, 128], BF16))
    ar["onesc"] = E(nc.sbuf_tensor(_u("onesc"), [128, 1], BF16))
    ar["onesr"] = E(nc.sbuf_tensor(_u("onesr"), [1, 128], F32))
    ar["cvals"] = E(nc.sbuf_tensor(_u("cvals"), [1, 4], F32))
    ar["onescf"] = E(nc.sbuf_tensor(_u("onescf"), [128, 1], F32))
    t0 = bb.dma("sp", ar["cvals"][:], t_cv, "ld_const")
    tcf = nc.inline_tensor(np.ones((128, 1), dtype=np.float32), "c_onescf").ap()
    t4 = bb.dma("sp", ar["onescf"][:], tcf, "ld_const")
    t1 = bb.dma("sp", ar["ident"][:], t_id, "ld_const")
    t2 = bb.dma("sp", ar["onesc"][:], t_oc, "ld_const")
    t3 = bb.dma("sp", ar["onesr"][:], t_or, "ld_const")
    ar["const_tok"] = [t0, t1, t2, t3, t4]


def stat_rows_store(bb, ar, srow, stats_hbm, tok_slice, sem, deps, width):
    s1 = bb.dma("sp", stats_hbm[0:1, tok_slice], srow[:, 0:width], sem, deps=deps)
    s2 = bb.dma("sp", stats_hbm[1:2, tok_slice], srow[:, width:2 * width], sem,
                deps=deps)
    return s2


def emit_input_stats(bb, ar, x_hbm, stats_hbm):
    nc = bb.nc
    with ExitStack() as sc:
        Es = sc.enter_context
        xt = [Es(nc.sbuf_tensor(_u(f"ist_x{i}"), [128, CH, TT], F32)) for i in range(2)]
        xb = Es(nc.sbuf_tensor(_u("ist_xb"), [128, CH, TT], BF16))
        xq = Es(nc.sbuf_tensor(_u("ist_xq"), [128, CH, TT], BF16))
        srow = Es(nc.sbuf_tensor(_u("ist_srow"), [1, 2 * TT], F32))
        p_sx = Es(nc.psum_tensor(_u("ist_psx"), [1, TT], F32))
        p_sq = Es(nc.psum_tensor(_u("ist_psq"), [1, TT], F32))
        xv = x_hbm.rearrange("(c p) t -> p c t", p=128)
        war = [None, None]
        mm_last = None
        ev_last = None
        st_tok = None
        for tt in range(NTT):
            sl = tt % 2
            ld = bb.dma("sp", xt[sl][:], xv[:, :, ts(tt, TT)], f"ist_x{sl}",
                        deps=[war[sl]])
            c1 = bb.op("act", lambda E_, s=sl: E_.copy(xb[:], xt[s][:]),
                       deps=[ld, mm_last])
            c2 = bb.op("act", lambda E_, s=sl: E_.activation(
                xq[:], xt[s][:], AF.Square), deps=[ld, mm_last])
            war[sl] = c2
            mt = mq = None
            for c in range(CH):
                mt = bb.op("pe", lambda E_, c=c: E_.matmul(
                    p_sx[:], ar["onesc"][:], xb[:, c, :],
                    start=(c == 0), stop=(c == CH - 1)),
                    deps=[c1, ev_last] + (ar["const_tok"] if tt == 0 else []))
            for c in range(CH):
                mq = bb.op("pe", lambda E_, c=c: E_.matmul(
                    p_sq[:], ar["onesc"][:], xq[:, c, :],
                    start=(c == 0), stop=(c == CH - 1)), deps=[c2, ev_last])
            mm_last = mq
            e1 = bb.op("dve", lambda E_: E_.tensor_copy(srow[:, 0:TT], p_sx[:]),
                       deps=[mt, st_tok])
            e2 = bb.op("dve", lambda E_: E_.tensor_copy(srow[:, TT:2 * TT], p_sq[:]),
                       deps=[mq, st_tok])
            ev_last = e2
            st_tok = stat_rows_store(bb, ar, srow, stats_hbm, ts(tt, TT),
                                     "ist_st", [e1, e2], TT)
        bb.fence()


def emit_rows_prologue(bb, ar, sc, stats_in, want_rms_gamma=False):
    nc = bb.nc
    Es = sc.enter_context
    m_row = Es(nc.sbuf_tensor(_u("m_row"), [1, T], F32))
    s_row = Es(nc.sbuf_tensor(_u("scl_row"), [1, T], F32))
    with ExitStack() as tsc:
        Et = tsc.enter_context
        r0 = Et(nc.sbuf_tensor(_u("rtmp0"), [1, T], F32))
        r1 = Et(nc.sbuf_tensor(_u("rtmp1"), [1, T], F32))
        r2 = Et(nc.sbuf_tensor(_u("rtmp2"), [1, T], F32))
        l0 = bb.dma("sp", r0[:], stats_in[0:1, :], "ld_rows")
        l1 = bb.dma("sp", r1[:], stats_in[1:2, :], "ld_rows")
        t1 = bb.op("dve", lambda E_: E_.tensor_scalar(
            m_row[:], r0[:], 1.0 / D, None, ALU.mult), deps=[l0])
        t2 = bb.op("dve", lambda E_: E_.tensor_mul(r0[:], m_row[:], m_row[:]),
                   deps=[t1])
        t3 = bb.op("dve", lambda E_: E_.scalar_tensor_tensor(
            r1[:], r1[:], 1.0 / D, r0[:], ALU.mult, ALU.subtract), deps=[t2, l1])
        t4 = bb.op("act", lambda E_: E_.activation(
            r2[:], r1[:], AF.Ln, bias=ar["cvals"][0:1, 0:1]),
            deps=[t3] + ar["const_tok"])
        t5 = bb.op("act", lambda E_: E_.activation(r0[:], r2[:], AF.Exp, scale=-0.5),
                   deps=[t4])
        if not want_rms_gamma:
            done = bb.op("dve", lambda E_: E_.tensor_copy(s_row[:], r0[:]),
                         deps=[t5])
        else:
            u1 = bb.op("act", lambda E_: E_.activation(
                r2[:], r1[:], AF.Ln, bias=ar["cvals"][0:1, 1:2],
                scale=float(D)), deps=[t3, t4])
            u2 = bb.op("act", lambda E_: E_.activation(
                r1[:], r2[:], AF.Exp, scale=0.5), deps=[u1])
            u3 = bb.op("dve", lambda E_: E_.tensor_mul(r1[:], r1[:], r0[:]),
                       deps=[u2, t5])
            u4 = bb.op("dve", lambda E_: E_.tensor_scalar(
                r1[:], r1[:], 1e-12, None, ALU.max), deps=[u3])
            u5 = bb.op("dve", lambda E_: E_.reciprocal(r2[:], r1[:]), deps=[u4])
            done = bb.op("dve", lambda E_: E_.scalar_tensor_tensor(
                s_row[:], r0[:], 32.0, r2[:], ALU.mult, ALU.mult), deps=[u5])
        bb.fence()
    return m_row, s_row, done


def emit_bcast2(bb, ar, p_a, p_b_, row_a, row_b, tslice, deps, sb_a, sb_b, width):
    out = []
    for p_dst, row, sb in ((p_a, row_a, sb_a), (p_b_, row_b, sb_b)):
        m = None
        for j in range(width // 512):
            m = bb.op("pe", lambda E_, p=p_dst, r=row, j=j: E_.matmul(
                p[:, ts(j, 512)], ar["onesr"][:],
                r[0:1, tslice][:, ts(j, 512)], start=True, stop=True), deps=deps)
        e = bb.op("act", lambda E_, p=p_dst, sb=sb: E_.copy(sb[:], p[:]),
                  deps=[m])
        out.append(e)
    return out


def emit_out_stats(bb, ar, src_f32, width, p_sx, p_sq, first, last, deps,
                   srow, stats_hbm, tok_slice, st_sem, shared=False):
    nxb, nxq = ar["nxb"], ar["nxq"]
    w = ar.get("stat_war")
    c1 = bb.op("act", lambda E_: E_.copy(nxb[:, 0:width], src_f32),
               deps=list(deps) + [w])
    c2 = bb.op("act", lambda E_: E_.activation(nxq[:, 0:width], src_f32, AF.Square),
               deps=list(deps) + [w])
    mt = mq = None
    for j in range(width // 512):
        mt = bb.op("pe", lambda E_, j=j: E_.matmul(
            p_sx[:, ts(j, 512)], ar["onesc"][:], nxb[:, ts(j, 512)],
            start=first, stop=last), deps=[c1, ar.get("stat_war2")])
        mq = bb.op("pe", lambda E_, j=j: E_.matmul(
            p_sq[:, ts(j, 512)], ar["onesc"][:], nxq[:, ts(j, 512)],
            start=(first and not shared), stop=last,
            skip_group_check=shared), deps=[c2, ar.get("stat_war2")])
    ar["stat_war"] = mq
    ret = mq
    if last:
        e1 = bb.op("dve", lambda E_: E_.tensor_copy(srow[:, 0:width], p_sx[:]),
                   deps=[mt, ar.get("stat_war3")])
        e2 = bb.op("dve", lambda E_: E_.tensor_copy(
            srow[:, width:2 * width], p_sq[:]), deps=[mq, ar.get("stat_war3")])
        st = stat_rows_store(bb, ar, srow, stats_hbm, tok_slice, st_sem,
                             [e1, e2], width)
        ar["stat_war2"] = e2
        ar["stat_war3"] = st
        ret = e2
    return ret


def emit_stats_batch(bb, ar, src, twidth, p_sx, p_sq, deps,
                     srow, stats_hbm, tok_slice, st_sem, shared=False,
                     src_hbm=None):
    nxb, nxq = ar["nxb"], ar["nxq"]
    nbc = ar.get("nxb_ch", CH)
    nj = twidth // 512
    mt = mq = None
    for b0 in range(0, CH, nbc):
        cs = slice(b0, b0 + nbc)
        W = nbc * twidth
        if src_hbm is not None:
            sv = (src_hbm[:, b0, tok_slice] if nbc == 1
                  else src_hbm[:, cs, tok_slice])
            dv = (nxb[:, 0:W] if nbc == 1
                  else nxb[:, 0:W].rearrange("p (c t) -> p c t", c=nbc))
            c1 = bb.dma("pop", dv, sv, "sb_ld",
                        deps=list(deps) + [ar.get("stat_war")])
            c2 = bb.op("act", lambda E_, W=W: E_.activation(
                nxq[:, 0:W], nxb[:, 0:W], AF.Square),
                deps=[c1, ar.get("stat_war")])
        else:
            c1 = bb.op("act", lambda E_, cs=cs, W=W: E_.copy(
                nxb[:, 0:W], src[:, cs, :].rearrange("p c t -> p (c t)")),
                deps=list(deps) + [ar.get("stat_war")])
            c2 = bb.op("act", lambda E_, cs=cs, W=W: E_.activation(
                nxq[:, 0:W], src[:, cs, :].rearrange("p c t -> p (c t)"),
                AF.Square), deps=list(deps) + [ar.get("stat_war")])
        for ci in range(nbc):
            c = b0 + ci
            for j in range(nj):
                o0 = ci * twidth + j * 512
                mt = bb.op("pe", lambda E_, o0=o0, j=j, c=c: E_.matmul(
                    p_sx[:, ts(j, 512)], ar["onesc"][:],
                    nxb[:, o0:o0 + 512],
                    start=(c == 0), stop=(c == CH - 1)),
                    deps=[c1, ar.get("stat_war2")])
                mq = bb.op("pe", lambda E_, o0=o0, j=j, c=c: E_.matmul(
                    p_sq[:, ts(j, 512)], ar["onesc"][:],
                    nxq[:, o0:o0 + 512],
                    start=(c == 0 and not shared), stop=(c == CH - 1),
                    skip_group_check=shared),
                    deps=[c2, ar.get("stat_war2")])
        ar["stat_war"] = mq
    e1 = bb.op("dve", lambda E_: E_.tensor_copy(srow[:, 0:twidth], p_sx[:]),
               deps=[mt, ar.get("stat_war3")])
    e2 = bb.op("dve", lambda E_: E_.tensor_copy(
        srow[:, twidth:2 * twidth], p_sq[:]), deps=[mq, ar.get("stat_war3")])
    st = stat_rows_store(bb, ar, srow, stats_hbm, tok_slice, st_sem,
                         [e1, e2], twidth)
    ar["stat_war2"] = e2
    ar["stat_war3"] = st
    return e2


def emit_apply_norm(bb, ar, x_view, z_view, mb, sb, gcol_slice, deps):
    i = ar.get("tscr_i", 0)
    t = ar["tscr"] if i == 0 else ar["tscr2"]
    war = "tscr_war" if i == 0 else "tscr_war2"
    ar["tscr_i"] = 1 - i
    s1 = bb.op("pop", lambda E_: E_.tensor_sub(t[:], x_view, mb[:]),
               deps=list(deps) + [ar.get(war)])
    if gcol_slice is not None:
        s2 = bb.op("dve", lambda E_: E_.scalar_tensor_tensor(
            z_view, t[:], gcol_slice, sb[:], ALU.mult, ALU.mult), deps=[s1])
    else:
        s2 = bb.op("dve", lambda E_: E_.tensor_mul(z_view, t[:], sb[:]),
                   deps=[s1])
    ar[war] = s2
    return s2



def emit_ffn0(bb, ar, x_in, x_out, stats_in, stats_out, compute_stats_out):
    nc = bb.nc
    t_w = ar["t_L0_ffn_weight"]
    t_alpha = ar["t_L0_ffn_alpha"]
    t_bias = ar["t_L0_ffn_bias"]

    reuse = ar.pop("ffn0_reuse", False)
    with ExitStack() as sc:
        Es = sc.enter_context
        m_row, s_row, rows_done = emit_rows_prologue(bb, ar, sc, stats_in)
        wT = Es(nc.sbuf_tensor(_u("f0_wT"), [128, CH, D], BF16))
        acol = Es(nc.sbuf_tensor(_u("f0_acol"), [128, 1], F32))
        hbias = Es(nc.sbuf_tensor(_u("f0_hbias"), [128, CH], F32))
        if "dbg_wt" in ar:
            ar["dbg_dcol_t"] = Es(nc.sbuf_tensor(_u("dbg_dc"), [128, 4], F32))

        if reuse:
            lwt = bb.dma("sp", wT[:].rearrange("p c d -> p (c d)"),
                         ar["weff_hbm"], "f0_wre")
            lac = bb.dma("sp", acol[:], ar["wmeta_hbm"][:, 0:1], "f0_wre",
                         allow_slow_non_contiguous=True)
            tr_last = None
            b4 = lac
            bb.fence([lwt, lac])
        if not reuse:
            with ExitStack() as tsc:
                Et = tsc.enter_context
                wch = [Et(nc.sbuf_tensor(_u(f"f0_w{i}"), [128, D], F32)) for i in range(2)]
                absb = Et(nc.sbuf_tensor(_u("f0_absb"), [128, D], BF16))
                scr = Et(nc.sbuf_tensor(_u("f0_scr"), [128, D], F32))
                wtn = Et(nc.sbuf_tensor(_u("f0_wtn"), [128, D], BF16))
                wtn2 = Et(nc.sbuf_tensor(_u("f0_wtn2"), [128, D], BF16))
                drow = Et(nc.sbuf_tensor(_u("f0_drow"), [1, 4], F32))
                dcol = Et(nc.sbuf_tensor(_u("f0_dcol"), [128, 1], F32))
                a11 = Et(nc.sbuf_tensor(_u("f0_a11"), [1, 1], F32))
                p_sum = Et(nc.psum_tensor(_u("f0_psum"), [1, 512], F32))
                p_bc = Et(nc.psum_tensor(_u("f0_pbc"), [128, 1], F32))
                p_tr = Et(nc.psum_tensor(_u("f0_ptr"), [128, 128], BF16))
                wv = t_w.rearrange("(c p) d -> p c d", p=128)

                war = [None, None]
                mprev = None
                for oc in range(CH):
                    sl = oc % 2
                    ld = bb.dma("pop", wch[sl][:], wv[:, oc, :], f"f0_w{sl}",
                                deps=[war[sl]])
                    ab = bb.op("act", lambda E_, s=sl: E_.activation(
                        scr[:], wch[s][:], AF.Abs),
                        deps=[ld, mprev] + ar["const_tok"])
                    war[sl] = ab
                    for j in range(2):
                        mprev = bb.op("pe", lambda E_, j=j, oc=oc: E_.matmul(
                            p_sum[:], ar["onescf"][:], scr[:, ts(j, 512)],
                            start=(oc == 0 and j == 0),
                            stop=(oc == CH - 1 and j == 1)), deps=[ab])
                r1 = bb.op("dve", lambda E_: E_.tensor_reduce(
                    drow[0:1, 0:1], p_sum[:], AX.X, ALU.add), deps=[mprev])
                r2 = bb.op("dve", lambda E_: E_.tensor_scalar(
                    drow[0:1, 1:2], drow[0:1, 0:1], 0.7 / (D * D), None, ALU.mult),
                    deps=[r1])
                b1 = bb.op("pe", lambda E_: E_.matmul(
                    p_bc[:], ar["onesr"][:], drow[0:1, 1:2], start=True, stop=True),
                    deps=[r2])
                b2 = bb.op("dve", lambda E_: E_.tensor_copy(dcol[:], p_bc[:]),
                           deps=[b1])
                la = bb.dma("sp", a11[:], t_alpha.rearrange("(a o) -> a o", o=1),
                            "ld_rows")
                b3 = bb.op("pe", lambda E_: E_.matmul(
                    p_bc[:], ar["onesr"][:], a11[:], start=True, stop=True),
                    deps=[la, b2])
                b4 = bb.op("dve", lambda E_: E_.tensor_scalar(
                    acol[:], p_bc[:], 0.5, None, ALU.mult), deps=[b3])

                if "dbg_wt" in ar:
                    bb.op("dve", lambda E_: E_.tensor_copy(
                        ar["dbg_dcol_t"][:, 0:1], dcol[:]), deps=[b2])
                    bb.op("dve", lambda E_: E_.tensor_copy(
                        ar["dbg_dcol_t"][0:1, 1:4].rearrange("a b -> a b"),
                        drow[0:1, 0:3]), deps=[b2])
                war = [r1, r1]
                war_ptr = None
                tr_last = None
                prev_a2 = prev_a4 = prev_tr = None
                for oc in range(CH):
                    sl = oc % 2
                    ld = bb.dma("pop", wch[sl][:], wv[:, oc, :], f"f0_w{sl}",
                                deps=[war[sl]])
                    a1 = bb.op("act", lambda E_, s=sl: E_.activation(
                        scr[:], wch[s][:], AF.Abs), deps=[ld, b2, prev_a2])
                    a2 = bb.op("dve", lambda E_: E_.tensor_scalar(
                        absb[:], scr[:], dcol[:], None, ALU.is_gt),
                        deps=[a1, b2, prev_a4])
                    a3 = bb.op("act", lambda E_, s=sl: E_.sign(wtn[:], wch[s][:]),
                               deps=[a1, prev_tr])
                    war[sl] = a3
                    a4 = bb.op("dve", lambda E_: E_.tensor_mul(
                        wtn2[:], wtn[:], absb[:]), deps=[a3, a2, prev_tr])
                    prev_a2, prev_a4 = a2, a4
                    for dc in range(CH):
                        tr = bb.op("pe", lambda E_, dc=dc: E_.transpose(
                            p_tr[:], wtn2[:, ts(dc, 128)], ar["ident"][:]),
                            deps=[a4, war_ptr])
                        ev = bb.op("act", lambda E_, dc=dc, oc=oc: E_.copy(
                            wT[:, dc, ts(oc, 128)], p_tr[:]), deps=[tr])
                        war_ptr = ev
                        tr_last = ev
                    prev_tr = tr_last
                bb.fence()

            if "dbg_wt" in ar:
                wtf = Es(nc.sbuf_tensor(_u("dbg_wtf"), [128, CH * D], F32))
                ct = bb.op("dve", lambda E_: E_.tensor_copy(
                    wtf[:], wT[:].rearrange("p c d -> p (c d)")), deps=[tr_last])
                bb.dma("sp", ar["dbg_wt"], wtf[:], "dbg_s", deps=[ct])
                bb.dma("sp", ar["dbg_dcol"], ar["dbg_dcol_t"][:], "dbg_s",
                       deps=[tr_last])
            sw = bb.dma("sp", ar["weff_hbm"],
                        wT[:].rearrange("p c d -> p (c d)"), "f0_wsp",
                        deps=[tr_last])
            sa = bb.dma("sp", ar["wmeta_hbm"][:, 0:1], acol[:], "f0_wsp",
                        deps=[b4], allow_slow_non_contiguous=True)
        lb = bb.dma("sp", hbias[:], t_bias.rearrange("(c p) -> p c", p=128),
                    "ld_rows", allow_slow_non_contiguous=True)
        hb = bb.op("dve", lambda E_: E_.tensor_scalar(
            hbias[:], hbias[:], 0.5, None, ALU.mult), deps=[lb])

        xt = [Es(nc.sbuf_tensor(_u(f"f0_x{i}"), [128, CH, TT], F32)) for i in range(2)]
        zt = Es(nc.sbuf_tensor(_u("f0_z"), [128, CH, TT], BF16))
        nx = [Es(nc.sbuf_tensor(_u(f"f0_nx{i}"), [128, CH, TT], F32)) for i in range(2)]
        sbb = Es(nc.sbuf_tensor(_u("f0_sb"), [128, TT], F32))
        mbb = Es(nc.sbuf_tensor(_u("f0_mb"), [128, TT], F32))
        dlt2 = [Es(nc.sbuf_tensor(_u(f"f0_dlt{i}"), [128, TT], F32))
                for i in range(2)]
        ar["tscr"] = Es(nc.sbuf_tensor(_u("f0_tscr"), [128, TT], F32))
        ar["tscr2"] = Es(nc.sbuf_tensor(_u("f0_tscr2"), [128, TT], F32))
        ar["nxb"] = Es(nc.sbuf_tensor(_u("f0_nxb"), [128, CH * TT], BF16))
        ar["nxq"] = Es(nc.sbuf_tensor(_u("f0_nxq"), [128, CH * TT], BF16))
        ar["nxb_ch"] = CH
        srow = Es(nc.sbuf_tensor(_u("f0_srow"), [1, 2 * TT], F32))
        p_mm = [Es(nc.psum_tensor(_u(f"f0_pm{i}"), [128, 512], F32)) for i in range(2)]
        p_ba = Es(nc.psum_tensor(_u("f0_pba"), [128, 512], F32))
        p_bb2 = Es(nc.psum_tensor(_u("f0_pbb"), [128, 512], F32))
        p_sx = Es(nc.psum_tensor(_u("f0_psx"), [1, TT], F32))
        p_sq = Es(nc.psum_tensor(_u("f0_psq"), [1, TT], F32))

        xv = x_in.rearrange("(c p) t -> p c t", p=128)
        ov = x_out.rearrange("(c p) t -> p c t", p=128)
        war_x = [None, None]
        war_nx = [None, None]
        war_z = None
        war_bc = None
        war_mm = [None, None]
        war_dlt = None
        stats_job = None
        for tt in range(NTT):
            sl = tt % 2
            ld = bb.dma("sp", xt[sl][:], xv[:, :, ts(tt, TT)], f"f0_x{sl}",
                        deps=[war_x[sl]])
            bc = emit_bcast2(bb, ar, p_ba, p_bb2, s_row, m_row, ts(tt, TT),
                             [rows_done, war_bc], sbb, mbb, TT)
            zt_toks = []
            for c in range(CH):
                zt_toks.append(emit_apply_norm(
                    bb, ar, xt[sl][:, c, :], zt[:, c, :], mbb, sbb, None,
                    deps=[ld, war_z] + bc))
            war_bc = zt_toks[-1]
            ep_toks = []
            for oc in range(CH):
                psl = oc % 2
                mm = None
                for kc in range(CH):
                    mm = bb.op("pe", lambda E_, kc=kc, oc=oc, p=psl: E_.matmul(
                        p_mm[p][:], wT[:, kc, ts(oc, 128)], zt[:, kc, :],
                        start=(kc == 0), stop=(kc == CH - 1)),
                        deps=[zt_toks[kc], war_mm[psl]] +
                        ([tr_last] if tt == 0 and oc == 0 else []))
                dl_ = dlt2[oc % 2]
                e1 = bb.op("act", lambda E_, oc=oc, p=psl, dl_=dl_: E_.activation(
                    dl_[:], p_mm[p][:], AF.Identity, bias=hbias[:, oc:oc + 1],
                    scale=acol[:]), deps=[mm, hb, b4, war_dlt])
                e2 = bb.op("dve", lambda E_, oc=oc, s=sl, dl_=dl_: E_.tensor_add(
                    nx[s][:, oc, :], xt[s][:, oc, :], dl_[:]),
                    deps=[e1, war_nx[sl], ar.get("stat_war")])
                war_mm[psl] = e1
                war_dlt = ep_toks[-1] if ep_toks else None
                ep_toks.append(e2)
            war_x[sl] = ep_toks[-1]
            war_z = ep_toks[-1]
            st = bb.dma("sp", ov[:, :, ts(tt, TT)], nx[sl][:], "st_x",
                        deps=ep_toks)
            war_nx[sl] = st
            if compute_stats_out:
                if stats_job is not None:
                    emit_stats_batch(bb, ar, *stats_job)
                stats_job = (nx[sl][:], TT, p_sx, p_sq, [ep_toks[-1]],
                             srow, stats_out, ts(tt, TT), "f0_sst")
        if compute_stats_out and stats_job is not None:
            emit_stats_batch(bb, ar, *stats_job)
        bb.fence()



def emit_rwkv(bb, ar, x_in, x_out, stats_in, stats_out, compute_stats_out):
    nc = bb.nc
    t_ws = {"r": ar["t_L1_ffn_W_r"], "k": ar["t_L1_ffn_W_k"],
            "v": ar["t_L1_ffn_W_v"]}

    with ExitStack() as sc:
        Es = sc.enter_context
        m_row, s_row, rows_done = emit_rows_prologue(bb, ar, sc, stats_in)
        wT = {n: Es(nc.sbuf_tensor(_u(f"rw_wT{n}"), [128, CH, D], BF16))
              for n in ("r", "k", "v")}
        with ExitStack() as tsc:
            Et = tsc.enter_context
            wch = [Et(nc.sbuf_tensor(_u(f"rw_w{i}"), [128, D], BF16)) for i in range(2)]
            p_tr = Et(nc.psum_tensor(_u("rw_ptr"), [128, 128], BF16))
            war = [None, None]
            war_ptr = None
            tr_last = None
            for name in ("r", "k", "v"):
                wv_ = t_ws[name].rearrange("(c p) d -> p c d", p=128)
                for dc in range(CH):
                    sl = dc % 2
                    ld = bb.dma("pop", wch[sl][:], wv_[:, dc, :], f"rw_w{sl}",
                                deps=[war[sl]])
                    for cc in range(CH):
                        tr = bb.op("pe", lambda E_, cc=cc, s=sl: E_.transpose(
                            p_tr[:], wch[s][:, ts(cc, 128)], ar["ident"][:]),
                            deps=[ld, war_ptr] + ar["const_tok"])
                        ev = bb.op("act", lambda E_, cc=cc, dc=dc, n=name: E_.copy(
                            wT[n][:, cc, ts(dc, 128)], p_tr[:]), deps=[tr])
                        war_ptr = ev
                        tr_last = ev
                    war[sl] = tr_last
            bb.fence()

        mur = Es(nc.sbuf_tensor(_u("rw_mur"), [128, CH], F32))
        muk = Es(nc.sbuf_tensor(_u("rw_muk"), [128, CH], F32))
        lr = bb.dma("sp", mur[:], ar["t_L1_ffn_mu_r"].rearrange("(c p) -> p c", p=128),
                    "ld_rows", allow_slow_non_contiguous=True)
        lk = bb.dma("sp", muk[:], ar["t_L1_ffn_mu_k"].rearrange("(c p) -> p c", p=128),
                    "ld_rows", allow_slow_non_contiguous=True)

        xt = [Es(nc.sbuf_tensor(_u(f"rw_x{i}"), [128, CH, TT], F32)) for i in range(2)]
        xn = [Es(nc.sbuf_tensor(_u(f"rw_xn{i}"), [128, CH, TT + 1], F32))
              for i in range(2)]
        dt_ = Es(nc.sbuf_tensor(_u("rw_d"), [128, TT], F32))
        xr = Es(nc.sbuf_tensor(_u("rw_xr"), [128, CH, TT], BF16))
        xk = Es(nc.sbuf_tensor(_u("rw_xk"), [128, CH, TT], BF16))

        k2 = Es(nc.sbuf_tensor(_u("rw_k2"), [128, CH, TT], BF16))

        rt = Es(nc.sbuf_tensor(_u("rw_r"), [128, TT], F32))
        rl = Es(nc.sbuf_tensor(_u("rw_rl"), [128, TT], F32))
        dlt = Es(nc.sbuf_tensor(_u("rw_dlt"), [128, TT], F32))
        nx = Es(nc.sbuf_tensor(_u("rw_nx"), [128, CH, TT], F32))
        sbb = Es(nc.sbuf_tensor(_u("rw_sb"), [128, TT], F32))
        mbb = Es(nc.sbuf_tensor(_u("rw_mb"), [128, TT], F32))

        ar["tscr"] = Es(nc.sbuf_tensor(_u("rw_tscr"), [128, TT], F32))
        ar["tscr2"] = Es(nc.sbuf_tensor(_u("rw_tscr2"), [128, TT], F32))
        ar["nxb"] = Es(nc.sbuf_tensor(_u("rw_nxb"), [128, TT], BF16))
        ar["nxq"] = Es(nc.sbuf_tensor(_u("rw_nxq"), [128, TT], BF16))
        ar["nxb_ch"] = 1
        srow = Es(nc.sbuf_tensor(_u("rw_srow"), [1, 2 * TT], F32))
        p_mm = [Es(nc.psum_tensor(_u(f"rw_pm{i}"), [128, 512], F32)) for i in range(2)]
        p_ba = Es(nc.psum_tensor(_u("rw_pba"), [128, 512], F32))
        p_bb2 = Es(nc.psum_tensor(_u("rw_pbb"), [128, 512], F32))
        p_sx = Es(nc.psum_tensor(_u("rw_psx"), [1, TT], F32))
        p_sq = Es(nc.psum_tensor(_u("rw_psq"), [1, TT], F32))

        hz = bb.op("dve", lambda E_: E_.memset(xn[0][:, :, 0:1], 0.0))

        xv = x_in.rearrange("(c p) t -> p c t", p=128)
        ov = x_out.rearrange("(c p) t -> p c t", p=128)
        war_x = [None, None]
        war_nx = None
        war_xn = [hz, None]
        stats_job = None
        war_rl = None
        war_mm = [None, None]
        war_bc = None
        war_dlt = None
        halo_tok = hz
        for tt in range(NTT):
            sl = tt % 2
            nsl = (tt + 1) % 2
            ld = bb.dma("sp", xt[sl][:], xv[:, :, ts(tt, TT)], f"rw_x{sl}",
                        deps=[war_x[sl]])
            bc = emit_bcast2(bb, ar, p_ba, p_bb2, s_row, m_row, ts(tt, TT),
                             [rows_done, war_bc], sbb, mbb, TT)
            xn_toks = []
            for c in range(CH):
                i_ = ar.get("tscr_i", 0)
                t_ = ar["tscr"] if i_ == 0 else ar["tscr2"]
                wkey = "tscr_war" if i_ == 0 else "tscr_war2"
                ar["tscr_i"] = 1 - i_
                s1 = bb.op("pop", lambda E_, c=c, s=sl, t_=t_: E_.tensor_sub(
                    t_[:], xt[s][:, c, :], mbb[:]),
                    deps=[ld, war_xn[sl], ar.get(wkey)] + bc)
                s2 = bb.op("dve", lambda E_, c=c, s=sl, t_=t_: E_.tensor_mul(
                    xn[s][:, c, 1:TT + 1], t_[:], sbb[:]), deps=[s1])
                ar[wkey] = s2
                xn_toks.append(s2)
            war_bc = xn_toks[-1]
            if tt < NTT - 1:
                halo_new = bb.op("dve", lambda E_, s=sl, nn=nsl: E_.tensor_copy(
                    xn[nn][:, :, 0:1], xn[s][:, :, TT:TT + 1]),
                    deps=xn_toks + [war_xn[nsl]])

            mix_r, mix_k = [], []
            war_dt = None
            for c in range(CH):
                d1 = bb.op("dve", lambda E_, c=c, s=sl: E_.tensor_sub(
                    dt_[:], xn[s][:, c, 1:TT + 1], xn[s][:, c, 0:TT]),
                    deps=[xn_toks[c], halo_tok] +
                    (war_dt if isinstance(war_dt, list) else [war_dt]))
                r1 = bb.op("dve", lambda E_, c=c, s=sl: E_.scalar_tensor_tensor(
                    xr[:, c, :], dt_[:], mur[:, c:c + 1],
                    xn[s][:, c, 0:TT], ALU.mult, ALU.add), deps=[d1, lr])
                k1 = bb.op("dve", lambda E_, c=c, s=sl: E_.scalar_tensor_tensor(
                    xk[:, c, :], dt_[:], muk[:, c:c + 1],
                    xn[s][:, c, 0:TT], ALU.mult, ALU.add), deps=[d1, lk])
                war_dt = [k1, r1]
                mix_r.append(r1)
                mix_k.append(k1)
            if tt < NTT - 1:
                halo_tok = halo_new
            war_xn[sl] = mix_k[-1]

            k_toks = []
            for oc in range(CH):
                psl = oc % 2
                mm = None
                for kc in range(CH):
                    mm = bb.op("pe", lambda E_, kc=kc, oc=oc, p=psl: E_.matmul(
                        p_mm[p][:], wT["k"][:, kc, ts(oc, 128)], xk[:, kc, :],
                        start=(kc == 0), stop=(kc == CH - 1)),
                        deps=[mix_k[kc], war_mm[psl]] +
                        ([tr_last] if tt == 0 and oc == 0 else []))
                e1 = bb.op("act", lambda E_, p=psl: E_.activation(
                    rl[:], p_mm[p][:], AF.Relu), deps=[mm, war_rl])
                e2 = bb.op("act", lambda E_, oc=oc: E_.activation(
                    k2[:, oc, :], rl[:], AF.Square), deps=[e1])
                war_mm[psl] = e2
                war_rl = e2
                k_toks.append(e2)
            if compute_stats_out and stats_job is not None:
                emit_stats_batch(bb, ar, *stats_job, src_hbm=ov)
                stats_job = None
            ep_toks = []
            war_rt = None
            for oc in range(CH):
                psl = oc % 2
                mm = None
                for kc in range(CH):
                    mm = bb.op("pe", lambda E_, kc=kc, oc=oc, p=psl: E_.matmul(
                        p_mm[p][:], wT["r"][:, kc, ts(oc, 128)], xr[:, kc, :],
                        start=(kc == 0), stop=(kc == CH - 1)),
                        deps=[mix_r[kc], war_mm[psl]])
                sg = bb.op("act", lambda E_, p=psl: E_.activation(
                    rt[:], p_mm[p][:], AF.Sigmoid), deps=[mm, war_rt])
                war_mm[psl] = sg
                mm = None
                for kc in range(CH):
                    mm = bb.op("pe", lambda E_, kc=kc, oc=oc, p=psl: E_.matmul(
                        p_mm[p][:], wT["v"][:, kc, ts(oc, 128)], k2[:, kc, :],
                        start=(kc == 0), stop=(kc == CH - 1)),
                        deps=[k_toks[kc], war_mm[psl]])
                e1 = bb.op("dve", lambda E_, oc=oc, p=psl: E_.scalar_tensor_tensor(
                    dlt[:], rt[:], 0.5, p_mm[p][:], ALU.mult,
                    ALU.mult), deps=[mm, sg, war_dlt])
                war_rt = e1
                e2 = bb.op("pop", lambda E_, oc=oc, s=sl: E_.tensor_add(
                    nx[:, oc, :], xt[s][:, oc, :], dlt[:]),
                    deps=[e1, war_nx])
                war_mm[psl] = e1
                war_dlt = e2
                ep_toks.append(e2)
            war_x[sl] = ep_toks[-1]
            st = bb.dma("sp", ov[:, :, ts(tt, TT)], nx[:], "st_x",
                        deps=ep_toks)
            war_nx = st
            if compute_stats_out:
                stats_job = (None, TT, p_sx, p_sq, [st], srow, stats_out,
                             ts(tt, TT), "rw_sst")
        if compute_stats_out and stats_job is not None:
            emit_stats_batch(bb, ar, *stats_job, src_hbm=ov)
        bb.fence()



def emit_attn0(bb, ar, x_in, x_out, stats_in, stats_out, compute_stats_out):
    nc = bb.nc
    q_hbm, k_hbm = ar["q_hbm"], ar["k_hbm"]
    QT = 1024
    NQT = T // QT

    with ExitStack() as sc:
        Es = sc.enter_context
        vplus = Es(nc.sbuf_tensor(_u("a0_vplus"), [128, 32, NH, 66], BF16))
        m_row, s_row, rows_done = emit_rows_prologue(bb, ar, sc, stats_in,
                                                     want_rms_gamma=True)
        gcol = Es(nc.sbuf_tensor(_u("a0_gcol"), [128, CH], F32))
        lg = bb.dma("sp", gcol[:],
                    ar["t_L0_attn_gamma"].rearrange("(c p) -> p c", p=128),
                    "ld_rows", allow_slow_non_contiguous=True)
        lg2 = bb.op("dve", lambda E_: E_.tensor_scalar(
            gcol[:], gcol[:], 1.0, None, ALU.add), deps=[lg])
        vset = bb.op("dve", lambda E_: E_.memset(vplus[:], 1.0))
        negCall = Es(nc.sbuf_tensor(_u("a0_negCall"), [128, CH], F32))

        with ExitStack() as p1:
            Ep = p1.enter_context
            wq = Ep(nc.sbuf_tensor(_u("a0_wq"), [128, CH, 3 * D], BF16))
            lw = []
            wv_ = ar["t_L0_attn_w_qkv"].rearrange("(c p) n -> p c n", p=128)
            for kc in range(CH):
                lw.append(bb.dma("pop", wq[:, kc, :], wv_[:, kc, :], "a0_wld"))
            xt = [Ep(nc.sbuf_tensor(_u(f"a0_x{i}"), [128, CH, TT], F32))
                  for i in range(2)]
            zt = Ep(nc.sbuf_tensor(_u("a0_z"), [128, CH, TT], BF16))
            stg = [Ep(nc.sbuf_tensor(_u(f"a0_stg{i}"), [128, TT], BF16))
                   for i in range(2)]
            sbb = Ep(nc.sbuf_tensor(_u("a0_sb"), [128, TT], F32))
            mbb = Ep(nc.sbuf_tensor(_u("a0_mb"), [128, TT], F32))
            ar["tscr"] = Ep(nc.sbuf_tensor(_u("a0_tscr"), [128, TT], F32))
            ar["tscr2"] = Ep(nc.sbuf_tensor(_u("a0_tscr2"), [128, TT], F32))
            p_mm = [Ep(nc.psum_tensor(_u(f"a0_pm{i}"), [128, 512], F32))
                    for i in range(2)]
            p_v = Ep(nc.psum_tensor(_u("a0_pv"), [128, 1024], F32))
            p_ba = Ep(nc.psum_tensor(_u("a0_pba"), [128, 512], F32))
            p_bb2 = Ep(nc.psum_tensor(_u("a0_pbb"), [128, 512], F32))
            p_st = Ep(nc.psum_tensor(_u("a0_pst"), [128, 512], F32))
            sqsc = Ep(nc.sbuf_tensor(_u("a0_sqsc"), [128, TT], BF16))
            qkmax = Ep(nc.sbuf_tensor(_u("a0_qkmax"), [1, 64], F32))
            kbarf = Ep(nc.sbuf_tensor(_u("a0_kbarf"), [128, CH], F32))
            kbred = Ep(nc.sbuf_tensor(_u("a0_kbred"), [128, 1], F32))
            kb2c = Ep(nc.sbuf_tensor(_u("a0_kb2"), [128, CH], BF16))
            qkz = bb.op("dve", lambda E_: E_.memset(qkmax[:], 0.0))
            kbz = bb.op("dve", lambda E_: E_.memset(kbarf[:], 0.0))

            xv = x_in.rearrange("(c p) t -> p c t", p=128)
            war_x = [None, None]
            war_z = None
            war_bc = None
            war_mm = [None, None]
            war_stg = [None, None]
            war_pv = None
            for tt in range(NTT):
                sl = tt % 2
                ld = bb.dma("sp", xt[sl][:], xv[:, :, ts(tt, TT)], f"a0_x{sl}",
                            deps=[war_x[sl]])
                bc = emit_bcast2(bb, ar, p_ba, p_bb2, s_row, m_row, ts(tt, TT),
                                 [rows_done, war_bc], sbb, mbb, TT)
                zt_toks = []
                for c in range(CH):
                    zt_toks.append(emit_apply_norm(
                        bb, ar, xt[sl][:, c, :], zt[:, c, :], mbb, sbb,
                        gcol[:, c:c + 1], deps=[ld, war_z, lg2] + bc))
                war_bc = zt_toks[-1]
                war_x[sl] = zt_toks[-1]
                for oc in range(16):
                    psl = oc % 2
                    mm = None
                    for kc in range(CH):
                        mm = bb.op("pe", lambda E_, kc=kc, oc=oc, p=psl:
                                   E_.matmul(
                            p_mm[p][:], wq[:, kc, ts(oc, 128)], zt[:, kc, :],
                            start=(kc == 0), stop=(kc == CH - 1)),
                            deps=[zt_toks[kc], lw[kc], war_mm[psl]])
                    scale = 0.125 if oc < 8 else 1.0
                    ev = bb.op("act", lambda E_, p=psl, s=psl, sc_=scale:
                               E_.activation(stg[s][:], p_mm[p][:], AF.Copy,
                                             scale=sc_),
                               deps=[mm, war_stg[psl]])
                    war_mm[psl] = ev
                    dst = q_hbm if oc < 8 else k_hbm
                    st = bb.dma("sp", dst[oc % 8, :, ts(tt, TT)], stg[psl][:],
                                "a0_qkst", deps=[ev])
                    war_stg[psl] = st
                    sq = bb.op("dve", lambda E_, s=psl: E_.tensor_mul(
                        sqsc[:], stg[s][:], stg[s][:]),
                        deps=[ev, ar.get("a0_sq_war")])
                    msq = bb.op("pe", lambda E_: E_.matmul(
                        p_st[0:1, :], ar["onesc"][:], sqsc[:],
                        start=True, stop=True),
                        deps=[sq, ar.get("a0_pst_war")])
                    rmx = bb.op("dve", lambda E_, oc=oc: E_.tensor_reduce(
                        qkmax[0:1, 32 + oc % 2:33 + oc % 2], p_st[0:1, :],
                        AX.X, ALU.max), deps=[msq, qkz])
                    upd = bb.op("dve", lambda E_, oc=oc: E_.tensor_tensor(
                        qkmax[0:1, oc:oc + 1], qkmax[0:1, oc:oc + 1],
                        qkmax[0:1, 32 + oc % 2:33 + oc % 2], ALU.max),
                        deps=[rmx])
                    ar["a0_sq_war"] = msq
                    ar["a0_pst_war"] = rmx
                    stat_last = upd
                    if oc >= 8:
                        kr = bb.op("dve", lambda E_, s=psl: E_.tensor_reduce(
                            kbred[:], stg[s][:], AX.X, ALU.add),
                            deps=[ev, ar.get("a0_kbr_war")])
                        ka = bb.op("dve", lambda E_, oc=oc: E_.tensor_add(
                            kbarf[:, oc - 8:oc - 7], kbarf[:, oc - 8:oc - 7],
                            kbred[:]), deps=[kr, kbz])
                        ar["a0_kbr_war"] = ka
                        stat_last = ka
                for tsub in range(TT // 128):
                    ktg = tt * 4 + tsub
                    mm = None
                    for kc in range(CH):
                        for j in range(2):
                            mm = bb.op("pe", lambda E_, kc=kc, j=j, t0=tsub:
                                       E_.matmul(
                                p_v[:, ts(j, 512)], zt[:, kc, ts(t0, 128)],
                                wq[:, kc, 2048 + j * 512:2048 + (j + 1) * 512],
                                start=(kc == 0), stop=(kc == CH - 1)),
                                deps=[zt_toks[kc], war_pv])
                    ev = bb.op("act", lambda E_, k=ktg: E_.copy(
                        vplus[:, k, :, 0:64],
                        p_v[:].rearrange("p (h d) -> p h d", h=NH)),
                        deps=[mm, vset])
                    war_pv = ev
                    war_z = ev
            kq = bb.op("dve", lambda E_: E_.tensor_mul(
                kb2c[:], kbarf[:], kbarf[:]),
                deps=[ar.get("a0_kbr_war"), kbz])
            kbn_ev = []
            for hh in range(2):
                mmk = bb.op("pe", lambda E_, hh=hh: E_.matmul(
                    p_st[0:1, 0:CH], ar["onesc"][64 * hh:64 * hh + 64, :],
                    kb2c[64 * hh:64 * hh + 64, :], start=True, stop=True),
                    deps=[kq, ar.get("a0_pst_war"), stat_last] + kbn_ev)
                evk = bb.op("dve", lambda E_, hh=hh: E_.tensor_copy(
                    qkmax[0:1, 40 + 8 * hh:48 + 8 * hh], p_st[0:1, 0:CH]),
                    deps=[mmk])
                kbn_ev.append(evk)
            cdone = []
            war_nc = None
            for p in range(CH):
                t1 = bb.op("dve", lambda E_, p=p: E_.tensor_tensor(
                    qkmax[0:1, 56:57], qkmax[0:1, 2 * p:2 * p + 1],
                    qkmax[0:1, 2 * p + 1:2 * p + 2], ALU.max),
                    deps=[stat_last] + cdone + kbn_ev)
                t2 = bb.op("dve", lambda E_, p=p: E_.tensor_tensor(
                    qkmax[0:1, 57:58], qkmax[0:1, 8 + 2 * p:9 + 2 * p],
                    qkmax[0:1, 9 + 2 * p:10 + 2 * p], ALU.max), deps=[t1])
                t3 = bb.op("dve", lambda E_, p=p: E_.tensor_tensor(
                    qkmax[0:1, 58:59], qkmax[0:1, 40 + p:41 + p],
                    qkmax[0:1, 48 + p:49 + p], ALU.max), deps=[t2])
                l1 = bb.op("act", lambda E_: E_.activation(
                    qkmax[0:1, 59:60], qkmax[0:1, 56:57], AF.Ln,
                    bias=ar["cvals"][0:1, 1:2]), deps=[t3])
                l2 = bb.op("act", lambda E_: E_.activation(
                    qkmax[0:1, 60:61], qkmax[0:1, 57:58], AF.Ln,
                    bias=ar["cvals"][0:1, 1:2]), deps=[l1])
                l3 = bb.op("act", lambda E_: E_.activation(
                    qkmax[0:1, 61:62], qkmax[0:1, 58:59], AF.Ln,
                    bias=ar["cvals"][0:1, 1:2]), deps=[l2])
                a1 = bb.op("dve", lambda E_: E_.tensor_add(
                    qkmax[0:1, 62:63], qkmax[0:1, 59:60], qkmax[0:1, 60:61]),
                    deps=[l2, l3])
                a2 = bb.op("act", lambda E_: E_.activation(
                    qkmax[0:1, 62:63], qkmax[0:1, 62:63], AF.Exp, scale=0.5),
                    deps=[a1])
                b1 = bb.op("dve", lambda E_: E_.tensor_add(
                    qkmax[0:1, 63:64], qkmax[0:1, 59:60], qkmax[0:1, 61:62]),
                    deps=[a2, l3])
                b2 = bb.op("act", lambda E_: E_.activation(
                    qkmax[0:1, 63:64], qkmax[0:1, 63:64], AF.Exp, scale=0.5),
                    deps=[b1])
                b3 = bb.op("dve", lambda E_: E_.tensor_scalar(
                    qkmax[0:1, 63:64], qkmax[0:1, 63:64], -1.0 / T, None,
                    ALU.mult), deps=[b2])
                c1 = bb.op("dve", lambda E_: E_.tensor_scalar(
                    qkmax[0:1, 63:64], qkmax[0:1, 63:64], 60.0, None,
                    ALU.add), deps=[b3])
                c2 = bb.op("dve", lambda E_: E_.tensor_tensor(
                    qkmax[0:1, 63:64], qkmax[0:1, 62:63], qkmax[0:1, 63:64],
                    ALU.min), deps=[c1])
                c3 = bb.op("dve", lambda E_: E_.tensor_scalar(
                    qkmax[0:1, 62:63], qkmax[0:1, 62:63], -70.0, None,
                    ALU.add), deps=[c2])
                c4 = bb.op("dve", lambda E_: E_.tensor_tensor(
                    qkmax[0:1, 63:64], qkmax[0:1, 63:64], qkmax[0:1, 62:63],
                    ALU.max), deps=[c3])
                c5 = bb.op("dve", lambda E_: E_.tensor_scalar(
                    qkmax[0:1, 63:64], qkmax[0:1, 63:64], -1.0, None,
                    ALU.mult), deps=[c4])
                bcm = bb.op("pe", lambda E_, p=p: E_.matmul(
                    p_st[:, 0:1], ar["onesr"][:], qkmax[0:1, 63:64],
                    start=True, stop=True), deps=[c5, war_nc, kbn_ev[-1]])
                ncv = bb.op("dve", lambda E_, p=p: E_.tensor_copy(
                    negCall[:, p:p + 1], p_st[:, 0:1]), deps=[bcm])
                war_nc = ncv
                cdone.append(ncv)
            ar["a0_c_done"] = cdone
            bb.fence()

        QW = 512
        NQW = T // QW
        NSLOT = 3
        qbuf = Es(nc.sbuf_tensor(_u("a0_qb"), [128, CH, QW], BF16))
        kbuf = [Es(nc.sbuf_tensor(_u(f"a0_kb{i}"), [128, T], BF16))
                for i in range(2)]
        ptil = [Es(nc.sbuf_tensor(_u(f"a0_pt{i}"), [128, 1024], BF16))
                for i in range(NSLOT)]
        attnT = Es(nc.sbuf_tensor(_u("a0_attnT"), [128, CH, QW], BF16))
        numsA = Es(nc.sbuf_tensor(_u("a0_numsA"), [64, QW], F32))
        numsB = Es(nc.sbuf_tensor(_u("a0_numsB"), [64, QW], F32))
        rdenA = Es(nc.sbuf_tensor(_u("a0_rdenA"), [1, QW], F32))
        rdenB = Es(nc.sbuf_tensor(_u("a0_rdenB"), [1, QW], F32))
        wo = Es(nc.sbuf_tensor(_u("a0_wo"), [128, CH, D], BF16))
        xsl = [Es(nc.sbuf_tensor(_u(f"a0_xs{i}"), [128, QW], F32))
               for i in range(2)]
        nxs = Es(nc.sbuf_tensor(_u("a0_nxs"), [128, QW], F32))
        ar["nxb"] = Es(nc.sbuf_tensor(_u("a0_nxb"), [128, QW], BF16))
        ar["nxq"] = Es(nc.sbuf_tensor(_u("a0_nxq"), [128, QW], BF16))
        srow = Es(nc.sbuf_tensor(_u("a0_srow"), [1, 2 * QW], F32))
        p_S = [Es(nc.psum_tensor(_u(f"a0_pS{i}"), [128, 1024], F32))
               for i in range(NSLOT)]
        p_oA = Es(nc.psum_tensor(_u("a0_poA"), [65, 512], F32))
        p_oB = Es(nc.psum_tensor(_u("a0_poB"), [65, 512], F32))

        wov = ar["t_L0_attn_w_out"].rearrange("(c p) n -> p c n", p=128)
        lwo = [bb.dma("pop", wo[:, ic, :], wov[:, ic, :], "a0_wld")
               for ic in range(CH)]

        war_q = None
        war_k = [None, None]
        war_attnT = None
        war_xr = [None, None]
        war_nxs = None
        war_oA = None
        war_oB = None
        exp_toks = [None] * NSLOT
        av_toks = [None] * NSLOT
        qv = q_hbm.rearrange("c p t -> p c t")
        xrv = x_in.rearrange("(c p) t -> p c t", p=128)
        ov = x_out.rearrange("(c p) t -> p c t", p=128)
        cdone = ar["a0_c_done"]

        NP = NH // 2
        lk_tok = [None] * NP
        for qt in range(NQW):
            lq = bb.dma("sp", qbuf[:], qv[:, :, ts(qt, QW)], "a0_q0",
                        deps=[war_q])
            if qt == 0:
                lk_tok[0] = bb.dma("sp", kbuf[0][:], k_hbm[0, :, :], "a0_k0",
                                   deps=[war_k[0]])
            for p in range(NP):
                psl = p % 2
                qhA = qbuf[0:64, p, :]
                qhB = qbuf[64:128, p, :]
                lk = lk_tok[p]
                ncc = cdone[p]
                if p + 1 < NP:
                    npsl = (p + 1) % 2
                    lk_tok[p + 1] = bb.dma(
                        "sp", kbuf[npsl][:], k_hbm[p + 1, :, :],
                        f"a0_k{npsl}", deps=[war_k[npsl]])
                elif qt + 1 < NQW:
                    lk_tok[0] = bb.dma("sp", kbuf[0][:], k_hbm[0, :, :],
                                       "a0_k0", deps=[war_k[0]])
                pend = []
                last_avA = last_avB = None
                for kt in range(34):
                    if kt < 32:
                        Ssl = kt % NSLOT
                        mmA = bb.op("pe", lambda E_, kt=kt, s=Ssl, ks=psl,
                                    qh=qhA: E_.matmul(
                            p_S[s][:, 0:512], kbuf[ks][0:64, ts(kt, 128)],
                            qh[:], start=True, stop=True,
                            tile_position=(0, 0)),
                            deps=[lk, lq, exp_toks[Ssl]])
                        mmB = bb.op("pe", lambda E_, kt=kt, s=Ssl, ks=psl,
                                    qh=qhB: E_.matmul(
                            p_S[s][:, 512:1024], kbuf[ks][64:128, ts(kt, 128)],
                            qh[:], start=True, stop=True,
                            tile_position=(64, 0)),
                            deps=[lk])
                        ex = bb.op("act", lambda E_, s=Ssl, p=p: E_.activation(
                            ptil[s][:], p_S[s][:], AF.Exp,
                            bias=negCall[:, p:p + 1]),
                            deps=[mmA, mmB, ncc, av_toks[Ssl]])
                        exp_toks[Ssl] = ex
                        pend.append((kt, Ssl, ex))
                    if pend and pend[0][0] <= kt - 2:
                        pkt, pSsl, pex = pend.pop(0)
                        avA = bb.op("pe", lambda E_, kt=pkt, s=pSsl, p=p:
                                    E_.matmul(
                            p_oA[:], vplus[:, kt, 2 * p, 0:65],
                            ptil[s][:, 0:512],
                            start=(kt == 0), stop=(kt == 31)),
                            deps=[pex, war_oA])
                        avB = bb.op("pe", lambda E_, kt=pkt, s=pSsl, p=p:
                                    E_.matmul(
                            p_oB[:], vplus[:, kt, 2 * p + 1, 0:65],
                            ptil[s][:, 512:1024],
                            start=(kt == 0), stop=(kt == 31)),
                            deps=[pex, war_oB])
                        av_toks[pSsl] = avB
                        last_avA, last_avB = avA, avB
                rcA = bb.op("dve", lambda E_: E_.reciprocal(
                    rdenA[:], p_oA[64:65, :]), deps=[last_avA])
                rcB = bb.op("dve", lambda E_: E_.reciprocal(
                    rdenB[:], p_oB[64:65, :]), deps=[last_avB])
                rbA = bb.op("pe", lambda E_: E_.matmul(
                    p_S[0][0:64, 0:512], ar["onesr"][:, 0:64], rdenA[:],
                    start=True, stop=True),
                    deps=[rcA, exp_toks[0], av_toks[0]])
                rbB = bb.op("pe", lambda E_: E_.matmul(
                    p_S[1][0:64, 0:512], ar["onesr"][:, 0:64], rdenB[:],
                    start=True, stop=True),
                    deps=[rcB, exp_toks[1], av_toks[1]])
                nevA = bb.op("act", lambda E_: E_.copy(numsA[:], p_oA[0:64, :]),
                             deps=[last_avA, rcA])
                nevB = bb.op("act", lambda E_: E_.copy(numsB[:], p_oB[0:64, :]),
                             deps=[last_avB, rcB])
                dA = bb.op("dve", lambda E_, p=p: E_.tensor_mul(
                    attnT[0:64, p, :], numsA[:], p_S[0][0:64, 0:512]),
                    deps=[nevA, rbA, war_attnT])
                dB = bb.op("dve", lambda E_, p=p: E_.tensor_mul(
                    attnT[64:128, p, :], numsB[:], p_S[1][0:64, 0:512]),
                    deps=[nevB, rbB])
                exp_toks[0] = dA
                exp_toks[1] = dB
                war_oA, war_oB = dA, dB
                war_k[psl] = dB
            head_done = dB
            stat_first = True
            wo_last = [exp_toks[2], exp_toks[2]]
            for oc in range(CH):
                xls = oc % 2
                wsl = oc % 2
                lx = bb.dma("sp", xsl[xls][:], xrv[:, oc, ts(qt, QW)],
                            f"a0_xr{xls}", deps=[war_xr[xls]])
                mm = None
                for ic in range(CH):
                    mm = bb.op("pe", lambda E_, ic=ic, oc=oc, w=wsl: E_.matmul(
                        p_S[2][:, ts(w, 512)], wo[:, ic, ts(oc, 128)],
                        attnT[:, ic, :],
                        start=(ic == 0), stop=(ic == CH - 1)),
                        deps=[head_done, lwo[ic], wo_last[wsl]])
                ad2 = bb.op("dve", lambda E_, x=xls, w=wsl: E_.tensor_add(
                    nxs[:], xsl[x][:], p_S[2][:, ts(w, 512)]),
                    deps=[mm, lx, war_nxs])
                wo_last[wsl] = ad2
                war_xr[xls] = ad2
                st = bb.dma("sp", ov[:, oc, ts(qt, QW)], nxs[:], "st_x",
                            deps=[ad2])
                war_nxs = st
                if compute_stats_out:
                    stok = emit_out_stats(
                        bb, ar, nxs[:], QW, p_oA[0:1, :], p_oB[0:1, :],
                        stat_first, oc == CH - 1, [ad2, war_oA, war_oB],
                        srow, stats_out, ts(qt, QW), "a0_sst")
                    stat_first = False
                    if oc == CH - 1:
                        war_oA = stok
                        war_oB = stok
            war_attnT = ad2
            exp_toks[2] = ad2
            war_q = ad2
        bb.fence()



def emit_attn1(bb, ar, x_in, x_out, stats_in, stats_out, compute_stats_out):
    nc = bb.nc
    dbg = os.environ.get("DBG5", "")

    with ExitStack() as sc:
        Es = sc.enter_context
        qres = Es(nc.sbuf_tensor(_u("a1_qres"), [128, CH, T], BF16))
        kvk = Es(nc.sbuf_tensor(_u("a1_kvk"), [64, NH, 66], BF16))
        projp = Es(nc.sbuf_tensor(_u("a1_proj"), [128, 64], BF16))
        lp1 = bb.dma("pop", projp[0:64, :], ar["t_L1_attn_proj"], "a1_wld")
        lp = bb.dma("pop", projp[64:128, :], ar["t_L1_attn_proj"], "a1_wld",
                    deps=[lp1])
        m_row, s_row, rows_done = emit_rows_prologue(bb, ar, sc, stats_in)

        with ExitStack() as pa:
            Ep = pa.enter_context
            wq_ = Ep(nc.sbuf_tensor(_u("a1_wq"), [128, CH, D], BF16))
            wk_ = Ep(nc.sbuf_tensor(_u("a1_wk"), [128, CH, D], BF16))
            wv2 = Ep(nc.sbuf_tensor(_u("a1_wv"), [128, CH, D], BF16))
            lws = []
            for wname, wt in (("Wq", wq_), ("Wk", wk_), ("Wv", wv2)):
                wv_ = ar["t_L1_attn_" + wname].rearrange("(c p) n -> p c n", p=128)
                for kc in range(CH):
                    lws.append(bb.dma("pop", wt[:, kc, :], wv_[:, kc, :],
                                      "a1_wld"))
            xt = [Ep(nc.sbuf_tensor(_u("a1_x0"), [128, CH, TT], F32))]
            xt.append(xt[0])
            zt = Ep(nc.sbuf_tensor(_u("a1_z"), [128, CH, TT], BF16))
            ktile = Ep(nc.sbuf_tensor(_u("a1_kt"), [128, CH, TT], BF16))
            vtile = Ep(nc.sbuf_tensor(_u("a1_vt"), [128, 4, NH, 66], BF16))
            kfs = Ep(nc.sbuf_tensor(_u("a1_kfs"), [128, 512], F32))
            kfb = Ep(nc.sbuf_tensor(_u("a1_kfb"), [128, 512], BF16))
            sbb = Ep(nc.sbuf_tensor(_u("a1_sb"), [128, TT], F32))
            mbb = Ep(nc.sbuf_tensor(_u("a1_mb"), [128, TT], F32))
            ar["tscr"] = Ep(nc.sbuf_tensor(_u("a1_tscr"), [128, TT], F32))
            ar["tscr2"] = Ep(nc.sbuf_tensor(_u("a1_tscr2"), [128, TT], F32))
            p_mm2 = [Ep(nc.psum_tensor(_u(f"a1_pmm{i}"), [128, 512], F32))
                     for i in range(2)]
            p_v = Ep(nc.psum_tensor(_u("a1_pv"), [128, 1024], F32))
            p_kf = Ep(nc.psum_tensor(_u("a1_pkf"), [128, 512], F32))
            p_kv = Ep(nc.psum_tensor(_u("a1_pkv"), [64, 512], F32))
            kv_acc = Ep(nc.sbuf_tensor(_u("a1_kvacc"), [64, NH, 66], F32))

            vset = bb.op("dve", lambda E_: E_.memset(vtile[:], 1.0))
            kz = bb.op("dve", lambda E_: E_.memset(kv_acc[:], 0.0))
            xv = x_in.rearrange("(c p) t -> p c t", p=128)
            war_x = [None, None]
            war_z = None
            war_bc = None
            war_mm = None
            war_pv = None
            war_kf = None
            war_kfs = None
            war_kt = None
            kv_last = [None] * NH
            for tt in range(NTT):
                sl = 0
                ld = bb.dma("sp", xt[sl][:], xv[:, :, ts(tt, TT)], "a1_x0",
                            deps=[war_x[sl]])
                bc = emit_bcast2(bb, ar, p_v[:, 0:512], p_v[:, 512:1024],
                                 s_row, m_row, ts(tt, TT),
                                 [rows_done, war_bc, war_pv], sbb, mbb, TT)
                zt_toks = []
                for c in range(CH):
                    zt_toks.append(emit_apply_norm(
                        bb, ar, xt[sl][:, c, :], zt[:, c, :], mbb, sbb, None,
                        deps=[ld, war_z] + bc))
                war_x[sl] = zt_toks[-1]
                war_mm2 = [war_mm, war_mm]
                for di, (dst, wt) in enumerate(((qres, wq_), (ktile, wk_))):
                    for oc in range(CH):
                        pi = (di * CH + oc) % 2
                        mm = None
                        for kc in range(CH):
                            mm = bb.op("pe", lambda E_, kc=kc, oc=oc, w=wt,
                                       pi=pi: E_.matmul(
                                p_mm2[pi][:], w[:, kc, ts(oc, 128)],
                                zt[:, kc, :],
                                start=(kc == 0), stop=(kc == CH - 1)),
                                deps=[zt_toks[kc], war_mm2[pi], lws[-1]])
                        if dst is qres:
                            ev = bb.op("act", lambda E_, oc=oc, t0=tt, pi=pi:
                                       E_.copy(
                                qres[:, oc, ts(t0, TT)], p_mm2[pi][:]),
                                deps=[mm])
                        else:
                            ev = bb.op("act", lambda E_, oc=oc, pi=pi: E_.copy(
                                ktile[:, oc, :], p_mm2[pi][:]),
                                deps=[mm, war_kt])
                        war_mm2[pi] = ev
                war_mm = war_mm2[0] if war_mm2[1] is None else war_mm2[1]
                war_mm = ev
                v_toks = []
                for tsub in range(4):
                    mm = None
                    for kc in range(CH):
                        for j in range(2):
                            mm = bb.op("pe", lambda E_, kc=kc, j=j, t0=tsub:
                                       E_.matmul(
                                p_v[:, ts(j, 512)], zt[:, kc, ts(t0, 128)],
                                wv2[:, kc, ts(j, 512)],
                                start=(kc == 0), stop=(kc == CH - 1)),
                                deps=[zt_toks[kc], war_pv] + bc)
                    ev = bb.op("act", lambda E_, t0=tsub: E_.copy(
                        vtile[:, t0, :, 0:64],
                        p_v[:].rearrange("p (h d) -> p h d", h=NH)),
                        deps=[mm, vset, war_pv])
                    war_pv = ev
                    v_toks.append(ev)
                war_z = war_pv
                war_bc = v_toks[-1]
                pairs = [] if "nokf" in dbg else [
                    (h, tsub) for h in range(NH) for tsub in range(4)]
                for b0 in range(0, len(pairs), 4):
                    batch = pairs[b0:b0 + 4]
                    mk = None
                    for jj, (h, tsub) in enumerate(batch):
                        hb, hc = 64 * (h % 2), h // 2
                        mk = bb.op("pe", lambda E_, jj=jj, h=h, t0=tsub, hb=hb,
                                   hc=hc, nb=len(batch): E_.matmul(
                            p_kf[:, ts(jj, 64)],
                            ktile[hb:hb + 64, hc, ts(t0, 128)],
                            projp[hb:hb + 64, :],
                            start=(jj == 0), stop=(jj == nb - 1)),
                            deps=[war_mm, lp, war_kf])
                    if "norl" in dbg:
                        war_kf = mk
                        war_kfs = mk
                        war_kt = mk
                        continue
                    W = 64 * len(batch)
                    rl_ = bb.op("act", lambda E_, W=W: E_.activation(
                        kfs[:, 0:W], p_kf[:, 0:W], AF.Relu),
                        deps=[mk, war_kfs])
                    if "noep" in dbg:
                        war_kf = rl_
                        war_kfs = rl_
                        war_kt = rl_
                        continue
                    ep_ = bb.op("dve", lambda E_, W=W: E_.tensor_scalar(
                        kfb[:, 0:W], kfs[:, 0:W], 1e-6, None, ALU.add),
                        deps=[rl_, war_kfs])
                    war_kf = ep_
                    mv_toks = []
                    if "nomv" in dbg:
                        war_kfs = ep_
                        war_kt = war_kfs
                        continue
                    bh = batch[0][0]
                    for jj, (h, tsub) in enumerate(batch):
                        mv = bb.op("pe", lambda E_, jj=jj, h=h, t0=tsub,
                                   nb=len(batch): E_.matmul(
                            p_kv[:, 0:65],
                            kfb[:, ts(jj, 64)], vtile[:, t0, h, 0:65],
                            start=(jj == 0), stop=(jj == nb - 1)),
                            deps=[ep_, v_toks[tsub], kv_last[0]])
                        mv_toks.append(mv)
                    acc = bb.op("dve", lambda E_, h=bh: E_.tensor_add(
                        kv_acc[:, h, 0:65], kv_acc[:, h, 0:65],
                        p_kv[0:64, 0:65]), deps=[mv_toks[-1], kz])
                    kv_last[0] = acc
                    war_kfs = acc
                    war_pv = acc
                war_kt = war_kfs
            kv_ev = []
            if "nokf" not in dbg:
                ev = bb.op("dve", lambda E_: E_.tensor_copy(
                    kvk[:], kv_acc[:]), deps=[kv_last[0]])
                kv_ev.append(ev)
            if "dbg_kvk" in ar:
                kvf = Ep(nc.sbuf_tensor(_u("dbg_kvf"), [64, NH * 66], F32))
                ck = bb.op("dve", lambda E_: E_.tensor_copy(
                    kvf[:], kvk[:].rearrange("p a b -> p (a b)")), deps=kv_ev)
                bb.dma("sp", ar["dbg_kvk"], kvf[:], "dbg_s", deps=[ck])
            bb.fence()

        if "aonly" in dbg:
            bb.fence()
            return
        with ExitStack() as pb:
            Ep = pb.enter_context
            wo_ = Ep(nc.sbuf_tensor(_u("a1_wo"), [128, CH, D], BF16))
            lwo = []
            wv_ = ar["t_L1_attn_Wo"].rearrange("(c p) n -> p c n", p=128)
            for ic in range(CH):
                lwo.append(bb.dma("pop", wo_[:, ic, :], wv_[:, ic, :], "a1_wld"))
            bocol = Ep(nc.sbuf_tensor(_u("a1_bo"), [128, CH], F32))
            lbo = bb.dma("sp", bocol[:],
                         ar["t_L1_attn_bo"].rearrange("(c p) -> p c", p=128),
                         "ld_rows", allow_slow_non_contiguous=True)
            attnT = Ep(nc.sbuf_tensor(_u("a1_attnT"), [128, CH, TT], BF16))
            qfs2 = [Ep(nc.sbuf_tensor(_u(f"a1_qfs{i}"), [64, TT], F32))
                    for i in range(2)]
            qfb2 = [Ep(nc.sbuf_tensor(_u(f"a1_qfb{i}"), [64, TT], BF16))
                    for i in range(2)]
            nums = Ep(nc.sbuf_tensor(_u("a1_nums"), [64, TT], F32))
            rd1 = Ep(nc.sbuf_tensor(_u("a1_rd1"), [1, TT], F32))
            rd2 = Ep(nc.sbuf_tensor(_u("a1_rd2"), [1, TT], F32))
            dlt = Ep(nc.sbuf_tensor(_u("a1_dlt"), [128, TT], F32))
            xslb = [Ep(nc.sbuf_tensor(_u(f"a1_xs{i}"), [128, CH, TT], F32))
                    for i in range(2)]
            nxs = [Ep(nc.sbuf_tensor(_u(f"a1_nxs{i}"), [128, CH, TT], F32))
                   for i in range(2)]
            ar["nxb"] = Ep(nc.sbuf_tensor(_u("a1_nxb"), [128, TT], BF16))
            ar["nxq"] = Ep(nc.sbuf_tensor(_u("a1_nxq"), [128, TT], BF16))
            srow = Ep(nc.sbuf_tensor(_u("a1_srow"), [1, 2 * TT], F32))
            p_qf2 = [Ep(nc.psum_tensor(_u(f"a1_pqf{i}"), [64, TT], F32))
                     for i in range(2)]
            p_num2 = [Ep(nc.psum_tensor(_u(f"a1_pnum{i}"), [65, TT], F32))
                      for i in range(2)]
            p_rb = Ep(nc.psum_tensor(_u("a1_prb"), [64, TT], F32))
            p_wo = [Ep(nc.psum_tensor(_u(f"a1_pwo{i}"), [128, 512], F32))
                    for i in range(2)]
            p_st1 = Ep(nc.psum_tensor(_u("a1_pst1"), [128, TT], F32))
            p_sx = p_st1[0:1, :]
            p_sq = p_st1[32:33, :]

            xrv = x_in.rearrange("(c p) t -> p c t", p=128)
            ov = x_out.rearrange("(c p) t -> p c t", p=128)
            war_xs = [None, None]
            war_nxs = [None, None]
            war_qf = None
            war_qfs = None
            war_qfb = None
            war_num2 = [None, None]
            war_d1 = None
            war_rd2 = None
            war_nums = None
            war_rb = None
            war_attnT = None
            war_wo = [None, None]
            war_dlt = None
            def emit_qf(h, qt):
                nonlocal war_qf, war_qfs, war_qfb
                hb, hc, pi = 64 * (h % 2), h // 2, h % 2
                mq = bb.op("pe", lambda E_, hb=hb, hc=hc, q=qt, pi=pi:
                           E_.matmul(
                    p_qf2[pi][:], projp[hb:hb + 64, :],
                    qres[hb:hb + 64, hc, ts(q, TT)],
                    start=True, stop=True), deps=[lp, war_qf[pi]] + kv_ev)
                rl_ = bb.op("act", lambda E_, pi=pi: E_.activation(
                    qfs2[pi][:], p_qf2[pi][:], AF.Relu),
                    deps=[mq, war_qfs[pi]])
                war_qf[pi] = rl_
                ep_ = bb.op("dve", lambda E_, pi=pi: E_.tensor_scalar(
                    qfb2[pi][:], qfs2[pi][:], 1e-6, None, ALU.add),
                    deps=[rl_, war_qfb[pi]])
                war_qfs[pi] = ep_
                return ep_

            war_qf = [None, None]
            war_qfs = [None, None]
            war_qfb = [None, None]
            qf_tok = [None] * NH
            for qt in range(NTT):
                qf_tok[0] = emit_qf(0, qt)
                for h in range(NH):
                    hb, hc, pi = 64 * (h % 2), h // 2, h % 2
                    if h + 1 < NH:
                        qf_tok[h + 1] = emit_qf(h + 1, qt)
                    mn = bb.op("pe", lambda E_, h=h, pi=pi: E_.matmul(
                        p_num2[pi][:], kvk[:, h, 0:65], qfb2[pi][:],
                        start=True, stop=True),
                        deps=[qf_tok[h], war_num2[pi]] + kv_ev)
                    war_qfb[pi] = mn
                    d1 = bb.op("dve", lambda E_, pi=pi: E_.tensor_scalar(
                        rd1[:], p_num2[pi][64:65, :], 1e-6, None, ALU.add),
                        deps=[mn, war_d1])
                    war_d1 = d1
                    d2 = bb.op("dve", lambda E_: E_.reciprocal(rd2[:], rd1[:]),
                               deps=[d1, war_rd2])
                    rb = bb.op("pe", lambda E_: E_.matmul(
                        p_rb[:], ar["onesr"][:, 0:64], rd2[:],
                        start=True, stop=True), deps=[d2, war_rb])
                    war_rd2 = rb
                    nev = bb.op("act", lambda E_, pi=pi: E_.copy(
                        nums[:], p_num2[pi][0:64, :]),
                        deps=[mn, d1, war_nums])
                    war_num2[pi] = nev
                    dm = bb.op("dve", lambda E_, h=h, hb=hb, hc=hc: E_.tensor_mul(
                        attnT[hb:hb + 64, hc, :], nums[:], p_rb[:]),
                        deps=[nev, rb, war_attnT])
                    war_rb = dm
                    war_nums = dm
                head_done = war_rb
                xls = qt % 2
                lx = bb.dma("sp", xslb[xls][:], xrv[:, :, ts(qt, TT)],
                            f"a1_xr{xls}", deps=[war_xs[xls]])
                ep_toks = []
                for oc in range(CH):
                    psl = oc % 2
                    mm = None
                    for ic in range(CH):
                        mm = bb.op("pe", lambda E_, ic=ic, oc=oc, p=psl:
                                   E_.matmul(
                            p_wo[p][:], wo_[:, ic, ts(oc, 128)],
                            attnT[:, ic, :], start=(ic == 0),
                            stop=(ic == CH - 1)),
                            deps=[head_done, lwo[ic], war_wo[psl]])
                    e1 = bb.op("act", lambda E_, oc=oc, p=psl: E_.activation(
                        dlt[:], p_wo[p][:], AF.Identity, bias=bocol[:, oc:oc + 1]),
                        deps=[mm, lbo, war_dlt])
                    e2 = bb.op("pop", lambda E_, oc=oc, x=xls: E_.tensor_add(
                        nxs[x][:, oc, :], xslb[x][:, oc, :], dlt[:]),
                        deps=[e1, lx, war_nxs[xls]])
                    war_wo[psl] = e1
                    war_dlt = e2
                    ep_toks.append(e2)
                    if compute_stats_out:
                        emit_out_stats(bb, ar, nxs[xls][:, oc, :], TT,
                                       p_sx, p_sq, oc == 0, oc == CH - 1,
                                       [e2], srow, stats_out, ts(qt, TT),
                                       "a1_sst", shared=True)
                war_xs[xls] = ep_toks[-1]
                war_attnT = ep_toks[-1]
                st = bb.dma("sp", ov[:, :, ts(qt, TT)], nxs[xls][:], "st_x",
                            deps=ep_toks)
                war_nxs[xls] = st
            bb.fence()



SHAPES = {
    "L0_attn_gamma": [D], "L0_attn_w_qkv": [D, 3 * D], "L0_attn_w_out": [D, D],
    "L0_ffn_weight": [D, D], "L0_ffn_alpha": [1], "L0_ffn_bias": [D],
    "L1_attn_Wq": [D, D], "L1_attn_Wk": [D, D], "L1_attn_Wv": [D, D],
    "L1_attn_proj": [64, 64], "L1_attn_Wo": [D, D], "L1_attn_bo": [D],
    "L1_ffn_W_r": [D, D], "L1_ffn_W_k": [D, D], "L1_ffn_W_v": [D, D],
    "L1_ffn_mu_r": [D], "L1_ffn_mu_k": [D],
}

SUB_WEIGHTS = {
    1: ["L0_ffn_weight", "L0_ffn_alpha", "L0_ffn_bias"],
    2: ["L0_attn_gamma", "L0_attn_w_qkv", "L0_attn_w_out"],
    3: ["L0_ffn_weight", "L0_ffn_alpha", "L0_ffn_bias"],
    4: ["L1_ffn_W_r", "L1_ffn_W_k", "L1_ffn_W_v", "L1_ffn_mu_r", "L1_ffn_mu_k"],
    5: ["L1_attn_Wq", "L1_attn_Wk", "L1_attn_Wv", "L1_attn_proj",
        "L1_attn_Wo", "L1_attn_bo"],
    6: ["L1_ffn_W_r", "L1_ffn_W_k", "L1_ffn_W_v", "L1_ffn_mu_r", "L1_ffn_mu_k"],
}

EMITTERS = {1: emit_ffn0, 2: emit_attn0, 3: emit_ffn0,
            4: emit_rwkv, 5: emit_attn1, 6: emit_rwkv}


def build_group(subs):
    nc = bass.Bass("TRN2", target_bir_lowering=False)
    stack = ExitStack()
    bb = BB(nc, stack)
    ar = {"E": stack.enter_context}

    names = []
    for s in subs:
        for w in SUB_WEIGHTS[s]:
            if w not in names:
                names.append(w)
    for w in names:
        ar["t_" + w] = nc.dram_tensor(w, SHAPES[w], F32, kind="ExternalInput").ap()
    x_in = nc.dram_tensor("x_in", [D, T], F32, kind="ExternalInput").ap()
    x_out = nc.dram_tensor("x_out", [D, T], F32, kind="ExternalOutput").ap()
    xa = nc.dram_tensor("x_buf_a", [D, T], F32, kind="Internal").ap()
    xb = nc.dram_tensor("x_buf_b", [D, T], F32, kind="Internal").ap()
    st_a = nc.dram_tensor("stats_a", [2, T], F32, kind="Internal").ap()
    st_b = nc.dram_tensor("stats_b", [2, T], F32, kind="Internal").ap()
    if 1 in subs or 3 in subs:
        ar["weff_hbm"] = nc.dram_tensor("weff_hbm", [128, CH * D], BF16,
                                        kind="Internal").ap()
        ar["wmeta_hbm"] = nc.dram_tensor("wmeta_hbm", [128, 2], F32,
                                         kind="Internal").ap()
    if 2 in subs:
        ar["q_hbm"] = nc.dram_tensor("q_hbm", [CH, 128, T], BF16,
                                     kind="Internal").ap()
        ar["k_hbm"] = nc.dram_tensor("k_hbm", [CH, 128, T], BF16,
                                     kind="Internal").ap()

    with stack:
        emit_consts(bb, ar)
        n = len(subs)
        bufs = [x_in]
        for i in range(n - 1):
            bufs.append(xa if i % 2 == 0 else xb)
        bufs.append(x_out)
        stats = [st_a, st_b]
        emit_input_stats(bb, ar, x_in, st_a)
        for i, s in enumerate(subs):
            ar.pop("stat_war", None)
            ar.pop("stat_war2", None)
            ar.pop("stat_war3", None)
            if s == 3 and 1 in subs[:i]:
                ar["ffn0_reuse"] = True
            EMITTERS[s](bb, ar, bufs[i], bufs[i + 1], stats[i % 2],
                        stats[(i + 1) % 2], i < n - 1)
        final = [(s, v) for s, v in bb.cnt.items() if s == "st_x"]
        bb.finalize(final)
    return nc, names


_PROGRAM_CACHE = {}


def _get_program(subs):
    key = tuple(subs)
    if key not in _PROGRAM_CACHE:
        _PROGRAM_CACHE[key] = build_group(list(subs))
    return _PROGRAM_CACHE[key]


GROUPS = [[1, 2, 3, 4, 5, 6]]


def kernel(**inputs):
    x = np.ascontiguousarray(inputs["L0_attn_x"], dtype=np.float32)
    cur = [np.ascontiguousarray(x[i].T) for i in range(NCORES)]
    for subs in GROUPS:
        nc, names = _get_program(tuple(subs))
        in_maps = []
        for i in range(NCORES):
            m = {w: np.ascontiguousarray(inputs[w], dtype=np.float32)
                 for w in names}
            m["x_in"] = cur[i]
            in_maps.append(m)
        res = run_bass_kernel_spmd(nc, in_maps, core_ids=list(range(NCORES)))
        cur = [np.asarray(r["x_out"]) for r in res.results]
    out = np.stack([c.T for c in cur], axis=0)
    return np.ascontiguousarray(out, dtype=np.float32)


# version 13
# speedup vs baseline: 1.0484x; 1.0484x over previous
import os
import numpy as np
from contextlib import ExitStack

import concourse.bass as bass
import concourse.mybir as mybir
from concourse.bass_utils import run_bass_kernel_spmd

F32 = mybir.dt.float32
BF16 = mybir.dt.bfloat16
AF = mybir.ActivationFunctionType
ALU = mybir.AluOpType
AX = mybir.AxisListType

NCORES = 8
T = 4096
D = 1024
CH = 8
TT = 512
NTT = T // TT
NH = 16
DH = 64

ENGS = ("pe", "act", "dve", "pop", "sp")


class BB:

    def __init__(self, nc, stack):
        self.nc = nc
        self.stack = stack
        self.q = {e: [] for e in ENGS}
        self.sems = {}
        self.cnt = {}
        self.hi = {e: {} for e in ENGS}
        self.pend = {e: [] for e in ENGS}
        self.same_sync = bool(int(os.environ.get("BB_SAME_ENGINE_SYNC", "1")))
        for e in ENGS:
            self._mksem("g_" + e)

    def _mksem(self, name):
        if name not in self.sems:
            self.sems[name] = self.stack.enter_context(self.nc.semaphore(name))
            self.cnt[name] = 0
        return self.sems[name]

    def _waits(self, eng, deps):
        out = []
        for d in deps:
            if d is None:
                continue
            s, v = d
            if s == "g_" + eng and not self.same_sync:
                continue
            if self.hi[eng].get(s, 0) >= v:
                continue
            self.hi[eng][s] = v
            out.append((s, v))
        return out

    def op(self, eng, fn, deps=()):
        deps = list(deps) + self.pend[eng]
        self.pend[eng] = []
        waits = self._waits(eng, deps)
        s = "g_" + eng
        self.cnt[s] += 1
        tok = (s, self.cnt[s])
        self.q[eng].append((fn, waits, [(s, 1)]))
        return tok

    def dma(self, eng, out_ap, in_ap, sem, deps=(), **kw):
        deps = list(deps) + self.pend[eng]
        self.pend[eng] = []
        waits = self._waits(eng, deps)
        self._mksem(sem)
        self.cnt[sem] += 16
        tok = (sem, self.cnt[sem])

        def fn(E, out_ap=out_ap, in_ap=in_ap, kw=kw):
            return E.dma_start(out_ap, in_ap, **kw)

        self.q[eng].append((fn, waits, [(sem, 16)]))
        return tok

    def fence(self, toks=None):
        if toks is None:
            toks = [(s, v) for s, v in self.cnt.items() if v > 0]
        for e in ENGS:
            self.pend[e] = list({*self.pend[e], *toks})

    def finalize(self, final_toks):
        nc = self.nc
        waits = self._waits("sp", list(final_toks))
        if waits:
            self.q["sp"].append((None, waits, []))
        with nc.Block() as block:
            emap = dict(pe=block.tensor, act=block.scalar, dve=block.vector,
                        pop=block.gpsimd, sp=block.sync)
            for e in ENGS:
                items = self.q[e]
                if not items:
                    continue

                def body(E, items=items):
                    for fn, ws, incs in items:
                        for s, v in ws:
                            E.wait_ge(self.sems[s], v)
                        if fn is None:
                            continue
                        ins = fn(E)
                        for s, amt in incs:
                            ins = ins.then_inc(self.sems[s], amt)

                emap[e](body)



_NAME_N = [0]


def _u(n):
    _NAME_N[0] += 1
    return f"{n}_{_NAME_N[0]}"


def ts(i, n):
    return slice(i * n, (i + 1) * n)



def emit_consts(bb, ar):
    nc = bb.nc
    import ml_dtypes
    ident_np = np.eye(128, dtype=np.float32).astype(ml_dtypes.bfloat16)
    onesc_np = np.ones((128, 1), dtype=np.float32).astype(ml_dtypes.bfloat16)
    onesr_np = np.ones((1, 128), dtype=np.float32)
    cvals_np = np.array([[1e-5, 1e-30, 0.0, 0.0]], dtype=np.float32)
    t_cv = nc.inline_tensor(cvals_np, "c_cvals").ap()
    t_id = nc.inline_tensor(ident_np, "c_ident").ap()
    t_oc = nc.inline_tensor(onesc_np, "c_onesc").ap()
    t_or = nc.inline_tensor(onesr_np, "c_onesr").ap()
    E = ar["E"]
    ar["ident"] = E(nc.sbuf_tensor(_u("ident"), [128, 128], BF16))
    ar["onesc"] = E(nc.sbuf_tensor(_u("onesc"), [128, 1], BF16))
    ar["onesr"] = E(nc.sbuf_tensor(_u("onesr"), [1, 128], F32))
    ar["cvals"] = E(nc.sbuf_tensor(_u("cvals"), [1, 4], F32))
    ar["onescf"] = E(nc.sbuf_tensor(_u("onescf"), [128, 1], F32))
    t0 = bb.dma("sp", ar["cvals"][:], t_cv, "ld_const")
    tcf = nc.inline_tensor(np.ones((128, 1), dtype=np.float32), "c_onescf").ap()
    t4 = bb.dma("sp", ar["onescf"][:], tcf, "ld_const")
    t1 = bb.dma("sp", ar["ident"][:], t_id, "ld_const")
    t2 = bb.dma("sp", ar["onesc"][:], t_oc, "ld_const")
    t3 = bb.dma("sp", ar["onesr"][:], t_or, "ld_const")
    ar["const_tok"] = [t0, t1, t2, t3, t4]


def stat_rows_store(bb, ar, srow, stats_hbm, tok_slice, sem, deps, width):
    s1 = bb.dma("sp", stats_hbm[0:1, tok_slice], srow[:, 0:width], sem, deps=deps)
    s2 = bb.dma("sp", stats_hbm[1:2, tok_slice], srow[:, width:2 * width], sem,
                deps=deps)
    return s2


def emit_input_stats(bb, ar, x_hbm, stats_hbm):
    nc = bb.nc
    with ExitStack() as sc:
        Es = sc.enter_context
        xt = [Es(nc.sbuf_tensor(_u(f"ist_x{i}"), [128, CH, TT], F32)) for i in range(2)]
        xb = Es(nc.sbuf_tensor(_u("ist_xb"), [128, CH, TT], BF16))
        xq = Es(nc.sbuf_tensor(_u("ist_xq"), [128, CH, TT], BF16))
        srow = Es(nc.sbuf_tensor(_u("ist_srow"), [1, 2 * TT], F32))
        p_sx = Es(nc.psum_tensor(_u("ist_psx"), [1, TT], F32))
        p_sq = Es(nc.psum_tensor(_u("ist_psq"), [1, TT], F32))
        xv = x_hbm.rearrange("(c p) t -> p c t", p=128)
        war = [None, None]
        mm_last = None
        ev_last = None
        st_tok = None
        for tt in range(NTT):
            sl = tt % 2
            ld = bb.dma("sp", xt[sl][:], xv[:, :, ts(tt, TT)], f"ist_x{sl}",
                        deps=[war[sl]])
            c1 = bb.op("act", lambda E_, s=sl: E_.copy(xb[:], xt[s][:]),
                       deps=[ld, mm_last])
            c2 = bb.op("act", lambda E_, s=sl: E_.activation(
                xq[:], xt[s][:], AF.Square), deps=[ld, mm_last])
            war[sl] = c2
            mt = mq = None
            for c in range(CH):
                mt = bb.op("pe", lambda E_, c=c: E_.matmul(
                    p_sx[:], ar["onesc"][:], xb[:, c, :],
                    start=(c == 0), stop=(c == CH - 1)),
                    deps=[c1, ev_last] + (ar["const_tok"] if tt == 0 else []))
            for c in range(CH):
                mq = bb.op("pe", lambda E_, c=c: E_.matmul(
                    p_sq[:], ar["onesc"][:], xq[:, c, :],
                    start=(c == 0), stop=(c == CH - 1)), deps=[c2, ev_last])
            mm_last = mq
            e1 = bb.op("dve", lambda E_: E_.tensor_copy(srow[:, 0:TT], p_sx[:]),
                       deps=[mt, st_tok])
            e2 = bb.op("dve", lambda E_: E_.tensor_copy(srow[:, TT:2 * TT], p_sq[:]),
                       deps=[mq, st_tok])
            ev_last = e2
            st_tok = stat_rows_store(bb, ar, srow, stats_hbm, ts(tt, TT),
                                     "ist_st", [e1, e2], TT)
        bb.fence()


def emit_rows_prologue(bb, ar, sc, stats_in, want_rms_gamma=False):
    nc = bb.nc
    Es = sc.enter_context
    m_row = Es(nc.sbuf_tensor(_u("m_row"), [1, T], F32))
    s_row = Es(nc.sbuf_tensor(_u("scl_row"), [1, T], F32))
    with ExitStack() as tsc:
        Et = tsc.enter_context
        r0 = Et(nc.sbuf_tensor(_u("rtmp0"), [1, T], F32))
        r1 = Et(nc.sbuf_tensor(_u("rtmp1"), [1, T], F32))
        r2 = Et(nc.sbuf_tensor(_u("rtmp2"), [1, T], F32))
        l0 = bb.dma("sp", r0[:], stats_in[0:1, :], "ld_rows")
        l1 = bb.dma("sp", r1[:], stats_in[1:2, :], "ld_rows")
        t1 = bb.op("dve", lambda E_: E_.tensor_scalar(
            m_row[:], r0[:], 1.0 / D, None, ALU.mult), deps=[l0])
        t2 = bb.op("dve", lambda E_: E_.tensor_mul(r0[:], m_row[:], m_row[:]),
                   deps=[t1])
        t3 = bb.op("dve", lambda E_: E_.scalar_tensor_tensor(
            r1[:], r1[:], 1.0 / D, r0[:], ALU.mult, ALU.subtract), deps=[t2, l1])
        t4 = bb.op("act", lambda E_: E_.activation(
            r2[:], r1[:], AF.Ln, bias=ar["cvals"][0:1, 0:1]),
            deps=[t3] + ar["const_tok"])
        t5 = bb.op("act", lambda E_: E_.activation(r0[:], r2[:], AF.Exp, scale=-0.5),
                   deps=[t4])
        if not want_rms_gamma:
            done = bb.op("dve", lambda E_: E_.tensor_copy(s_row[:], r0[:]),
                         deps=[t5])
        else:
            u1 = bb.op("act", lambda E_: E_.activation(
                r2[:], r1[:], AF.Ln, bias=ar["cvals"][0:1, 1:2],
                scale=float(D)), deps=[t3, t4])
            u2 = bb.op("act", lambda E_: E_.activation(
                r1[:], r2[:], AF.Exp, scale=0.5), deps=[u1])
            u3 = bb.op("dve", lambda E_: E_.tensor_mul(r1[:], r1[:], r0[:]),
                       deps=[u2, t5])
            u4 = bb.op("dve", lambda E_: E_.tensor_scalar(
                r1[:], r1[:], 1e-12, None, ALU.max), deps=[u3])
            u5 = bb.op("dve", lambda E_: E_.reciprocal(r2[:], r1[:]), deps=[u4])
            done = bb.op("dve", lambda E_: E_.scalar_tensor_tensor(
                s_row[:], r0[:], 32.0, r2[:], ALU.mult, ALU.mult), deps=[u5])
        bb.fence()
    return m_row, s_row, done


def emit_bcast2(bb, ar, p_a, p_b_, row_a, row_b, tslice, deps, sb_a, sb_b, width):
    out = []
    for p_dst, row, sb in ((p_a, row_a, sb_a), (p_b_, row_b, sb_b)):
        m = None
        for j in range(width // 512):
            m = bb.op("pe", lambda E_, p=p_dst, r=row, j=j: E_.matmul(
                p[:, ts(j, 512)], ar["onesr"][:],
                r[0:1, tslice][:, ts(j, 512)], start=True, stop=True), deps=deps)
        e = bb.op("act", lambda E_, p=p_dst, sb=sb: E_.copy(sb[:], p[:]),
                  deps=[m])
        out.append(e)
    return out


def emit_out_stats(bb, ar, src_f32, width, p_sx, p_sq, first, last, deps,
                   srow, stats_hbm, tok_slice, st_sem, shared=False):
    nxb, nxq = ar["nxb"], ar["nxq"]
    w = ar.get("stat_war")
    c1 = bb.op("act", lambda E_: E_.copy(nxb[:, 0:width], src_f32),
               deps=list(deps) + [w])
    c2 = bb.op("act", lambda E_: E_.activation(nxq[:, 0:width], src_f32, AF.Square),
               deps=list(deps) + [w])
    mt = mq = None
    for j in range(width // 512):
        mt = bb.op("pe", lambda E_, j=j: E_.matmul(
            p_sx[:, ts(j, 512)], ar["onesc"][:], nxb[:, ts(j, 512)],
            start=first, stop=last), deps=[c1, ar.get("stat_war2")])
        mq = bb.op("pe", lambda E_, j=j: E_.matmul(
            p_sq[:, ts(j, 512)], ar["onesc"][:], nxq[:, ts(j, 512)],
            start=(first and not shared), stop=last,
            skip_group_check=shared), deps=[c2, ar.get("stat_war2")])
    ar["stat_war"] = mq
    ret = mq
    if last:
        e1 = bb.op("dve", lambda E_: E_.tensor_copy(srow[:, 0:width], p_sx[:]),
                   deps=[mt, ar.get("stat_war3")])
        e2 = bb.op("dve", lambda E_: E_.tensor_copy(
            srow[:, width:2 * width], p_sq[:]), deps=[mq, ar.get("stat_war3")])
        st = stat_rows_store(bb, ar, srow, stats_hbm, tok_slice, st_sem,
                             [e1, e2], width)
        ar["stat_war2"] = e2
        ar["stat_war3"] = st
        ret = e2
    return ret


def emit_stats_batch(bb, ar, src, twidth, p_sx, p_sq, deps,
                     srow, stats_hbm, tok_slice, st_sem, shared=False,
                     src_hbm=None):
    nxb, nxq = ar["nxb"], ar["nxq"]
    nbc = ar.get("nxb_ch", CH)
    nj = twidth // 512
    mt = mq = None
    for b0 in range(0, CH, nbc):
        cs = slice(b0, b0 + nbc)
        W = nbc * twidth
        if src_hbm is not None:
            sv = (src_hbm[:, b0, tok_slice] if nbc == 1
                  else src_hbm[:, cs, tok_slice])
            dv = (nxb[:, 0:W] if nbc == 1
                  else nxb[:, 0:W].rearrange("p (c t) -> p c t", c=nbc))
            c1 = bb.dma("pop", dv, sv, "sb_ld",
                        deps=list(deps) + [ar.get("stat_war")])
            c2 = bb.op("act", lambda E_, W=W: E_.activation(
                nxq[:, 0:W], nxb[:, 0:W], AF.Square),
                deps=[c1, ar.get("stat_war")])
        else:
            c1 = bb.op("act", lambda E_, cs=cs, W=W: E_.copy(
                nxb[:, 0:W], src[:, cs, :].rearrange("p c t -> p (c t)")),
                deps=list(deps) + [ar.get("stat_war")])
            c2 = bb.op("act", lambda E_, cs=cs, W=W: E_.activation(
                nxq[:, 0:W], src[:, cs, :].rearrange("p c t -> p (c t)"),
                AF.Square), deps=list(deps) + [ar.get("stat_war")])
        for ci in range(nbc):
            c = b0 + ci
            for j in range(nj):
                o0 = ci * twidth + j * 512
                mt = bb.op("pe", lambda E_, o0=o0, j=j, c=c: E_.matmul(
                    p_sx[:, ts(j, 512)], ar["onesc"][:],
                    nxb[:, o0:o0 + 512],
                    start=(c == 0), stop=(c == CH - 1)),
                    deps=[c1, ar.get("stat_war2")])
                mq = bb.op("pe", lambda E_, o0=o0, j=j, c=c: E_.matmul(
                    p_sq[:, ts(j, 512)], ar["onesc"][:],
                    nxq[:, o0:o0 + 512],
                    start=(c == 0 and not shared), stop=(c == CH - 1),
                    skip_group_check=shared),
                    deps=[c2, ar.get("stat_war2")])
        ar["stat_war"] = mq
    e1 = bb.op("dve", lambda E_: E_.tensor_copy(srow[:, 0:twidth], p_sx[:]),
               deps=[mt, ar.get("stat_war3")])
    e2 = bb.op("dve", lambda E_: E_.tensor_copy(
        srow[:, twidth:2 * twidth], p_sq[:]), deps=[mq, ar.get("stat_war3")])
    st = stat_rows_store(bb, ar, srow, stats_hbm, tok_slice, st_sem,
                         [e1, e2], twidth)
    ar["stat_war2"] = e2
    ar["stat_war3"] = st
    return e2


def emit_apply_norm(bb, ar, x_view, z_view, mb, sb, gcol_slice, deps):
    i = ar.get("tscr_i", 0)
    t = ar["tscr"] if i == 0 else ar["tscr2"]
    war = "tscr_war" if i == 0 else "tscr_war2"
    ar["tscr_i"] = 1 - i
    s1 = bb.op("pop", lambda E_: E_.tensor_sub(t[:], x_view, mb[:]),
               deps=list(deps) + [ar.get(war)])
    if gcol_slice is not None:
        s2 = bb.op("dve", lambda E_: E_.scalar_tensor_tensor(
            z_view, t[:], gcol_slice, sb[:], ALU.mult, ALU.mult), deps=[s1])
    else:
        s2 = bb.op("dve", lambda E_: E_.tensor_mul(z_view, t[:], sb[:]),
                   deps=[s1])
    ar[war] = s2
    return s2



def emit_ffn0(bb, ar, x_in, x_out, stats_in, stats_out, compute_stats_out):
    nc = bb.nc
    t_w = ar["t_L0_ffn_weight"]
    t_alpha = ar["t_L0_ffn_alpha"]
    t_bias = ar["t_L0_ffn_bias"]

    reuse = ar.pop("ffn0_reuse", False)
    with ExitStack() as sc:
        Es = sc.enter_context
        m_row, s_row, rows_done = emit_rows_prologue(bb, ar, sc, stats_in)
        wT = Es(nc.sbuf_tensor(_u("f0_wT"), [128, CH, D], BF16))
        acol = Es(nc.sbuf_tensor(_u("f0_acol"), [128, 1], F32))
        hbias = Es(nc.sbuf_tensor(_u("f0_hbias"), [128, CH], F32))
        if "dbg_wt" in ar:
            ar["dbg_dcol_t"] = Es(nc.sbuf_tensor(_u("dbg_dc"), [128, 4], F32))

        if reuse:
            lwt = bb.dma("sp", wT[:].rearrange("p c d -> p (c d)"),
                         ar["weff_hbm"], "f0_wre")
            lac = bb.dma("sp", acol[:], ar["wmeta_hbm"][:, 0:1], "f0_wre",
                         allow_slow_non_contiguous=True)
            tr_last = None
            b4 = lac
            bb.fence([lwt, lac])
        if not reuse:
            with ExitStack() as tsc:
                Et = tsc.enter_context
                wch = [Et(nc.sbuf_tensor(_u(f"f0_w{i}"), [128, D], F32)) for i in range(2)]
                absb = Et(nc.sbuf_tensor(_u("f0_absb"), [128, D], BF16))
                scr = Et(nc.sbuf_tensor(_u("f0_scr"), [128, D], F32))
                wtn = Et(nc.sbuf_tensor(_u("f0_wtn"), [128, D], BF16))
                wtn2 = Et(nc.sbuf_tensor(_u("f0_wtn2"), [128, D], BF16))
                drow = Et(nc.sbuf_tensor(_u("f0_drow"), [1, 4], F32))
                dcol = Et(nc.sbuf_tensor(_u("f0_dcol"), [128, 1], F32))
                a11 = Et(nc.sbuf_tensor(_u("f0_a11"), [1, 1], F32))
                p_sum = Et(nc.psum_tensor(_u("f0_psum"), [1, 512], F32))
                p_bc = Et(nc.psum_tensor(_u("f0_pbc"), [128, 1], F32))
                p_tr = Et(nc.psum_tensor(_u("f0_ptr"), [128, 128], BF16))
                wv = t_w.rearrange("(c p) d -> p c d", p=128)

                war = [None, None]
                mprev = None
                for oc in range(CH):
                    sl = oc % 2
                    ld = bb.dma("pop", wch[sl][:], wv[:, oc, :], f"f0_w{sl}",
                                deps=[war[sl]])
                    ab = bb.op("act", lambda E_, s=sl: E_.activation(
                        scr[:], wch[s][:], AF.Abs),
                        deps=[ld, mprev] + ar["const_tok"])
                    war[sl] = ab
                    for j in range(2):
                        mprev = bb.op("pe", lambda E_, j=j, oc=oc: E_.matmul(
                            p_sum[:], ar["onescf"][:], scr[:, ts(j, 512)],
                            start=(oc == 0 and j == 0),
                            stop=(oc == CH - 1 and j == 1)), deps=[ab])
                r1 = bb.op("dve", lambda E_: E_.tensor_reduce(
                    drow[0:1, 0:1], p_sum[:], AX.X, ALU.add), deps=[mprev])
                r2 = bb.op("dve", lambda E_: E_.tensor_scalar(
                    drow[0:1, 1:2], drow[0:1, 0:1], 0.7 / (D * D), None, ALU.mult),
                    deps=[r1])
                b1 = bb.op("pe", lambda E_: E_.matmul(
                    p_bc[:], ar["onesr"][:], drow[0:1, 1:2], start=True, stop=True),
                    deps=[r2])
                b2 = bb.op("dve", lambda E_: E_.tensor_copy(dcol[:], p_bc[:]),
                           deps=[b1])
                la = bb.dma("sp", a11[:], t_alpha.rearrange("(a o) -> a o", o=1),
                            "ld_rows")
                b3 = bb.op("pe", lambda E_: E_.matmul(
                    p_bc[:], ar["onesr"][:], a11[:], start=True, stop=True),
                    deps=[la, b2])
                b4 = bb.op("dve", lambda E_: E_.tensor_scalar(
                    acol[:], p_bc[:], 0.5, None, ALU.mult), deps=[b3])

                if "dbg_wt" in ar:
                    bb.op("dve", lambda E_: E_.tensor_copy(
                        ar["dbg_dcol_t"][:, 0:1], dcol[:]), deps=[b2])
                    bb.op("dve", lambda E_: E_.tensor_copy(
                        ar["dbg_dcol_t"][0:1, 1:4].rearrange("a b -> a b"),
                        drow[0:1, 0:3]), deps=[b2])
                war = [r1, r1]
                war_ptr = None
                tr_last = None
                prev_a2 = prev_a4 = prev_tr = None
                for oc in range(CH):
                    sl = oc % 2
                    ld = bb.dma("pop", wch[sl][:], wv[:, oc, :], f"f0_w{sl}",
                                deps=[war[sl]])
                    a1 = bb.op("act", lambda E_, s=sl: E_.activation(
                        scr[:], wch[s][:], AF.Abs), deps=[ld, b2, prev_a2])
                    a2 = bb.op("dve", lambda E_: E_.tensor_scalar(
                        absb[:], scr[:], dcol[:], None, ALU.is_gt),
                        deps=[a1, b2, prev_a4])
                    a3 = bb.op("act", lambda E_, s=sl: E_.sign(wtn[:], wch[s][:]),
                               deps=[a1, prev_tr])
                    war[sl] = a3
                    a4 = bb.op("dve", lambda E_: E_.tensor_mul(
                        wtn2[:], wtn[:], absb[:]), deps=[a3, a2, prev_tr])
                    prev_a2, prev_a4 = a2, a4
                    for dc in range(CH):
                        tr = bb.op("pe", lambda E_, dc=dc: E_.transpose(
                            p_tr[:], wtn2[:, ts(dc, 128)], ar["ident"][:]),
                            deps=[a4, war_ptr])
                        ev = bb.op("act", lambda E_, dc=dc, oc=oc: E_.copy(
                            wT[:, dc, ts(oc, 128)], p_tr[:]), deps=[tr])
                        war_ptr = ev
                        tr_last = ev
                    prev_tr = tr_last
                bb.fence()

            if "dbg_wt" in ar:
                wtf = Es(nc.sbuf_tensor(_u("dbg_wtf"), [128, CH * D], F32))
                ct = bb.op("dve", lambda E_: E_.tensor_copy(
                    wtf[:], wT[:].rearrange("p c d -> p (c d)")), deps=[tr_last])
                bb.dma("sp", ar["dbg_wt"], wtf[:], "dbg_s", deps=[ct])
                bb.dma("sp", ar["dbg_dcol"], ar["dbg_dcol_t"][:], "dbg_s",
                       deps=[tr_last])
            sw = bb.dma("sp", ar["weff_hbm"],
                        wT[:].rearrange("p c d -> p (c d)"), "f0_wsp",
                        deps=[tr_last])
            sa = bb.dma("sp", ar["wmeta_hbm"][:, 0:1], acol[:], "f0_wsp",
                        deps=[b4], allow_slow_non_contiguous=True)
        lb = bb.dma("sp", hbias[:], t_bias.rearrange("(c p) -> p c", p=128),
                    "ld_rows", allow_slow_non_contiguous=True)
        hb = bb.op("dve", lambda E_: E_.tensor_scalar(
            hbias[:], hbias[:], 0.5, None, ALU.mult), deps=[lb])

        xt = [Es(nc.sbuf_tensor(_u(f"f0_x{i}"), [128, CH, TT], F32)) for i in range(2)]
        zt = Es(nc.sbuf_tensor(_u("f0_z"), [128, CH, TT], BF16))
        nx = [Es(nc.sbuf_tensor(_u(f"f0_nx{i}"), [128, CH, TT], F32)) for i in range(2)]
        sbb = Es(nc.sbuf_tensor(_u("f0_sb"), [128, TT], F32))
        mbb = Es(nc.sbuf_tensor(_u("f0_mb"), [128, TT], F32))
        dlt2 = [Es(nc.sbuf_tensor(_u(f"f0_dlt{i}"), [128, TT], F32))
                for i in range(2)]
        ar["tscr"] = Es(nc.sbuf_tensor(_u("f0_tscr"), [128, TT], F32))
        ar["tscr2"] = Es(nc.sbuf_tensor(_u("f0_tscr2"), [128, TT], F32))
        ar["nxb"] = Es(nc.sbuf_tensor(_u("f0_nxb"), [128, CH * TT], BF16))
        ar["nxq"] = Es(nc.sbuf_tensor(_u("f0_nxq"), [128, CH * TT], BF16))
        ar["nxb_ch"] = CH
        srow = Es(nc.sbuf_tensor(_u("f0_srow"), [1, 2 * TT], F32))
        p_mm = [Es(nc.psum_tensor(_u(f"f0_pm{i}"), [128, 512], F32)) for i in range(2)]
        p_ba = Es(nc.psum_tensor(_u("f0_pba"), [128, 512], F32))
        p_bb2 = Es(nc.psum_tensor(_u("f0_pbb"), [128, 512], F32))
        p_sx = Es(nc.psum_tensor(_u("f0_psx"), [1, TT], F32))
        p_sq = Es(nc.psum_tensor(_u("f0_psq"), [1, TT], F32))

        xv = x_in.rearrange("(c p) t -> p c t", p=128)
        ov = x_out.rearrange("(c p) t -> p c t", p=128)
        war_x = [None, None]
        war_nx = [None, None]
        war_z = None
        war_bc = None
        war_mm = [None, None]
        war_dlt = None
        stats_job = None
        for tt in range(NTT):
            sl = tt % 2
            ld = bb.dma("sp", xt[sl][:], xv[:, :, ts(tt, TT)], f"f0_x{sl}",
                        deps=[war_x[sl]])
            bc = emit_bcast2(bb, ar, p_ba, p_bb2, s_row, m_row, ts(tt, TT),
                             [rows_done, war_bc], sbb, mbb, TT)
            zt_toks = []
            for c in range(CH):
                zt_toks.append(emit_apply_norm(
                    bb, ar, xt[sl][:, c, :], zt[:, c, :], mbb, sbb, None,
                    deps=[ld, war_z] + bc))
            war_bc = zt_toks[-1]
            ep_toks = []
            for oc in range(CH):
                psl = oc % 2
                mm = None
                for kc in range(CH):
                    mm = bb.op("pe", lambda E_, kc=kc, oc=oc, p=psl: E_.matmul(
                        p_mm[p][:], wT[:, kc, ts(oc, 128)], zt[:, kc, :],
                        start=(kc == 0), stop=(kc == CH - 1)),
                        deps=[zt_toks[kc], war_mm[psl]] +
                        ([tr_last] if tt == 0 and oc == 0 else []))
                dl_ = dlt2[oc % 2]
                e1 = bb.op("act", lambda E_, oc=oc, p=psl, dl_=dl_: E_.activation(
                    dl_[:], p_mm[p][:], AF.Identity, bias=hbias[:, oc:oc + 1],
                    scale=acol[:]), deps=[mm, hb, b4, war_dlt])
                e2 = bb.op("dve", lambda E_, oc=oc, s=sl, dl_=dl_: E_.tensor_add(
                    nx[s][:, oc, :], xt[s][:, oc, :], dl_[:]),
                    deps=[e1, war_nx[sl], ar.get("stat_war")])
                war_mm[psl] = e1
                war_dlt = ep_toks[-1] if ep_toks else None
                ep_toks.append(e2)
            war_x[sl] = ep_toks[-1]
            war_z = ep_toks[-1]
            st = bb.dma("sp", ov[:, :, ts(tt, TT)], nx[sl][:], "st_x",
                        deps=ep_toks)
            war_nx[sl] = st
            if compute_stats_out:
                if stats_job is not None:
                    emit_stats_batch(bb, ar, *stats_job)
                stats_job = (nx[sl][:], TT, p_sx, p_sq, [ep_toks[-1]],
                             srow, stats_out, ts(tt, TT), "f0_sst")
        if compute_stats_out and stats_job is not None:
            emit_stats_batch(bb, ar, *stats_job)
        bb.fence()



def emit_rwkv(bb, ar, x_in, x_out, stats_in, stats_out, compute_stats_out):
    nc = bb.nc
    t_ws = {"r": ar["t_L1_ffn_W_r"], "k": ar["t_L1_ffn_W_k"],
            "v": ar["t_L1_ffn_W_v"]}

    with ExitStack() as sc:
        Es = sc.enter_context
        m_row, s_row, rows_done = emit_rows_prologue(bb, ar, sc, stats_in)
        wT = {n: Es(nc.sbuf_tensor(_u(f"rw_wT{n}"), [128, CH, D], BF16))
              for n in ("r", "k", "v")}
        with ExitStack() as tsc:
            Et = tsc.enter_context
            wch = [Et(nc.sbuf_tensor(_u(f"rw_w{i}"), [128, D], BF16)) for i in range(2)]
            p_tr = Et(nc.psum_tensor(_u("rw_ptr"), [128, 128], BF16))
            war = [None, None]
            war_ptr = None
            tr_last = None
            for name in ("r", "k", "v"):
                wv_ = t_ws[name].rearrange("(c p) d -> p c d", p=128)
                for dc in range(CH):
                    sl = dc % 2
                    ld = bb.dma("pop", wch[sl][:], wv_[:, dc, :], f"rw_w{sl}",
                                deps=[war[sl]])
                    for cc in range(CH):
                        tr = bb.op("pe", lambda E_, cc=cc, s=sl: E_.transpose(
                            p_tr[:], wch[s][:, ts(cc, 128)], ar["ident"][:]),
                            deps=[ld, war_ptr] + ar["const_tok"])
                        ev = bb.op("act", lambda E_, cc=cc, dc=dc, n=name: E_.copy(
                            wT[n][:, cc, ts(dc, 128)], p_tr[:]), deps=[tr])
                        war_ptr = ev
                        tr_last = ev
                    war[sl] = tr_last
            bb.fence()

        mur = Es(nc.sbuf_tensor(_u("rw_mur"), [128, CH], F32))
        muk = Es(nc.sbuf_tensor(_u("rw_muk"), [128, CH], F32))
        lr = bb.dma("sp", mur[:], ar["t_L1_ffn_mu_r"].rearrange("(c p) -> p c", p=128),
                    "ld_rows", allow_slow_non_contiguous=True)
        lk = bb.dma("sp", muk[:], ar["t_L1_ffn_mu_k"].rearrange("(c p) -> p c", p=128),
                    "ld_rows", allow_slow_non_contiguous=True)

        xt = [Es(nc.sbuf_tensor(_u(f"rw_x{i}"), [128, CH, TT], F32)) for i in range(2)]
        xn = [Es(nc.sbuf_tensor(_u(f"rw_xn{i}"), [128, CH, TT + 1], F32))
              for i in range(2)]
        dt_ = Es(nc.sbuf_tensor(_u("rw_d"), [128, TT], F32))
        xr = Es(nc.sbuf_tensor(_u("rw_xr"), [128, CH, TT], BF16))
        xk = Es(nc.sbuf_tensor(_u("rw_xk"), [128, CH, TT], BF16))

        k2 = Es(nc.sbuf_tensor(_u("rw_k2"), [128, CH, TT], BF16))

        rt = Es(nc.sbuf_tensor(_u("rw_r"), [128, TT], F32))
        rl = Es(nc.sbuf_tensor(_u("rw_rl"), [128, TT], F32))
        dlt = Es(nc.sbuf_tensor(_u("rw_dlt"), [128, TT], F32))
        nx = Es(nc.sbuf_tensor(_u("rw_nx"), [128, CH, TT], F32))
        sbb = Es(nc.sbuf_tensor(_u("rw_sb"), [128, TT], F32))
        mbb = Es(nc.sbuf_tensor(_u("rw_mb"), [128, TT], F32))

        ar["tscr"] = Es(nc.sbuf_tensor(_u("rw_tscr"), [128, TT], F32))
        ar["tscr2"] = Es(nc.sbuf_tensor(_u("rw_tscr2"), [128, TT], F32))
        ar["nxb"] = Es(nc.sbuf_tensor(_u("rw_nxb"), [128, TT], BF16))
        ar["nxq"] = Es(nc.sbuf_tensor(_u("rw_nxq"), [128, TT], BF16))
        ar["nxb_ch"] = 1
        srow = Es(nc.sbuf_tensor(_u("rw_srow"), [1, 2 * TT], F32))
        p_mm = [Es(nc.psum_tensor(_u(f"rw_pm{i}"), [128, 512], F32)) for i in range(2)]
        p_ba = Es(nc.psum_tensor(_u("rw_pba"), [128, 512], F32))
        p_bb2 = Es(nc.psum_tensor(_u("rw_pbb"), [128, 512], F32))
        p_sx = Es(nc.psum_tensor(_u("rw_psx"), [1, TT], F32))
        p_sq = Es(nc.psum_tensor(_u("rw_psq"), [1, TT], F32))

        hz = bb.op("dve", lambda E_: E_.memset(xn[0][:, :, 0:1], 0.0))

        xv = x_in.rearrange("(c p) t -> p c t", p=128)
        ov = x_out.rearrange("(c p) t -> p c t", p=128)
        W = {"x": [None, None], "nx": None, "xn": [[hz], [hz]],
             "mm": [None, None], "rl": None, "dlt": None, "bc": None,
             "dt": None, "xr": None, "xk": None}
        apply_toks, kmix_toks, rmix_toks = {}, {}, {}
        bc_toks, ld_toks = {}, {}
        halo_toks = {0: hz}
        stats_job = None

        def e_ld(tt):
            sl = tt % 2
            ld_toks[tt] = bb.dma("sp", xt[sl][:], xv[:, :, ts(tt, TT)],
                                 f"rw_x{sl}", deps=[W["x"][sl]])

        def e_bc(tt):
            bc_toks[tt] = emit_bcast2(bb, ar, p_ba, p_bb2, s_row, m_row,
                                      ts(tt, TT), [rows_done, W["bc"]],
                                      sbb, mbb, TT)

        def e_apply(tt, cs):
            sl = tt % 2
            nsl = (tt + 1) % 2
            toks = apply_toks.setdefault(tt, [])
            for c in cs:
                i_ = ar.get("tscr_i", 0)
                t_ = ar["tscr"] if i_ == 0 else ar["tscr2"]
                wkey = "tscr_war" if i_ == 0 else "tscr_war2"
                ar["tscr_i"] = 1 - i_
                s1 = bb.op("pop", lambda E_, c=c, s=sl, t_=t_: E_.tensor_sub(
                    t_[:], xt[s][:, c, :], mbb[:]),
                    deps=[ld_toks[tt], ar.get(wkey)] + W["xn"][sl] +
                    bc_toks[tt])
                s2 = bb.op("dve", lambda E_, c=c, s=sl, t_=t_: E_.tensor_mul(
                    xn[s][:, c, 1:TT + 1], t_[:], sbb[:]), deps=[s1])
                ar[wkey] = s2
                toks.append(s2)
            if len(toks) == CH:
                W["bc"] = toks[-1]
                if tt + 1 < NTT:
                    halo_toks[tt + 1] = bb.op(
                        "dve", lambda E_, s=sl, nn=nsl: E_.tensor_copy(
                            xn[nn][:, :, 0:1], xn[s][:, :, TT:TT + 1]),
                        deps=toks + W["xn"][nsl])

        def e_kmix(tt, cs):
            sl = tt % 2
            toks = kmix_toks.setdefault(tt, [])
            for c in cs:
                d1 = bb.op("pop", lambda E_, c=c, s=sl: E_.tensor_sub(
                    dt_[:], xn[s][:, c, 1:TT + 1], xn[s][:, c, 0:TT]),
                    deps=[apply_toks[tt][c], halo_toks[tt], W["dt"]])
                k1 = bb.op("dve", lambda E_, c=c, s=sl: E_.scalar_tensor_tensor(
                    xk[:, c, :], dt_[:], muk[:, c:c + 1],
                    xn[s][:, c, 0:TT], ALU.mult, ALU.add),
                    deps=[d1, lk, W["xk"]])
                W["dt"] = k1
                toks.append(k1)

        def e_rmix(tt):
            sl = tt % 2
            toks = rmix_toks.setdefault(tt, [])
            for c in range(CH):
                d1 = bb.op("pop", lambda E_, c=c, s=sl: E_.tensor_sub(
                    dt_[:], xn[s][:, c, 1:TT + 1], xn[s][:, c, 0:TT]),
                    deps=[apply_toks[tt][c], halo_toks[tt], W["dt"]])
                r1 = bb.op("dve", lambda E_, c=c, s=sl: E_.scalar_tensor_tensor(
                    xr[:, c, :], dt_[:], mur[:, c:c + 1],
                    xn[s][:, c, 0:TT], ALU.mult, ALU.add),
                    deps=[d1, lr, W["xr"]])
                W["dt"] = r1
                toks.append(r1)
            W["xn"][sl] = [toks[-1]] + \
                ([halo_toks[tt + 1]] if tt + 1 in halo_toks else [])

        e_ld(0)
        e_bc(0)
        e_apply(0, range(CH))
        e_kmix(0, range(CH))
        for tt in range(NTT):
            sl = tt % 2
            e_rmix(tt)
            k_toks = []
            k_last = None
            for oc in range(CH):
                psl = oc % 2
                mm = None
                for kc in range(CH):
                    mm = bb.op("pe", lambda E_, kc=kc, oc=oc, p=psl: E_.matmul(
                        p_mm[p][:], wT["k"][:, kc, ts(oc, 128)], xk[:, kc, :],
                        start=(kc == 0), stop=(kc == CH - 1)),
                        deps=[kmix_toks[tt][kc], W["mm"][psl]] +
                        ([tr_last] if tt == 0 and oc == 0 else []))
                k_last = mm
                e1 = bb.op("act", lambda E_, p=psl: E_.activation(
                    rl[:], p_mm[p][:], AF.Relu), deps=[mm, W["rl"]])
                e2 = bb.op("act", lambda E_, oc=oc: E_.activation(
                    k2[:, oc, :], rl[:], AF.Square), deps=[e1])
                W["mm"][psl] = e2
                W["rl"] = e2
                k_toks.append(e2)
            W["xk"] = k_last
            if compute_stats_out and stats_job is not None:
                emit_stats_batch(bb, ar, *stats_job, src_hbm=ov)
                stats_job = None
            if tt + 1 < NTT:
                e_ld(tt + 1)
                e_bc(tt + 1)
            ep_toks = []
            war_rt = W["dlt"]
            r_last = None
            for oc in range(CH):
                psl = oc % 2
                mm = None
                for kc in range(CH):
                    mm = bb.op("pe", lambda E_, kc=kc, oc=oc, p=psl: E_.matmul(
                        p_mm[p][:], wT["r"][:, kc, ts(oc, 128)], xr[:, kc, :],
                        start=(kc == 0), stop=(kc == CH - 1)),
                        deps=[rmix_toks[tt][kc], W["mm"][psl]])
                r_last = mm
                sg = bb.op("act", lambda E_, p=psl: E_.activation(
                    rt[:], p_mm[p][:], AF.Sigmoid), deps=[mm, war_rt])
                W["mm"][psl] = sg
                mm = None
                for kc in range(CH):
                    mm = bb.op("pe", lambda E_, kc=kc, oc=oc, p=psl: E_.matmul(
                        p_mm[p][:], wT["v"][:, kc, ts(oc, 128)], k2[:, kc, :],
                        start=(kc == 0), stop=(kc == CH - 1)),
                        deps=[k_toks[kc], W["mm"][psl]])
                e1 = bb.op("dve", lambda E_, oc=oc, p=psl:
                           E_.scalar_tensor_tensor(
                    dlt[:], rt[:], 0.5, p_mm[p][:], ALU.mult,
                    ALU.mult), deps=[mm, sg, W["dlt"]])
                war_rt = e1
                e2 = bb.op("pop", lambda E_, oc=oc, s=sl: E_.tensor_add(
                    nx[:, oc, :], xt[s][:, oc, :], dlt[:]),
                    deps=[e1, W["nx"]])
                W["mm"][psl] = e1
                W["dlt"] = e2
                ep_toks.append(e2)
                if tt + 1 < NTT:
                    if oc < 4:
                        e_apply(tt + 1, [2 * oc, 2 * oc + 1])
                    else:
                        e_kmix(tt + 1, [2 * (oc - 4), 2 * (oc - 4) + 1])
            W["xr"] = r_last
            W["x"][sl] = ep_toks[-1]
            st = bb.dma("sp", ov[:, :, ts(tt, TT)], nx[:], "st_x",
                        deps=ep_toks)
            W["nx"] = st
            if compute_stats_out:
                stats_job = (None, TT, p_sx, p_sq, [st], srow, stats_out,
                             ts(tt, TT), "rw_sst")
        if compute_stats_out and stats_job is not None:
            emit_stats_batch(bb, ar, *stats_job, src_hbm=ov)
        bb.fence()



def emit_attn0(bb, ar, x_in, x_out, stats_in, stats_out, compute_stats_out):
    nc = bb.nc
    q_hbm, k_hbm = ar["q_hbm"], ar["k_hbm"]
    QT = 1024
    NQT = T // QT

    with ExitStack() as sc:
        Es = sc.enter_context
        vplus = Es(nc.sbuf_tensor(_u("a0_vplus"), [128, 32, NH, 66], BF16))
        m_row, s_row, rows_done = emit_rows_prologue(bb, ar, sc, stats_in,
                                                     want_rms_gamma=True)
        gcol = Es(nc.sbuf_tensor(_u("a0_gcol"), [128, CH], F32))
        lg = bb.dma("sp", gcol[:],
                    ar["t_L0_attn_gamma"].rearrange("(c p) -> p c", p=128),
                    "ld_rows", allow_slow_non_contiguous=True)
        lg2 = bb.op("dve", lambda E_: E_.tensor_scalar(
            gcol[:], gcol[:], 1.0, None, ALU.add), deps=[lg])
        vset = bb.op("dve", lambda E_: E_.memset(vplus[:], 1.0))
        negCall = Es(nc.sbuf_tensor(_u("a0_negCall"), [128, CH], F32))

        with ExitStack() as p1:
            Ep = p1.enter_context
            wq = Ep(nc.sbuf_tensor(_u("a0_wq"), [128, CH, 3 * D], BF16))
            lw = []
            wv_ = ar["t_L0_attn_w_qkv"].rearrange("(c p) n -> p c n", p=128)
            for kc in range(CH):
                lw.append(bb.dma("pop", wq[:, kc, :], wv_[:, kc, :], "a0_wld"))
            xt = [Ep(nc.sbuf_tensor(_u(f"a0_x{i}"), [128, CH, TT], F32))
                  for i in range(2)]
            zt = Ep(nc.sbuf_tensor(_u("a0_z"), [128, CH, TT], BF16))
            stg = [Ep(nc.sbuf_tensor(_u(f"a0_stg{i}"), [128, TT], BF16))
                   for i in range(2)]
            sbb = Ep(nc.sbuf_tensor(_u("a0_sb"), [128, TT], F32))
            mbb = Ep(nc.sbuf_tensor(_u("a0_mb"), [128, TT], F32))
            ar["tscr"] = Ep(nc.sbuf_tensor(_u("a0_tscr"), [128, TT], F32))
            ar["tscr2"] = Ep(nc.sbuf_tensor(_u("a0_tscr2"), [128, TT], F32))
            p_mm = [Ep(nc.psum_tensor(_u(f"a0_pm{i}"), [128, 512], F32))
                    for i in range(2)]
            p_v = Ep(nc.psum_tensor(_u("a0_pv"), [128, 1024], F32))
            p_ba = Ep(nc.psum_tensor(_u("a0_pba"), [128, 512], F32))
            p_bb2 = Ep(nc.psum_tensor(_u("a0_pbb"), [128, 512], F32))
            p_st = Ep(nc.psum_tensor(_u("a0_pst"), [128, 512], F32))
            sqsc = Ep(nc.sbuf_tensor(_u("a0_sqsc"), [128, TT], BF16))
            qkmax = Ep(nc.sbuf_tensor(_u("a0_qkmax"), [1, 64], F32))
            kbarf = Ep(nc.sbuf_tensor(_u("a0_kbarf"), [128, CH], F32))
            kbred = Ep(nc.sbuf_tensor(_u("a0_kbred"), [128, 1], F32))
            kb2c = Ep(nc.sbuf_tensor(_u("a0_kb2"), [128, CH], BF16))
            qkz = bb.op("dve", lambda E_: E_.memset(qkmax[:], 0.0))
            kbz = bb.op("dve", lambda E_: E_.memset(kbarf[:], 0.0))

            xv = x_in.rearrange("(c p) t -> p c t", p=128)
            war_x = [None, None]
            war_z = None
            war_bc = None
            war_mm = [None, None]
            war_stg = [None, None]
            war_pv = None
            for tt in range(NTT):
                sl = tt % 2
                ld = bb.dma("sp", xt[sl][:], xv[:, :, ts(tt, TT)], f"a0_x{sl}",
                            deps=[war_x[sl]])
                bc = emit_bcast2(bb, ar, p_ba, p_bb2, s_row, m_row, ts(tt, TT),
                                 [rows_done, war_bc], sbb, mbb, TT)
                zt_toks = []
                for c in range(CH):
                    zt_toks.append(emit_apply_norm(
                        bb, ar, xt[sl][:, c, :], zt[:, c, :], mbb, sbb,
                        gcol[:, c:c + 1], deps=[ld, war_z, lg2] + bc))
                war_bc = zt_toks[-1]
                war_x[sl] = zt_toks[-1]
                for oc in range(16):
                    psl = oc % 2
                    mm = None
                    for kc in range(CH):
                        mm = bb.op("pe", lambda E_, kc=kc, oc=oc, p=psl:
                                   E_.matmul(
                            p_mm[p][:], wq[:, kc, ts(oc, 128)], zt[:, kc, :],
                            start=(kc == 0), stop=(kc == CH - 1)),
                            deps=[zt_toks[kc], lw[kc], war_mm[psl]])
                    scale = 0.125 if oc < 8 else 1.0
                    ev = bb.op("act", lambda E_, p=psl, s=psl, sc_=scale:
                               E_.activation(stg[s][:], p_mm[p][:], AF.Copy,
                                             scale=sc_),
                               deps=[mm, war_stg[psl]])
                    war_mm[psl] = ev
                    dst = q_hbm if oc < 8 else k_hbm
                    st = bb.dma("sp", dst[oc % 8, :, ts(tt, TT)], stg[psl][:],
                                "a0_qkst", deps=[ev])
                    war_stg[psl] = st
                    sq = bb.op("dve", lambda E_, s=psl: E_.tensor_mul(
                        sqsc[:], stg[s][:], stg[s][:]),
                        deps=[ev, ar.get("a0_sq_war")])
                    msq = bb.op("pe", lambda E_: E_.matmul(
                        p_st[0:1, :], ar["onesc"][:], sqsc[:],
                        start=True, stop=True),
                        deps=[sq, ar.get("a0_pst_war")])
                    rmx = bb.op("dve", lambda E_, oc=oc: E_.tensor_reduce(
                        qkmax[0:1, 32 + oc % 2:33 + oc % 2], p_st[0:1, :],
                        AX.X, ALU.max), deps=[msq, qkz])
                    upd = bb.op("dve", lambda E_, oc=oc: E_.tensor_tensor(
                        qkmax[0:1, oc:oc + 1], qkmax[0:1, oc:oc + 1],
                        qkmax[0:1, 32 + oc % 2:33 + oc % 2], ALU.max),
                        deps=[rmx])
                    ar["a0_sq_war"] = msq
                    ar["a0_pst_war"] = rmx
                    stat_last = upd
                    if oc >= 8:
                        kr = bb.op("dve", lambda E_, s=psl: E_.tensor_reduce(
                            kbred[:], stg[s][:], AX.X, ALU.add),
                            deps=[ev, ar.get("a0_kbr_war")])
                        ka = bb.op("dve", lambda E_, oc=oc: E_.tensor_add(
                            kbarf[:, oc - 8:oc - 7], kbarf[:, oc - 8:oc - 7],
                            kbred[:]), deps=[kr, kbz])
                        ar["a0_kbr_war"] = ka
                        stat_last = ka
                for tsub in range(TT // 128):
                    ktg = tt * 4 + tsub
                    mm = None
                    for kc in range(CH):
                        for j in range(2):
                            mm = bb.op("pe", lambda E_, kc=kc, j=j, t0=tsub:
                                       E_.matmul(
                                p_v[:, ts(j, 512)], zt[:, kc, ts(t0, 128)],
                                wq[:, kc, 2048 + j * 512:2048 + (j + 1) * 512],
                                start=(kc == 0), stop=(kc == CH - 1)),
                                deps=[zt_toks[kc], war_pv])
                    ev = bb.op("act", lambda E_, k=ktg: E_.copy(
                        vplus[:, k, :, 0:64],
                        p_v[:].rearrange("p (h d) -> p h d", h=NH)),
                        deps=[mm, vset])
                    war_pv = ev
                    war_z = ev
            kq = bb.op("dve", lambda E_: E_.tensor_mul(
                kb2c[:], kbarf[:], kbarf[:]),
                deps=[ar.get("a0_kbr_war"), kbz])
            kbn_ev = []
            for hh in range(2):
                mmk = bb.op("pe", lambda E_, hh=hh: E_.matmul(
                    p_st[0:1, 0:CH], ar["onesc"][64 * hh:64 * hh + 64, :],
                    kb2c[64 * hh:64 * hh + 64, :], start=True, stop=True),
                    deps=[kq, ar.get("a0_pst_war"), stat_last] + kbn_ev)
                evk = bb.op("dve", lambda E_, hh=hh: E_.tensor_copy(
                    qkmax[0:1, 40 + 8 * hh:48 + 8 * hh], p_st[0:1, 0:CH]),
                    deps=[mmk])
                kbn_ev.append(evk)
            cdone = []
            war_nc = None
            for p in range(CH):
                t1 = bb.op("dve", lambda E_, p=p: E_.tensor_tensor(
                    qkmax[0:1, 56:57], qkmax[0:1, 2 * p:2 * p + 1],
                    qkmax[0:1, 2 * p + 1:2 * p + 2], ALU.max),
                    deps=[stat_last] + cdone + kbn_ev)
                t2 = bb.op("dve", lambda E_, p=p: E_.tensor_tensor(
                    qkmax[0:1, 57:58], qkmax[0:1, 8 + 2 * p:9 + 2 * p],
                    qkmax[0:1, 9 + 2 * p:10 + 2 * p], ALU.max), deps=[t1])
                t3 = bb.op("dve", lambda E_, p=p: E_.tensor_tensor(
                    qkmax[0:1, 58:59], qkmax[0:1, 40 + p:41 + p],
                    qkmax[0:1, 48 + p:49 + p], ALU.max), deps=[t2])
                l1 = bb.op("act", lambda E_: E_.activation(
                    qkmax[0:1, 59:60], qkmax[0:1, 56:57], AF.Ln,
                    bias=ar["cvals"][0:1, 1:2]), deps=[t3])
                l2 = bb.op("act", lambda E_: E_.activation(
                    qkmax[0:1, 60:61], qkmax[0:1, 57:58], AF.Ln,
                    bias=ar["cvals"][0:1, 1:2]), deps=[l1])
                l3 = bb.op("act", lambda E_: E_.activation(
                    qkmax[0:1, 61:62], qkmax[0:1, 58:59], AF.Ln,
                    bias=ar["cvals"][0:1, 1:2]), deps=[l2])
                a1 = bb.op("dve", lambda E_: E_.tensor_add(
                    qkmax[0:1, 62:63], qkmax[0:1, 59:60], qkmax[0:1, 60:61]),
                    deps=[l2, l3])
                a2 = bb.op("act", lambda E_: E_.activation(
                    qkmax[0:1, 62:63], qkmax[0:1, 62:63], AF.Exp, scale=0.5),
                    deps=[a1])
                b1 = bb.op("dve", lambda E_: E_.tensor_add(
                    qkmax[0:1, 63:64], qkmax[0:1, 59:60], qkmax[0:1, 61:62]),
                    deps=[a2, l3])
                b2 = bb.op("act", lambda E_: E_.activation(
                    qkmax[0:1, 63:64], qkmax[0:1, 63:64], AF.Exp, scale=0.5),
                    deps=[b1])
                b3 = bb.op("dve", lambda E_: E_.tensor_scalar(
                    qkmax[0:1, 63:64], qkmax[0:1, 63:64], -1.0 / T, None,
                    ALU.mult), deps=[b2])
                c1 = bb.op("dve", lambda E_: E_.tensor_scalar(
                    qkmax[0:1, 63:64], qkmax[0:1, 63:64], 60.0, None,
                    ALU.add), deps=[b3])
                c2 = bb.op("dve", lambda E_: E_.tensor_tensor(
                    qkmax[0:1, 63:64], qkmax[0:1, 62:63], qkmax[0:1, 63:64],
                    ALU.min), deps=[c1])
                c3 = bb.op("dve", lambda E_: E_.tensor_scalar(
                    qkmax[0:1, 62:63], qkmax[0:1, 62:63], -70.0, None,
                    ALU.add), deps=[c2])
                c4 = bb.op("dve", lambda E_: E_.tensor_tensor(
                    qkmax[0:1, 63:64], qkmax[0:1, 63:64], qkmax[0:1, 62:63],
                    ALU.max), deps=[c3])
                c5 = bb.op("dve", lambda E_: E_.tensor_scalar(
                    qkmax[0:1, 63:64], qkmax[0:1, 63:64], -1.0, None,
                    ALU.mult), deps=[c4])
                bcm = bb.op("pe", lambda E_, p=p: E_.matmul(
                    p_st[:, 0:1], ar["onesr"][:], qkmax[0:1, 63:64],
                    start=True, stop=True), deps=[c5, war_nc, kbn_ev[-1]])
                ncv = bb.op("dve", lambda E_, p=p: E_.tensor_copy(
                    negCall[:, p:p + 1], p_st[:, 0:1]), deps=[bcm])
                war_nc = ncv
                cdone.append(ncv)
            ar["a0_c_done"] = cdone
            bb.fence()

        QW = 512
        NQW = T // QW
        NSLOT = 3
        qbuf = Es(nc.sbuf_tensor(_u("a0_qb"), [128, CH, QW], BF16))
        kbuf = [Es(nc.sbuf_tensor(_u(f"a0_kb{i}"), [128, T], BF16))
                for i in range(2)]
        ptil = [Es(nc.sbuf_tensor(_u(f"a0_pt{i}"), [128, 1024], BF16))
                for i in range(NSLOT)]
        attnT = Es(nc.sbuf_tensor(_u("a0_attnT"), [128, CH, QW], BF16))
        numsA = Es(nc.sbuf_tensor(_u("a0_numsA"), [64, QW], F32))
        numsB = Es(nc.sbuf_tensor(_u("a0_numsB"), [64, QW], F32))
        rdenA = Es(nc.sbuf_tensor(_u("a0_rdenA"), [1, QW], F32))
        rdenB = Es(nc.sbuf_tensor(_u("a0_rdenB"), [1, QW], F32))
        wo = Es(nc.sbuf_tensor(_u("a0_wo"), [128, CH, D], BF16))
        xsl = [Es(nc.sbuf_tensor(_u(f"a0_xs{i}"), [128, QW], F32))
               for i in range(2)]
        nxs = Es(nc.sbuf_tensor(_u("a0_nxs"), [128, QW], F32))
        ar["nxb"] = Es(nc.sbuf_tensor(_u("a0_nxb"), [128, QW], BF16))
        ar["nxq"] = Es(nc.sbuf_tensor(_u("a0_nxq"), [128, QW], BF16))
        srow = Es(nc.sbuf_tensor(_u("a0_srow"), [1, 2 * QW], F32))
        p_S = [Es(nc.psum_tensor(_u(f"a0_pS{i}"), [128, 1024], F32))
               for i in range(NSLOT)]
        p_oA = Es(nc.psum_tensor(_u("a0_poA"), [65, 512], F32))
        p_oB = Es(nc.psum_tensor(_u("a0_poB"), [65, 512], F32))

        wov = ar["t_L0_attn_w_out"].rearrange("(c p) n -> p c n", p=128)
        lwo = [bb.dma("pop", wo[:, ic, :], wov[:, ic, :], "a0_wld")
               for ic in range(CH)]

        war_q = None
        war_k = [None, None]
        war_attnT = None
        war_xr = [None, None]
        war_nxs = None
        war_oA = None
        war_oB = None
        exp_toks = [None] * NSLOT
        av_toks = [None] * NSLOT
        qv = q_hbm.rearrange("c p t -> p c t")
        xrv = x_in.rearrange("(c p) t -> p c t", p=128)
        ov = x_out.rearrange("(c p) t -> p c t", p=128)
        cdone = ar["a0_c_done"]

        NP = NH // 2
        lk_tok = [None] * NP
        for qt in range(NQW):
            lq = bb.dma("sp", qbuf[:], qv[:, :, ts(qt, QW)], "a0_q0",
                        deps=[war_q])
            if qt == 0:
                lk_tok[0] = bb.dma("sp", kbuf[0][:], k_hbm[0, :, :], "a0_k0",
                                   deps=[war_k[0]])
            for p in range(NP):
                psl = p % 2
                qhA = qbuf[0:64, p, :]
                qhB = qbuf[64:128, p, :]
                lk = lk_tok[p]
                ncc = cdone[p]
                if p + 1 < NP:
                    npsl = (p + 1) % 2
                    lk_tok[p + 1] = bb.dma(
                        "sp", kbuf[npsl][:], k_hbm[p + 1, :, :],
                        f"a0_k{npsl}", deps=[war_k[npsl]])
                elif qt + 1 < NQW:
                    lk_tok[0] = bb.dma("sp", kbuf[0][:], k_hbm[0, :, :],
                                       "a0_k0", deps=[war_k[0]])
                pend = []
                last_avA = last_avB = None
                for kt in range(34):
                    if kt < 32:
                        Ssl = kt % NSLOT
                        mmA = bb.op("pe", lambda E_, kt=kt, s=Ssl, ks=psl,
                                    qh=qhA: E_.matmul(
                            p_S[s][:, 0:512], kbuf[ks][0:64, ts(kt, 128)],
                            qh[:], start=True, stop=True,
                            tile_position=(0, 0)),
                            deps=[lk, lq, exp_toks[Ssl]])
                        mmB = bb.op("pe", lambda E_, kt=kt, s=Ssl, ks=psl,
                                    qh=qhB: E_.matmul(
                            p_S[s][:, 512:1024], kbuf[ks][64:128, ts(kt, 128)],
                            qh[:], start=True, stop=True,
                            tile_position=(64, 0)),
                            deps=[lk])
                        ex = bb.op("act", lambda E_, s=Ssl, p=p: E_.activation(
                            ptil[s][:], p_S[s][:], AF.Exp,
                            bias=negCall[:, p:p + 1]),
                            deps=[mmA, mmB, ncc, av_toks[Ssl]])
                        exp_toks[Ssl] = ex
                        pend.append((kt, Ssl, ex))
                    if pend and pend[0][0] <= kt - 2:
                        pkt, pSsl, pex = pend.pop(0)
                        avA = bb.op("pe", lambda E_, kt=pkt, s=pSsl, p=p:
                                    E_.matmul(
                            p_oA[:], vplus[:, kt, 2 * p, 0:65],
                            ptil[s][:, 0:512],
                            start=(kt == 0), stop=(kt == 31)),
                            deps=[pex, war_oA])
                        avB = bb.op("pe", lambda E_, kt=pkt, s=pSsl, p=p:
                                    E_.matmul(
                            p_oB[:], vplus[:, kt, 2 * p + 1, 0:65],
                            ptil[s][:, 512:1024],
                            start=(kt == 0), stop=(kt == 31)),
                            deps=[pex, war_oB])
                        av_toks[pSsl] = avB
                        last_avA, last_avB = avA, avB
                rcA = bb.op("dve", lambda E_: E_.reciprocal(
                    rdenA[:], p_oA[64:65, :]), deps=[last_avA])
                rcB = bb.op("dve", lambda E_: E_.reciprocal(
                    rdenB[:], p_oB[64:65, :]), deps=[last_avB])
                rbA = bb.op("pe", lambda E_: E_.matmul(
                    p_S[0][0:64, 0:512], ar["onesr"][:, 0:64], rdenA[:],
                    start=True, stop=True),
                    deps=[rcA, exp_toks[0], av_toks[0]])
                rbB = bb.op("pe", lambda E_: E_.matmul(
                    p_S[1][0:64, 0:512], ar["onesr"][:, 0:64], rdenB[:],
                    start=True, stop=True),
                    deps=[rcB, exp_toks[1], av_toks[1]])
                nevA = bb.op("act", lambda E_: E_.copy(numsA[:], p_oA[0:64, :]),
                             deps=[last_avA, rcA])
                nevB = bb.op("act", lambda E_: E_.copy(numsB[:], p_oB[0:64, :]),
                             deps=[last_avB, rcB])
                dA = bb.op("dve", lambda E_, p=p: E_.tensor_mul(
                    attnT[0:64, p, :], numsA[:], p_S[0][0:64, 0:512]),
                    deps=[nevA, rbA, war_attnT])
                dB = bb.op("dve", lambda E_, p=p: E_.tensor_mul(
                    attnT[64:128, p, :], numsB[:], p_S[1][0:64, 0:512]),
                    deps=[nevB, rbB])
                exp_toks[0] = dA
                exp_toks[1] = dB
                war_oA, war_oB = dA, dB
                war_k[psl] = dB
            head_done = dB
            stat_first = True
            wo_last = [exp_toks[2], exp_toks[2]]
            for oc in range(CH):
                xls = oc % 2
                wsl = oc % 2
                lx = bb.dma("sp", xsl[xls][:], xrv[:, oc, ts(qt, QW)],
                            f"a0_xr{xls}", deps=[war_xr[xls]])
                mm = None
                for ic in range(CH):
                    mm = bb.op("pe", lambda E_, ic=ic, oc=oc, w=wsl: E_.matmul(
                        p_S[2][:, ts(w, 512)], wo[:, ic, ts(oc, 128)],
                        attnT[:, ic, :],
                        start=(ic == 0), stop=(ic == CH - 1)),
                        deps=[head_done, lwo[ic], wo_last[wsl]])
                ad2 = bb.op("dve", lambda E_, x=xls, w=wsl: E_.tensor_add(
                    nxs[:], xsl[x][:], p_S[2][:, ts(w, 512)]),
                    deps=[mm, lx, war_nxs])
                wo_last[wsl] = ad2
                war_xr[xls] = ad2
                st = bb.dma("sp", ov[:, oc, ts(qt, QW)], nxs[:], "st_x",
                            deps=[ad2])
                war_nxs = st
                if compute_stats_out:
                    stok = emit_out_stats(
                        bb, ar, nxs[:], QW, p_oA[0:1, :], p_oB[0:1, :],
                        stat_first, oc == CH - 1, [ad2, war_oA, war_oB],
                        srow, stats_out, ts(qt, QW), "a0_sst")
                    stat_first = False
                    if oc == CH - 1:
                        war_oA = stok
                        war_oB = stok
            war_attnT = ad2
            exp_toks[2] = ad2
            war_q = ad2
        bb.fence()



def emit_attn1(bb, ar, x_in, x_out, stats_in, stats_out, compute_stats_out):
    nc = bb.nc
    dbg = os.environ.get("DBG5", "")

    with ExitStack() as sc:
        Es = sc.enter_context
        qres = Es(nc.sbuf_tensor(_u("a1_qres"), [128, CH, T], BF16))
        kvk = Es(nc.sbuf_tensor(_u("a1_kvk"), [64, NH, 66], BF16))
        projp = Es(nc.sbuf_tensor(_u("a1_proj"), [128, 64], BF16))
        lp1 = bb.dma("pop", projp[0:64, :], ar["t_L1_attn_proj"], "a1_wld")
        lp = bb.dma("pop", projp[64:128, :], ar["t_L1_attn_proj"], "a1_wld",
                    deps=[lp1])
        m_row, s_row, rows_done = emit_rows_prologue(bb, ar, sc, stats_in)

        with ExitStack() as pa:
            Ep = pa.enter_context
            wq_ = Ep(nc.sbuf_tensor(_u("a1_wq"), [128, CH, D], BF16))
            wk_ = Ep(nc.sbuf_tensor(_u("a1_wk"), [128, CH, D], BF16))
            wv2 = Ep(nc.sbuf_tensor(_u("a1_wv"), [128, CH, D], BF16))
            lws = []
            for wname, wt in (("Wq", wq_), ("Wk", wk_), ("Wv", wv2)):
                wv_ = ar["t_L1_attn_" + wname].rearrange("(c p) n -> p c n", p=128)
                for kc in range(CH):
                    lws.append(bb.dma("pop", wt[:, kc, :], wv_[:, kc, :],
                                      "a1_wld"))
            xt = [Ep(nc.sbuf_tensor(_u("a1_x0"), [128, CH, TT], F32))]
            xt.append(xt[0])
            zt = Ep(nc.sbuf_tensor(_u("a1_z"), [128, CH, TT], BF16))
            ktile = Ep(nc.sbuf_tensor(_u("a1_kt"), [128, CH, TT], BF16))
            vtile = Ep(nc.sbuf_tensor(_u("a1_vt"), [128, 4, NH, 66], BF16))
            kfs = Ep(nc.sbuf_tensor(_u("a1_kfs"), [128, 512], F32))
            kfb = Ep(nc.sbuf_tensor(_u("a1_kfb"), [128, 512], BF16))
            sbb = Ep(nc.sbuf_tensor(_u("a1_sb"), [128, TT], F32))
            mbb = Ep(nc.sbuf_tensor(_u("a1_mb"), [128, TT], F32))
            ar["tscr"] = Ep(nc.sbuf_tensor(_u("a1_tscr"), [128, TT], F32))
            ar["tscr2"] = Ep(nc.sbuf_tensor(_u("a1_tscr2"), [128, TT], F32))
            p_mm2 = [Ep(nc.psum_tensor(_u(f"a1_pmm{i}"), [128, 512], F32))
                     for i in range(2)]
            p_v = Ep(nc.psum_tensor(_u("a1_pv"), [128, 1024], F32))
            p_kf = Ep(nc.psum_tensor(_u("a1_pkf"), [128, 512], F32))
            p_kv = Ep(nc.psum_tensor(_u("a1_pkv"), [64, 512], F32))
            kv_acc = Ep(nc.sbuf_tensor(_u("a1_kvacc"), [64, NH, 66], F32))

            vset = bb.op("dve", lambda E_: E_.memset(vtile[:], 1.0))
            kz = bb.op("dve", lambda E_: E_.memset(kv_acc[:], 0.0))
            xv = x_in.rearrange("(c p) t -> p c t", p=128)
            war_x = [None, None]
            war_z = None
            war_bc = None
            war_mm = None
            war_pv = None
            war_kf = None
            war_kfs = None
            war_kt = None
            kv_last = [None] * NH
            for tt in range(NTT):
                sl = 0
                ld = bb.dma("sp", xt[sl][:], xv[:, :, ts(tt, TT)], "a1_x0",
                            deps=[war_x[sl]])
                bc = emit_bcast2(bb, ar, p_v[:, 0:512], p_v[:, 512:1024],
                                 s_row, m_row, ts(tt, TT),
                                 [rows_done, war_bc, war_pv], sbb, mbb, TT)
                zt_toks = []
                for c in range(CH):
                    zt_toks.append(emit_apply_norm(
                        bb, ar, xt[sl][:, c, :], zt[:, c, :], mbb, sbb, None,
                        deps=[ld, war_z] + bc))
                war_x[sl] = zt_toks[-1]
                war_mm2 = [war_mm, war_mm]
                for di, (dst, wt) in enumerate(((qres, wq_), (ktile, wk_))):
                    for oc in range(CH):
                        pi = (di * CH + oc) % 2
                        mm = None
                        for kc in range(CH):
                            mm = bb.op("pe", lambda E_, kc=kc, oc=oc, w=wt,
                                       pi=pi: E_.matmul(
                                p_mm2[pi][:], w[:, kc, ts(oc, 128)],
                                zt[:, kc, :],
                                start=(kc == 0), stop=(kc == CH - 1)),
                                deps=[zt_toks[kc], war_mm2[pi], lws[-1]])
                        if dst is qres:
                            ev = bb.op("act", lambda E_, oc=oc, t0=tt, pi=pi:
                                       E_.copy(
                                qres[:, oc, ts(t0, TT)], p_mm2[pi][:]),
                                deps=[mm])
                        else:
                            ev = bb.op("act", lambda E_, oc=oc, pi=pi: E_.copy(
                                ktile[:, oc, :], p_mm2[pi][:]),
                                deps=[mm, war_kt])
                        war_mm2[pi] = ev
                war_mm = war_mm2[0] if war_mm2[1] is None else war_mm2[1]
                war_mm = ev
                v_toks = []
                for tsub in range(4):
                    mm = None
                    for kc in range(CH):
                        for j in range(2):
                            mm = bb.op("pe", lambda E_, kc=kc, j=j, t0=tsub:
                                       E_.matmul(
                                p_v[:, ts(j, 512)], zt[:, kc, ts(t0, 128)],
                                wv2[:, kc, ts(j, 512)],
                                start=(kc == 0), stop=(kc == CH - 1)),
                                deps=[zt_toks[kc], war_pv] + bc)
                    ev = bb.op("act", lambda E_, t0=tsub: E_.copy(
                        vtile[:, t0, :, 0:64],
                        p_v[:].rearrange("p (h d) -> p h d", h=NH)),
                        deps=[mm, vset, war_pv])
                    war_pv = ev
                    v_toks.append(ev)
                war_z = war_pv
                war_bc = v_toks[-1]
                pairs = [] if "nokf" in dbg else [
                    (h, tsub) for h in range(NH) for tsub in range(4)]
                for b0 in range(0, len(pairs), 4):
                    batch = pairs[b0:b0 + 4]
                    mk = None
                    for jj, (h, tsub) in enumerate(batch):
                        hb, hc = 64 * (h % 2), h // 2
                        mk = bb.op("pe", lambda E_, jj=jj, h=h, t0=tsub, hb=hb,
                                   hc=hc, nb=len(batch): E_.matmul(
                            p_kf[:, ts(jj, 64)],
                            ktile[hb:hb + 64, hc, ts(t0, 128)],
                            projp[hb:hb + 64, :],
                            start=(jj == 0), stop=(jj == nb - 1)),
                            deps=[war_mm, lp, war_kf])
                    if "norl" in dbg:
                        war_kf = mk
                        war_kfs = mk
                        war_kt = mk
                        continue
                    W = 64 * len(batch)
                    rl_ = bb.op("act", lambda E_, W=W: E_.activation(
                        kfs[:, 0:W], p_kf[:, 0:W], AF.Relu),
                        deps=[mk, war_kfs])
                    if "noep" in dbg:
                        war_kf = rl_
                        war_kfs = rl_
                        war_kt = rl_
                        continue
                    ep_ = bb.op("dve", lambda E_, W=W: E_.tensor_scalar(
                        kfb[:, 0:W], kfs[:, 0:W], 1e-6, None, ALU.add),
                        deps=[rl_, war_kfs])
                    war_kf = ep_
                    mv_toks = []
                    if "nomv" in dbg:
                        war_kfs = ep_
                        war_kt = war_kfs
                        continue
                    bh = batch[0][0]
                    for jj, (h, tsub) in enumerate(batch):
                        mv = bb.op("pe", lambda E_, jj=jj, h=h, t0=tsub,
                                   nb=len(batch): E_.matmul(
                            p_kv[:, 0:65],
                            kfb[:, ts(jj, 64)], vtile[:, t0, h, 0:65],
                            start=(jj == 0), stop=(jj == nb - 1)),
                            deps=[ep_, v_toks[tsub], kv_last[0]])
                        mv_toks.append(mv)
                    acc = bb.op("dve", lambda E_, h=bh: E_.tensor_add(
                        kv_acc[:, h, 0:65], kv_acc[:, h, 0:65],
                        p_kv[0:64, 0:65]), deps=[mv_toks[-1], kz])
                    kv_last[0] = acc
                    war_kfs = acc
                    war_pv = acc
                war_kt = war_kfs
            kv_ev = []
            if "nokf" not in dbg:
                ev = bb.op("dve", lambda E_: E_.tensor_copy(
                    kvk[:], kv_acc[:]), deps=[kv_last[0]])
                kv_ev.append(ev)
            if "dbg_kvk" in ar:
                kvf = Ep(nc.sbuf_tensor(_u("dbg_kvf"), [64, NH * 66], F32))
                ck = bb.op("dve", lambda E_: E_.tensor_copy(
                    kvf[:], kvk[:].rearrange("p a b -> p (a b)")), deps=kv_ev)
                bb.dma("sp", ar["dbg_kvk"], kvf[:], "dbg_s", deps=[ck])
            bb.fence()

        if "aonly" in dbg:
            bb.fence()
            return
        with ExitStack() as pb:
            Ep = pb.enter_context
            wo_ = Ep(nc.sbuf_tensor(_u("a1_wo"), [128, CH, D], BF16))
            lwo = []
            wv_ = ar["t_L1_attn_Wo"].rearrange("(c p) n -> p c n", p=128)
            for ic in range(CH):
                lwo.append(bb.dma("pop", wo_[:, ic, :], wv_[:, ic, :], "a1_wld"))
            bocol = Ep(nc.sbuf_tensor(_u("a1_bo"), [128, CH], F32))
            lbo = bb.dma("sp", bocol[:],
                         ar["t_L1_attn_bo"].rearrange("(c p) -> p c", p=128),
                         "ld_rows", allow_slow_non_contiguous=True)
            attnT = Ep(nc.sbuf_tensor(_u("a1_attnT"), [128, CH, TT], BF16))
            qfs2 = [Ep(nc.sbuf_tensor(_u(f"a1_qfs{i}"), [64, TT], F32))
                    for i in range(2)]
            qfb2 = [Ep(nc.sbuf_tensor(_u(f"a1_qfb{i}"), [64, TT], BF16))
                    for i in range(2)]
            nums = Ep(nc.sbuf_tensor(_u("a1_nums"), [64, TT], F32))
            rd1 = Ep(nc.sbuf_tensor(_u("a1_rd1"), [1, TT], F32))
            rd2 = Ep(nc.sbuf_tensor(_u("a1_rd2"), [1, TT], F32))
            dlt = Ep(nc.sbuf_tensor(_u("a1_dlt"), [128, TT], F32))
            xslb = [Ep(nc.sbuf_tensor(_u(f"a1_xs{i}"), [128, CH, TT], F32))
                    for i in range(2)]
            nxs = [Ep(nc.sbuf_tensor(_u(f"a1_nxs{i}"), [128, CH, TT], F32))
                   for i in range(2)]
            ar["nxb"] = Ep(nc.sbuf_tensor(_u("a1_nxb"), [128, TT], BF16))
            ar["nxq"] = Ep(nc.sbuf_tensor(_u("a1_nxq"), [128, TT], BF16))
            srow = Ep(nc.sbuf_tensor(_u("a1_srow"), [1, 2 * TT], F32))
            p_qf2 = [Ep(nc.psum_tensor(_u(f"a1_pqf{i}"), [64, TT], F32))
                     for i in range(2)]
            p_num2 = [Ep(nc.psum_tensor(_u(f"a1_pnum{i}"), [65, TT], F32))
                      for i in range(2)]
            p_rb = Ep(nc.psum_tensor(_u("a1_prb"), [64, TT], F32))
            p_wo = [Ep(nc.psum_tensor(_u(f"a1_pwo{i}"), [128, 512], F32))
                    for i in range(2)]
            p_st1 = Ep(nc.psum_tensor(_u("a1_pst1"), [128, TT], F32))
            p_sx = p_st1[0:1, :]
            p_sq = p_st1[32:33, :]

            xrv = x_in.rearrange("(c p) t -> p c t", p=128)
            ov = x_out.rearrange("(c p) t -> p c t", p=128)
            war_xs = [None, None]
            war_nxs = [None, None]
            war_qf = None
            war_qfs = None
            war_qfb = None
            war_num2 = [None, None]
            war_d1 = None
            war_rd2 = None
            war_nums = None
            war_rb = None
            war_attnT = None
            war_wo = [None, None]
            war_dlt = None
            def emit_qf(h, qt):
                nonlocal war_qf, war_qfs, war_qfb
                hb, hc, pi = 64 * (h % 2), h // 2, h % 2
                mq = bb.op("pe", lambda E_, hb=hb, hc=hc, q=qt, pi=pi:
                           E_.matmul(
                    p_qf2[pi][:], projp[hb:hb + 64, :],
                    qres[hb:hb + 64, hc, ts(q, TT)],
                    start=True, stop=True), deps=[lp, war_qf[pi]] + kv_ev)
                rl_ = bb.op("act", lambda E_, pi=pi: E_.activation(
                    qfs2[pi][:], p_qf2[pi][:], AF.Relu),
                    deps=[mq, war_qfs[pi]])
                war_qf[pi] = rl_
                ep_ = bb.op("dve", lambda E_, pi=pi: E_.tensor_scalar(
                    qfb2[pi][:], qfs2[pi][:], 1e-6, None, ALU.add),
                    deps=[rl_, war_qfb[pi]])
                war_qfs[pi] = ep_
                return ep_

            war_qf = [None, None]
            war_qfs = [None, None]
            war_qfb = [None, None]
            qf_tok = [None] * NH
            for qt in range(NTT):
                qf_tok[0] = emit_qf(0, qt)
                for h in range(NH):
                    hb, hc, pi = 64 * (h % 2), h // 2, h % 2
                    if h + 1 < NH:
                        qf_tok[h + 1] = emit_qf(h + 1, qt)
                    mn = bb.op("pe", lambda E_, h=h, pi=pi: E_.matmul(
                        p_num2[pi][:], kvk[:, h, 0:65], qfb2[pi][:],
                        start=True, stop=True),
                        deps=[qf_tok[h], war_num2[pi]] + kv_ev)
                    war_qfb[pi] = mn
                    d1 = bb.op("dve", lambda E_, pi=pi: E_.tensor_scalar(
                        rd1[:], p_num2[pi][64:65, :], 1e-6, None, ALU.add),
                        deps=[mn, war_d1])
                    war_d1 = d1
                    d2 = bb.op("dve", lambda E_: E_.reciprocal(rd2[:], rd1[:]),
                               deps=[d1, war_rd2])
                    rb = bb.op("pe", lambda E_: E_.matmul(
                        p_rb[:], ar["onesr"][:, 0:64], rd2[:],
                        start=True, stop=True), deps=[d2, war_rb])
                    war_rd2 = rb
                    nev = bb.op("act", lambda E_, pi=pi: E_.copy(
                        nums[:], p_num2[pi][0:64, :]),
                        deps=[mn, d1, war_nums])
                    war_num2[pi] = nev
                    dm = bb.op("dve", lambda E_, h=h, hb=hb, hc=hc: E_.tensor_mul(
                        attnT[hb:hb + 64, hc, :], nums[:], p_rb[:]),
                        deps=[nev, rb, war_attnT])
                    war_rb = dm
                    war_nums = dm
                head_done = war_rb
                xls = qt % 2
                lx = bb.dma("sp", xslb[xls][:], xrv[:, :, ts(qt, TT)],
                            f"a1_xr{xls}", deps=[war_xs[xls]])
                ep_toks = []
                for oc in range(CH):
                    psl = oc % 2
                    mm = None
                    for ic in range(CH):
                        mm = bb.op("pe", lambda E_, ic=ic, oc=oc, p=psl:
                                   E_.matmul(
                            p_wo[p][:], wo_[:, ic, ts(oc, 128)],
                            attnT[:, ic, :], start=(ic == 0),
                            stop=(ic == CH - 1)),
                            deps=[head_done, lwo[ic], war_wo[psl]])
                    e1 = bb.op("act", lambda E_, oc=oc, p=psl: E_.activation(
                        dlt[:], p_wo[p][:], AF.Identity, bias=bocol[:, oc:oc + 1]),
                        deps=[mm, lbo, war_dlt])
                    e2 = bb.op("pop", lambda E_, oc=oc, x=xls: E_.tensor_add(
                        nxs[x][:, oc, :], xslb[x][:, oc, :], dlt[:]),
                        deps=[e1, lx, war_nxs[xls]])
                    war_wo[psl] = e1
                    war_dlt = e2
                    ep_toks.append(e2)
                    if compute_stats_out:
                        emit_out_stats(bb, ar, nxs[xls][:, oc, :], TT,
                                       p_sx, p_sq, oc == 0, oc == CH - 1,
                                       [e2], srow, stats_out, ts(qt, TT),
                                       "a1_sst", shared=True)
                war_xs[xls] = ep_toks[-1]
                war_attnT = ep_toks[-1]
                st = bb.dma("sp", ov[:, :, ts(qt, TT)], nxs[xls][:], "st_x",
                            deps=ep_toks)
                war_nxs[xls] = st
            bb.fence()



SHAPES = {
    "L0_attn_gamma": [D], "L0_attn_w_qkv": [D, 3 * D], "L0_attn_w_out": [D, D],
    "L0_ffn_weight": [D, D], "L0_ffn_alpha": [1], "L0_ffn_bias": [D],
    "L1_attn_Wq": [D, D], "L1_attn_Wk": [D, D], "L1_attn_Wv": [D, D],
    "L1_attn_proj": [64, 64], "L1_attn_Wo": [D, D], "L1_attn_bo": [D],
    "L1_ffn_W_r": [D, D], "L1_ffn_W_k": [D, D], "L1_ffn_W_v": [D, D],
    "L1_ffn_mu_r": [D], "L1_ffn_mu_k": [D],
}

SUB_WEIGHTS = {
    1: ["L0_ffn_weight", "L0_ffn_alpha", "L0_ffn_bias"],
    2: ["L0_attn_gamma", "L0_attn_w_qkv", "L0_attn_w_out"],
    3: ["L0_ffn_weight", "L0_ffn_alpha", "L0_ffn_bias"],
    4: ["L1_ffn_W_r", "L1_ffn_W_k", "L1_ffn_W_v", "L1_ffn_mu_r", "L1_ffn_mu_k"],
    5: ["L1_attn_Wq", "L1_attn_Wk", "L1_attn_Wv", "L1_attn_proj",
        "L1_attn_Wo", "L1_attn_bo"],
    6: ["L1_ffn_W_r", "L1_ffn_W_k", "L1_ffn_W_v", "L1_ffn_mu_r", "L1_ffn_mu_k"],
}

EMITTERS = {1: emit_ffn0, 2: emit_attn0, 3: emit_ffn0,
            4: emit_rwkv, 5: emit_attn1, 6: emit_rwkv}


def build_group(subs):
    nc = bass.Bass("TRN2", target_bir_lowering=False)
    stack = ExitStack()
    bb = BB(nc, stack)
    ar = {"E": stack.enter_context}

    names = []
    for s in subs:
        for w in SUB_WEIGHTS[s]:
            if w not in names:
                names.append(w)
    for w in names:
        ar["t_" + w] = nc.dram_tensor(w, SHAPES[w], F32, kind="ExternalInput").ap()
    x_in = nc.dram_tensor("x_in", [D, T], F32, kind="ExternalInput").ap()
    x_out = nc.dram_tensor("x_out", [D, T], F32, kind="ExternalOutput").ap()
    xa = nc.dram_tensor("x_buf_a", [D, T], F32, kind="Internal").ap()
    xb = nc.dram_tensor("x_buf_b", [D, T], F32, kind="Internal").ap()
    st_a = nc.dram_tensor("stats_a", [2, T], F32, kind="Internal").ap()
    st_b = nc.dram_tensor("stats_b", [2, T], F32, kind="Internal").ap()
    if 1 in subs or 3 in subs:
        ar["weff_hbm"] = nc.dram_tensor("weff_hbm", [128, CH * D], BF16,
                                        kind="Internal").ap()
        ar["wmeta_hbm"] = nc.dram_tensor("wmeta_hbm", [128, 2], F32,
                                         kind="Internal").ap()
    if 2 in subs:
        ar["q_hbm"] = nc.dram_tensor("q_hbm", [CH, 128, T], BF16,
                                     kind="Internal").ap()
        ar["k_hbm"] = nc.dram_tensor("k_hbm", [CH, 128, T], BF16,
                                     kind="Internal").ap()

    with stack:
        emit_consts(bb, ar)
        n = len(subs)
        bufs = [x_in]
        for i in range(n - 1):
            bufs.append(xa if i % 2 == 0 else xb)
        bufs.append(x_out)
        stats = [st_a, st_b]
        emit_input_stats(bb, ar, x_in, st_a)
        for i, s in enumerate(subs):
            ar.pop("stat_war", None)
            ar.pop("stat_war2", None)
            ar.pop("stat_war3", None)
            if s == 3 and 1 in subs[:i]:
                ar["ffn0_reuse"] = True
            EMITTERS[s](bb, ar, bufs[i], bufs[i + 1], stats[i % 2],
                        stats[(i + 1) % 2], i < n - 1)
        final = [(s, v) for s, v in bb.cnt.items() if s == "st_x"]
        bb.finalize(final)
    return nc, names


_PROGRAM_CACHE = {}


def _get_program(subs):
    key = tuple(subs)
    if key not in _PROGRAM_CACHE:
        _PROGRAM_CACHE[key] = build_group(list(subs))
    return _PROGRAM_CACHE[key]


GROUPS = [[1, 2, 3, 4, 5, 6]]


def kernel(**inputs):
    x = np.ascontiguousarray(inputs["L0_attn_x"], dtype=np.float32)
    cur = [np.ascontiguousarray(x[i].T) for i in range(NCORES)]
    for subs in GROUPS:
        nc, names = _get_program(tuple(subs))
        in_maps = []
        for i in range(NCORES):
            m = {w: np.ascontiguousarray(inputs[w], dtype=np.float32)
                 for w in names}
            m["x_in"] = cur[i]
            in_maps.append(m)
        res = run_bass_kernel_spmd(nc, in_maps, core_ids=list(range(NCORES)))
        cur = [np.asarray(r["x_out"]) for r in res.results]
    out = np.stack([c.T for c in cur], axis=0)
    return np.ascontiguousarray(out, dtype=np.float32)


# version 14
# speedup vs baseline: 1.0761x; 1.0264x over previous
import os
import numpy as np
from contextlib import ExitStack

import concourse.bass as bass
import concourse.mybir as mybir
from concourse.bass_utils import run_bass_kernel_spmd

F32 = mybir.dt.float32
BF16 = mybir.dt.bfloat16
AF = mybir.ActivationFunctionType
ALU = mybir.AluOpType
AX = mybir.AxisListType

NCORES = 8
T = 4096
D = 1024
CH = 8
TT = 512
NTT = T // TT
NH = 16
DH = 64

ENGS = ("pe", "act", "dve", "pop", "sp")


class BB:

    def __init__(self, nc, stack):
        self.nc = nc
        self.stack = stack
        self.q = {e: [] for e in ENGS}
        self.sems = {}
        self.cnt = {}
        self.hi = {e: {} for e in ENGS}
        self.pend = {e: [] for e in ENGS}
        self.same_sync = bool(int(os.environ.get("BB_SAME_ENGINE_SYNC", "1")))
        for e in ENGS:
            self._mksem("g_" + e)

    def _mksem(self, name):
        if name not in self.sems:
            self.sems[name] = self.stack.enter_context(self.nc.semaphore(name))
            self.cnt[name] = 0
        return self.sems[name]

    def _waits(self, eng, deps):
        out = []
        for d in deps:
            if d is None:
                continue
            s, v = d
            if s == "g_" + eng and not self.same_sync:
                continue
            if self.hi[eng].get(s, 0) >= v:
                continue
            self.hi[eng][s] = v
            out.append((s, v))
        return out

    def op(self, eng, fn, deps=()):
        deps = list(deps) + self.pend[eng]
        self.pend[eng] = []
        waits = self._waits(eng, deps)
        s = "g_" + eng
        self.cnt[s] += 1
        tok = (s, self.cnt[s])
        self.q[eng].append((fn, waits, [(s, 1)]))
        return tok

    def dma(self, eng, out_ap, in_ap, sem, deps=(), **kw):
        deps = list(deps) + self.pend[eng]
        self.pend[eng] = []
        waits = self._waits(eng, deps)
        self._mksem(sem)
        self.cnt[sem] += 16
        tok = (sem, self.cnt[sem])

        def fn(E, out_ap=out_ap, in_ap=in_ap, kw=kw):
            return E.dma_start(out_ap, in_ap, **kw)

        self.q[eng].append((fn, waits, [(sem, 16)]))
        return tok

    def fence(self, toks=None):
        if toks is None:
            toks = [(s, v) for s, v in self.cnt.items() if v > 0]
        for e in ENGS:
            self.pend[e] = list({*self.pend[e], *toks})

    def finalize(self, final_toks):
        nc = self.nc
        waits = self._waits("sp", list(final_toks))
        if waits:
            self.q["sp"].append((None, waits, []))
        with nc.Block() as block:
            emap = dict(pe=block.tensor, act=block.scalar, dve=block.vector,
                        pop=block.gpsimd, sp=block.sync)
            for e in ENGS:
                items = self.q[e]
                if not items:
                    continue

                def body(E, items=items):
                    for fn, ws, incs in items:
                        for s, v in ws:
                            E.wait_ge(self.sems[s], v)
                        if fn is None:
                            continue
                        ins = fn(E)
                        for s, amt in incs:
                            ins = ins.then_inc(self.sems[s], amt)

                emap[e](body)



_NAME_N = [0]


def _u(n):
    _NAME_N[0] += 1
    return f"{n}_{_NAME_N[0]}"


def ts(i, n):
    return slice(i * n, (i + 1) * n)



def emit_consts(bb, ar):
    nc = bb.nc
    import ml_dtypes
    ident_np = np.eye(128, dtype=np.float32).astype(ml_dtypes.bfloat16)
    onesc_np = np.ones((128, 1), dtype=np.float32).astype(ml_dtypes.bfloat16)
    onesr_np = np.ones((1, 128), dtype=np.float32)
    cvals_np = np.array([[1e-5, 1e-30, 0.0, 0.0]], dtype=np.float32)
    t_cv = nc.inline_tensor(cvals_np, "c_cvals").ap()
    t_id = nc.inline_tensor(ident_np, "c_ident").ap()
    t_oc = nc.inline_tensor(onesc_np, "c_onesc").ap()
    t_or = nc.inline_tensor(onesr_np, "c_onesr").ap()
    E = ar["E"]
    ar["ident"] = E(nc.sbuf_tensor(_u("ident"), [128, 128], BF16))
    ar["onesc"] = E(nc.sbuf_tensor(_u("onesc"), [128, 1], BF16))
    ar["onesr"] = E(nc.sbuf_tensor(_u("onesr"), [1, 128], F32))
    ar["cvals"] = E(nc.sbuf_tensor(_u("cvals"), [1, 4], F32))
    ar["onescf"] = E(nc.sbuf_tensor(_u("onescf"), [128, 1], F32))
    t0 = bb.dma("sp", ar["cvals"][:], t_cv, "ld_const")
    tcf = nc.inline_tensor(np.ones((128, 1), dtype=np.float32), "c_onescf").ap()
    t4 = bb.dma("sp", ar["onescf"][:], tcf, "ld_const")
    t1 = bb.dma("sp", ar["ident"][:], t_id, "ld_const")
    t2 = bb.dma("sp", ar["onesc"][:], t_oc, "ld_const")
    t3 = bb.dma("sp", ar["onesr"][:], t_or, "ld_const")
    ar["const_tok"] = [t0, t1, t2, t3, t4]


def stat_rows_store(bb, ar, srow, stats_hbm, tok_slice, sem, deps, width):
    s1 = bb.dma("sp", stats_hbm[0:1, tok_slice], srow[:, 0:width], sem, deps=deps)
    s2 = bb.dma("sp", stats_hbm[1:2, tok_slice], srow[:, width:2 * width], sem,
                deps=deps)
    return s2


def emit_input_stats(bb, ar, x_hbm, stats_hbm):
    nc = bb.nc
    with ExitStack() as sc:
        Es = sc.enter_context
        xt = [Es(nc.sbuf_tensor(_u(f"ist_x{i}"), [128, CH, TT], F32)) for i in range(2)]
        xb = Es(nc.sbuf_tensor(_u("ist_xb"), [128, CH, TT], BF16))
        xq = Es(nc.sbuf_tensor(_u("ist_xq"), [128, CH, TT], BF16))
        srow = Es(nc.sbuf_tensor(_u("ist_srow"), [1, 2 * TT], F32))
        p_sx = Es(nc.psum_tensor(_u("ist_psx"), [1, TT], F32))
        p_sq = Es(nc.psum_tensor(_u("ist_psq"), [1, TT], F32))
        xv = x_hbm.rearrange("(c p) t -> p c t", p=128)
        war = [None, None]
        mm_last = None
        ev_last = None
        st_tok = None
        for tt in range(NTT):
            sl = tt % 2
            ld = bb.dma("sp", xt[sl][:], xv[:, :, ts(tt, TT)], f"ist_x{sl}",
                        deps=[war[sl]])
            c1 = bb.op("act", lambda E_, s=sl: E_.copy(xb[:], xt[s][:]),
                       deps=[ld, mm_last])
            c2 = bb.op("act", lambda E_, s=sl: E_.activation(
                xq[:], xt[s][:], AF.Square), deps=[ld, mm_last])
            war[sl] = c2
            mt = mq = None
            for c in range(CH):
                mt = bb.op("pe", lambda E_, c=c: E_.matmul(
                    p_sx[:], ar["onesc"][:], xb[:, c, :],
                    start=(c == 0), stop=(c == CH - 1)),
                    deps=[c1, ev_last] + (ar["const_tok"] if tt == 0 else []))
            for c in range(CH):
                mq = bb.op("pe", lambda E_, c=c: E_.matmul(
                    p_sq[:], ar["onesc"][:], xq[:, c, :],
                    start=(c == 0), stop=(c == CH - 1)), deps=[c2, ev_last])
            mm_last = mq
            e1 = bb.op("dve", lambda E_: E_.tensor_copy(srow[:, 0:TT], p_sx[:]),
                       deps=[mt, st_tok])
            e2 = bb.op("dve", lambda E_: E_.tensor_copy(srow[:, TT:2 * TT], p_sq[:]),
                       deps=[mq, st_tok])
            ev_last = e2
            st_tok = stat_rows_store(bb, ar, srow, stats_hbm, ts(tt, TT),
                                     "ist_st", [e1, e2], TT)
        bb.fence()


def emit_rows_prologue(bb, ar, sc, stats_in, want_rms_gamma=False):
    nc = bb.nc
    Es = sc.enter_context
    m_row = Es(nc.sbuf_tensor(_u("m_row"), [1, T], F32))
    s_row = Es(nc.sbuf_tensor(_u("scl_row"), [1, T], F32))
    with ExitStack() as tsc:
        Et = tsc.enter_context
        r0 = Et(nc.sbuf_tensor(_u("rtmp0"), [1, T], F32))
        r1 = Et(nc.sbuf_tensor(_u("rtmp1"), [1, T], F32))
        r2 = Et(nc.sbuf_tensor(_u("rtmp2"), [1, T], F32))
        l0 = bb.dma("sp", r0[:], stats_in[0:1, :], "ld_rows")
        l1 = bb.dma("sp", r1[:], stats_in[1:2, :], "ld_rows")
        t1 = bb.op("dve", lambda E_: E_.tensor_scalar(
            m_row[:], r0[:], 1.0 / D, None, ALU.mult), deps=[l0])
        t2 = bb.op("dve", lambda E_: E_.tensor_mul(r0[:], m_row[:], m_row[:]),
                   deps=[t1])
        t3 = bb.op("dve", lambda E_: E_.scalar_tensor_tensor(
            r1[:], r1[:], 1.0 / D, r0[:], ALU.mult, ALU.subtract), deps=[t2, l1])
        t4 = bb.op("act", lambda E_: E_.activation(
            r2[:], r1[:], AF.Ln, bias=ar["cvals"][0:1, 0:1]),
            deps=[t3] + ar["const_tok"])
        t5 = bb.op("act", lambda E_: E_.activation(r0[:], r2[:], AF.Exp, scale=-0.5),
                   deps=[t4])
        if not want_rms_gamma:
            done = bb.op("dve", lambda E_: E_.tensor_copy(s_row[:], r0[:]),
                         deps=[t5])
        else:
            u1 = bb.op("act", lambda E_: E_.activation(
                r2[:], r1[:], AF.Ln, bias=ar["cvals"][0:1, 1:2],
                scale=float(D)), deps=[t3, t4])
            u2 = bb.op("act", lambda E_: E_.activation(
                r1[:], r2[:], AF.Exp, scale=0.5), deps=[u1])
            u3 = bb.op("dve", lambda E_: E_.tensor_mul(r1[:], r1[:], r0[:]),
                       deps=[u2, t5])
            u4 = bb.op("dve", lambda E_: E_.tensor_scalar(
                r1[:], r1[:], 1e-12, None, ALU.max), deps=[u3])
            u5 = bb.op("dve", lambda E_: E_.reciprocal(r2[:], r1[:]), deps=[u4])
            done = bb.op("dve", lambda E_: E_.scalar_tensor_tensor(
                s_row[:], r0[:], 32.0, r2[:], ALU.mult, ALU.mult), deps=[u5])
        bb.fence()
    return m_row, s_row, done


def emit_bcast2(bb, ar, p_a, p_b_, row_a, row_b, tslice, deps, sb_a, sb_b, width):
    out = []
    for p_dst, row, sb in ((p_a, row_a, sb_a), (p_b_, row_b, sb_b)):
        m = None
        for j in range(width // 512):
            m = bb.op("pe", lambda E_, p=p_dst, r=row, j=j: E_.matmul(
                p[:, ts(j, 512)], ar["onesr"][:],
                r[0:1, tslice][:, ts(j, 512)], start=True, stop=True), deps=deps)
        e = bb.op("act", lambda E_, p=p_dst, sb=sb: E_.copy(sb[:], p[:]),
                  deps=[m])
        out.append(e)
    return out


def emit_out_stats(bb, ar, src_f32, width, p_sx, p_sq, first, last, deps,
                   srow, stats_hbm, tok_slice, st_sem, shared=False):
    nxb, nxq = ar["nxb"], ar["nxq"]
    w = ar.get("stat_war")
    c1 = bb.op("act", lambda E_: E_.copy(nxb[:, 0:width], src_f32),
               deps=list(deps) + [w])
    c2 = bb.op("act", lambda E_: E_.activation(nxq[:, 0:width], src_f32, AF.Square),
               deps=list(deps) + [w])
    mt = mq = None
    for j in range(width // 512):
        mt = bb.op("pe", lambda E_, j=j: E_.matmul(
            p_sx[:, ts(j, 512)], ar["onesc"][:], nxb[:, ts(j, 512)],
            start=first, stop=last), deps=[c1, ar.get("stat_war2")])
        mq = bb.op("pe", lambda E_, j=j: E_.matmul(
            p_sq[:, ts(j, 512)], ar["onesc"][:], nxq[:, ts(j, 512)],
            start=(first and not shared), stop=last,
            skip_group_check=shared), deps=[c2, ar.get("stat_war2")])
    ar["stat_war"] = mq
    ret = mq
    if last:
        e1 = bb.op("dve", lambda E_: E_.tensor_copy(srow[:, 0:width], p_sx[:]),
                   deps=[mt, ar.get("stat_war3")])
        e2 = bb.op("dve", lambda E_: E_.tensor_copy(
            srow[:, width:2 * width], p_sq[:]), deps=[mq, ar.get("stat_war3")])
        st = stat_rows_store(bb, ar, srow, stats_hbm, tok_slice, st_sem,
                             [e1, e2], width)
        ar["stat_war2"] = e2
        ar["stat_war3"] = st
        ret = e2
    return ret


def emit_stats_batch(bb, ar, src, twidth, p_sx, p_sq, deps,
                     srow, stats_hbm, tok_slice, st_sem, shared=False,
                     src_hbm=None):
    nxb, nxq = ar["nxb"], ar["nxq"]
    nbc = ar.get("nxb_ch", CH)
    nj = twidth // 512
    mt = mq = None
    for b0 in range(0, CH, nbc):
        cs = slice(b0, b0 + nbc)
        W = nbc * twidth
        if src_hbm is not None:
            sv = (src_hbm[:, b0, tok_slice] if nbc == 1
                  else src_hbm[:, cs, tok_slice])
            dv = (nxb[:, 0:W] if nbc == 1
                  else nxb[:, 0:W].rearrange("p (c t) -> p c t", c=nbc))
            c1 = bb.dma("pop", dv, sv, "sb_ld",
                        deps=list(deps) + [ar.get("stat_war")])
            c2 = bb.op("act", lambda E_, W=W: E_.activation(
                nxq[:, 0:W], nxb[:, 0:W], AF.Square),
                deps=[c1, ar.get("stat_war")])
        else:
            c1 = bb.op("act", lambda E_, cs=cs, W=W: E_.copy(
                nxb[:, 0:W], src[:, cs, :].rearrange("p c t -> p (c t)")),
                deps=list(deps) + [ar.get("stat_war")])
            c2 = bb.op("act", lambda E_, cs=cs, W=W: E_.activation(
                nxq[:, 0:W], src[:, cs, :].rearrange("p c t -> p (c t)"),
                AF.Square), deps=list(deps) + [ar.get("stat_war")])
        for ci in range(nbc):
            c = b0 + ci
            for j in range(nj):
                o0 = ci * twidth + j * 512
                mt = bb.op("pe", lambda E_, o0=o0, j=j, c=c: E_.matmul(
                    p_sx[:, ts(j, 512)], ar["onesc"][:],
                    nxb[:, o0:o0 + 512],
                    start=(c == 0), stop=(c == CH - 1)),
                    deps=[c1, ar.get("stat_war2")])
                mq = bb.op("pe", lambda E_, o0=o0, j=j, c=c: E_.matmul(
                    p_sq[:, ts(j, 512)], ar["onesc"][:],
                    nxq[:, o0:o0 + 512],
                    start=(c == 0 and not shared), stop=(c == CH - 1),
                    skip_group_check=shared),
                    deps=[c2, ar.get("stat_war2")])
        ar["stat_war"] = mq
    e1 = bb.op("dve", lambda E_: E_.tensor_copy(srow[:, 0:twidth], p_sx[:]),
               deps=[mt, ar.get("stat_war3")])
    e2 = bb.op("dve", lambda E_: E_.tensor_copy(
        srow[:, twidth:2 * twidth], p_sq[:]), deps=[mq, ar.get("stat_war3")])
    st = stat_rows_store(bb, ar, srow, stats_hbm, tok_slice, st_sem,
                         [e1, e2], twidth)
    ar["stat_war2"] = e2
    ar["stat_war3"] = st
    return e2


def emit_apply_norm(bb, ar, x_view, z_view, mb, sb, gcol_slice, deps):
    i = ar.get("tscr_i", 0)
    t = ar["tscr"] if i == 0 else ar["tscr2"]
    war = "tscr_war" if i == 0 else "tscr_war2"
    ar["tscr_i"] = 1 - i
    s1 = bb.op("pop", lambda E_: E_.tensor_sub(t[:], x_view, mb[:]),
               deps=list(deps) + [ar.get(war)])
    if gcol_slice is not None:
        s2 = bb.op("dve", lambda E_: E_.scalar_tensor_tensor(
            z_view, t[:], gcol_slice, sb[:], ALU.mult, ALU.mult), deps=[s1])
    else:
        s2 = bb.op("dve", lambda E_: E_.tensor_mul(z_view, t[:], sb[:]),
                   deps=[s1])
    ar[war] = s2
    return s2



def emit_ffn0(bb, ar, x_in, x_out, stats_in, stats_out, compute_stats_out):
    nc = bb.nc
    t_w = ar["t_L0_ffn_weight"]
    t_alpha = ar["t_L0_ffn_alpha"]
    t_bias = ar["t_L0_ffn_bias"]

    reuse = ar.pop("ffn0_reuse", False)
    with ExitStack() as sc:
        Es = sc.enter_context
        m_row, s_row, rows_done = emit_rows_prologue(bb, ar, sc, stats_in)
        wT = Es(nc.sbuf_tensor(_u("f0_wT"), [128, CH, D], BF16))
        acol = Es(nc.sbuf_tensor(_u("f0_acol"), [128, 1], F32))
        hbias = Es(nc.sbuf_tensor(_u("f0_hbias"), [128, CH], F32))
        if "dbg_wt" in ar:
            ar["dbg_dcol_t"] = Es(nc.sbuf_tensor(_u("dbg_dc"), [128, 4], F32))

        if reuse:
            lwt = bb.dma("sp", wT[:].rearrange("p c d -> p (c d)"),
                         ar["weff_hbm"], "f0_wre")
            lac = bb.dma("sp", acol[:], ar["wmeta_hbm"][:, 0:1], "f0_wre",
                         allow_slow_non_contiguous=True)
            tr_last = None
            b4 = lac
            bb.fence([lwt, lac])
        if not reuse:
            with ExitStack() as tsc:
                Et = tsc.enter_context
                wch = [Et(nc.sbuf_tensor(_u(f"f0_w{i}"), [128, D], F32)) for i in range(2)]
                absb = Et(nc.sbuf_tensor(_u("f0_absb"), [128, D], BF16))
                scr = Et(nc.sbuf_tensor(_u("f0_scr"), [128, D], F32))
                wtn = Et(nc.sbuf_tensor(_u("f0_wtn"), [128, D], BF16))
                wtn2 = Et(nc.sbuf_tensor(_u("f0_wtn2"), [128, D], BF16))
                drow = Et(nc.sbuf_tensor(_u("f0_drow"), [1, 4], F32))
                dcol = Et(nc.sbuf_tensor(_u("f0_dcol"), [128, 1], F32))
                a11 = Et(nc.sbuf_tensor(_u("f0_a11"), [1, 1], F32))
                p_sum = Et(nc.psum_tensor(_u("f0_psum"), [1, 512], F32))
                p_bc = Et(nc.psum_tensor(_u("f0_pbc"), [128, 1], F32))
                p_tr = Et(nc.psum_tensor(_u("f0_ptr"), [128, 128], BF16))
                wv = t_w.rearrange("(c p) d -> p c d", p=128)

                war = [None, None]
                mprev = None
                for oc in range(CH):
                    sl = oc % 2
                    ld = bb.dma("pop", wch[sl][:], wv[:, oc, :], f"f0_w{sl}",
                                deps=[war[sl]])
                    ab = bb.op("act", lambda E_, s=sl: E_.activation(
                        scr[:], wch[s][:], AF.Abs),
                        deps=[ld, mprev] + ar["const_tok"])
                    war[sl] = ab
                    for j in range(2):
                        mprev = bb.op("pe", lambda E_, j=j, oc=oc: E_.matmul(
                            p_sum[:], ar["onescf"][:], scr[:, ts(j, 512)],
                            start=(oc == 0 and j == 0),
                            stop=(oc == CH - 1 and j == 1)), deps=[ab])
                r1 = bb.op("dve", lambda E_: E_.tensor_reduce(
                    drow[0:1, 0:1], p_sum[:], AX.X, ALU.add), deps=[mprev])
                r2 = bb.op("dve", lambda E_: E_.tensor_scalar(
                    drow[0:1, 1:2], drow[0:1, 0:1], 0.7 / (D * D), None, ALU.mult),
                    deps=[r1])
                b1 = bb.op("pe", lambda E_: E_.matmul(
                    p_bc[:], ar["onesr"][:], drow[0:1, 1:2], start=True, stop=True),
                    deps=[r2])
                b2 = bb.op("dve", lambda E_: E_.tensor_copy(dcol[:], p_bc[:]),
                           deps=[b1])
                la = bb.dma("sp", a11[:], t_alpha.rearrange("(a o) -> a o", o=1),
                            "ld_rows")
                b3 = bb.op("pe", lambda E_: E_.matmul(
                    p_bc[:], ar["onesr"][:], a11[:], start=True, stop=True),
                    deps=[la, b2])
                b4 = bb.op("dve", lambda E_: E_.tensor_scalar(
                    acol[:], p_bc[:], 0.5, None, ALU.mult), deps=[b3])

                if "dbg_wt" in ar:
                    bb.op("dve", lambda E_: E_.tensor_copy(
                        ar["dbg_dcol_t"][:, 0:1], dcol[:]), deps=[b2])
                    bb.op("dve", lambda E_: E_.tensor_copy(
                        ar["dbg_dcol_t"][0:1, 1:4].rearrange("a b -> a b"),
                        drow[0:1, 0:3]), deps=[b2])
                war = [r1, r1]
                war_ptr = None
                tr_last = None
                prev_a2 = prev_a4 = prev_tr = None
                for oc in range(CH):
                    sl = oc % 2
                    ld = bb.dma("pop", wch[sl][:], wv[:, oc, :], f"f0_w{sl}",
                                deps=[war[sl]])
                    a1 = bb.op("act", lambda E_, s=sl: E_.activation(
                        scr[:], wch[s][:], AF.Abs), deps=[ld, b2, prev_a2])
                    a2 = bb.op("dve", lambda E_: E_.tensor_scalar(
                        absb[:], scr[:], dcol[:], None, ALU.is_gt),
                        deps=[a1, b2, prev_a4])
                    a3 = bb.op("act", lambda E_, s=sl: E_.sign(wtn[:], wch[s][:]),
                               deps=[a1, prev_tr])
                    war[sl] = a3
                    a4 = bb.op("dve", lambda E_: E_.tensor_mul(
                        wtn2[:], wtn[:], absb[:]), deps=[a3, a2, prev_tr])
                    prev_a2, prev_a4 = a2, a4
                    for dc in range(CH):
                        tr = bb.op("pe", lambda E_, dc=dc: E_.transpose(
                            p_tr[:], wtn2[:, ts(dc, 128)], ar["ident"][:]),
                            deps=[a4, war_ptr])
                        ev = bb.op("act", lambda E_, dc=dc, oc=oc: E_.copy(
                            wT[:, dc, ts(oc, 128)], p_tr[:]), deps=[tr])
                        war_ptr = ev
                        tr_last = ev
                    prev_tr = tr_last
                bb.fence()

            if "dbg_wt" in ar:
                wtf = Es(nc.sbuf_tensor(_u("dbg_wtf"), [128, CH * D], F32))
                ct = bb.op("dve", lambda E_: E_.tensor_copy(
                    wtf[:], wT[:].rearrange("p c d -> p (c d)")), deps=[tr_last])
                bb.dma("sp", ar["dbg_wt"], wtf[:], "dbg_s", deps=[ct])
                bb.dma("sp", ar["dbg_dcol"], ar["dbg_dcol_t"][:], "dbg_s",
                       deps=[tr_last])
            sw = bb.dma("sp", ar["weff_hbm"],
                        wT[:].rearrange("p c d -> p (c d)"), "f0_wsp",
                        deps=[tr_last])
            sa = bb.dma("sp", ar["wmeta_hbm"][:, 0:1], acol[:], "f0_wsp",
                        deps=[b4], allow_slow_non_contiguous=True)
        lb = bb.dma("sp", hbias[:], t_bias.rearrange("(c p) -> p c", p=128),
                    "ld_rows", allow_slow_non_contiguous=True)
        hb = bb.op("dve", lambda E_: E_.tensor_scalar(
            hbias[:], hbias[:], 0.5, None, ALU.mult), deps=[lb])

        xt = [Es(nc.sbuf_tensor(_u(f"f0_x{i}"), [128, CH, TT], F32)) for i in range(2)]
        zt = Es(nc.sbuf_tensor(_u("f0_z"), [128, CH, TT], BF16))
        nx = [Es(nc.sbuf_tensor(_u(f"f0_nx{i}"), [128, CH, TT], F32)) for i in range(2)]
        sbb = Es(nc.sbuf_tensor(_u("f0_sb"), [128, TT], F32))
        mbb = Es(nc.sbuf_tensor(_u("f0_mb"), [128, TT], F32))
        dlt2 = [Es(nc.sbuf_tensor(_u(f"f0_dlt{i}"), [128, TT], F32))
                for i in range(2)]
        ar["tscr"] = Es(nc.sbuf_tensor(_u("f0_tscr"), [128, TT], F32))
        ar["tscr2"] = Es(nc.sbuf_tensor(_u("f0_tscr2"), [128, TT], F32))
        ar["nxb"] = Es(nc.sbuf_tensor(_u("f0_nxb"), [128, CH * TT], BF16))
        ar["nxq"] = Es(nc.sbuf_tensor(_u("f0_nxq"), [128, CH * TT], BF16))
        ar["nxb_ch"] = CH
        srow = Es(nc.sbuf_tensor(_u("f0_srow"), [1, 2 * TT], F32))
        p_mm = [Es(nc.psum_tensor(_u(f"f0_pm{i}"), [128, 512], F32)) for i in range(2)]
        p_ba = Es(nc.psum_tensor(_u("f0_pba"), [128, 512], F32))
        p_bb2 = Es(nc.psum_tensor(_u("f0_pbb"), [128, 512], F32))
        p_sx = Es(nc.psum_tensor(_u("f0_psx"), [1, TT], F32))
        p_sq = Es(nc.psum_tensor(_u("f0_psq"), [1, TT], F32))

        xv = x_in.rearrange("(c p) t -> p c t", p=128)
        ov = x_out.rearrange("(c p) t -> p c t", p=128)
        war_x = [None, None]
        war_nx = [None, None]
        war_z = None
        war_bc = None
        war_mm = [None, None]
        war_dlt = None
        stats_job = None
        ld_f0 = {0: bb.dma("sp", xt[0][:], xv[:, :, ts(0, TT)], "f0_x0")}
        for tt in range(NTT):
            sl = tt % 2
            ld = ld_f0[tt]
            bc = emit_bcast2(bb, ar, p_ba, p_bb2, s_row, m_row, ts(tt, TT),
                             [rows_done, war_bc], sbb, mbb, TT)
            zt_toks = []
            for c in range(CH):
                zt_toks.append(emit_apply_norm(
                    bb, ar, xt[sl][:, c, :], zt[:, c, :], mbb, sbb, None,
                    deps=[ld, war_z] + bc))
            war_bc = zt_toks[-1]
            if tt + 1 < NTT:
                nsl2 = (tt + 1) % 2
                ld_f0[tt + 1] = bb.dma(
                    "sp", xt[nsl2][:], xv[:, :, ts(tt + 1, TT)],
                    f"f0_x{nsl2}", deps=[war_x[nsl2]])
            ep_toks = []
            for oc in range(CH):
                psl = oc % 2
                mm = None
                for kc in range(CH):
                    mm = bb.op("pe", lambda E_, kc=kc, oc=oc, p=psl: E_.matmul(
                        p_mm[p][:], wT[:, kc, ts(oc, 128)], zt[:, kc, :],
                        start=(kc == 0), stop=(kc == CH - 1)),
                        deps=[zt_toks[kc], war_mm[psl]] +
                        ([tr_last] if tt == 0 and oc == 0 else []))
                dl_ = dlt2[oc % 2]
                e1 = bb.op("act", lambda E_, oc=oc, p=psl, dl_=dl_: E_.activation(
                    dl_[:], p_mm[p][:], AF.Identity, bias=hbias[:, oc:oc + 1],
                    scale=acol[:]), deps=[mm, hb, b4, war_dlt])
                e2 = bb.op("dve", lambda E_, oc=oc, s=sl, dl_=dl_: E_.tensor_add(
                    nx[s][:, oc, :], xt[s][:, oc, :], dl_[:]),
                    deps=[e1, war_nx[sl], ar.get("stat_war")])
                war_mm[psl] = e1
                war_dlt = ep_toks[-1] if ep_toks else None
                ep_toks.append(e2)
            war_x[sl] = ep_toks[-1]
            war_z = ep_toks[-1]
            st = bb.dma("sp", ov[:, :, ts(tt, TT)], nx[sl][:], "st_x",
                        deps=ep_toks)
            war_nx[sl] = st
            if compute_stats_out:
                if stats_job is not None:
                    emit_stats_batch(bb, ar, *stats_job)
                stats_job = (nx[sl][:], TT, p_sx, p_sq, [ep_toks[-1]],
                             srow, stats_out, ts(tt, TT), "f0_sst")
        if compute_stats_out and stats_job is not None:
            emit_stats_batch(bb, ar, *stats_job)
        bb.fence()



def emit_rwkv(bb, ar, x_in, x_out, stats_in, stats_out, compute_stats_out):
    nc = bb.nc
    t_ws = {"r": ar["t_L1_ffn_W_r"], "k": ar["t_L1_ffn_W_k"],
            "v": ar["t_L1_ffn_W_v"]}

    with ExitStack() as sc:
        Es = sc.enter_context
        m_row, s_row, rows_done = emit_rows_prologue(bb, ar, sc, stats_in)
        wT = {n: Es(nc.sbuf_tensor(_u(f"rw_wT{n}"), [128, CH, D], BF16))
              for n in ("r", "k", "v")}
        with ExitStack() as tsc:
            Et = tsc.enter_context
            wch = [Et(nc.sbuf_tensor(_u(f"rw_w{i}"), [128, D], BF16)) for i in range(2)]
            p_tr = Et(nc.psum_tensor(_u("rw_ptr"), [128, 128], BF16))
            war = [None, None]
            war_ptr = None
            tr_last = None
            for name in ("r", "k", "v"):
                wv_ = t_ws[name].rearrange("(c p) d -> p c d", p=128)
                for dc in range(CH):
                    sl = dc % 2
                    ld = bb.dma("pop", wch[sl][:], wv_[:, dc, :], f"rw_w{sl}",
                                deps=[war[sl]])
                    for cc in range(CH):
                        tr = bb.op("pe", lambda E_, cc=cc, s=sl: E_.transpose(
                            p_tr[:], wch[s][:, ts(cc, 128)], ar["ident"][:]),
                            deps=[ld, war_ptr] + ar["const_tok"])
                        ev = bb.op("act", lambda E_, cc=cc, dc=dc, n=name: E_.copy(
                            wT[n][:, cc, ts(dc, 128)], p_tr[:]), deps=[tr])
                        war_ptr = ev
                        tr_last = ev
                    war[sl] = tr_last
            bb.fence()

        mur = Es(nc.sbuf_tensor(_u("rw_mur"), [128, CH], F32))
        muk = Es(nc.sbuf_tensor(_u("rw_muk"), [128, CH], F32))
        lr = bb.dma("sp", mur[:], ar["t_L1_ffn_mu_r"].rearrange("(c p) -> p c", p=128),
                    "ld_rows", allow_slow_non_contiguous=True)
        lk = bb.dma("sp", muk[:], ar["t_L1_ffn_mu_k"].rearrange("(c p) -> p c", p=128),
                    "ld_rows", allow_slow_non_contiguous=True)

        xt = [Es(nc.sbuf_tensor(_u(f"rw_x{i}"), [128, CH, TT], F32)) for i in range(2)]
        xn = [Es(nc.sbuf_tensor(_u(f"rw_xn{i}"), [128, CH, TT + 1], F32))
              for i in range(2)]
        dt_ = Es(nc.sbuf_tensor(_u("rw_d"), [128, TT], F32))
        xr = Es(nc.sbuf_tensor(_u("rw_xr"), [128, CH, TT], BF16))
        xk = Es(nc.sbuf_tensor(_u("rw_xk"), [128, CH, TT], BF16))

        k2 = Es(nc.sbuf_tensor(_u("rw_k2"), [128, CH, TT], BF16))

        rt = Es(nc.sbuf_tensor(_u("rw_r"), [128, TT], F32))
        rl = Es(nc.sbuf_tensor(_u("rw_rl"), [128, TT], F32))
        dlt = Es(nc.sbuf_tensor(_u("rw_dlt"), [128, TT], F32))
        nx = Es(nc.sbuf_tensor(_u("rw_nx"), [128, CH, TT], F32))
        sbb = Es(nc.sbuf_tensor(_u("rw_sb"), [128, TT], F32))
        mbb = Es(nc.sbuf_tensor(_u("rw_mb"), [128, TT], F32))

        ar["tscr"] = Es(nc.sbuf_tensor(_u("rw_tscr"), [128, TT], F32))
        ar["tscr2"] = Es(nc.sbuf_tensor(_u("rw_tscr2"), [128, TT], F32))
        ar["nxb"] = Es(nc.sbuf_tensor(_u("rw_nxb"), [128, TT], BF16))
        ar["nxq"] = Es(nc.sbuf_tensor(_u("rw_nxq"), [128, TT], BF16))
        ar["nxb_ch"] = 1
        srow = Es(nc.sbuf_tensor(_u("rw_srow"), [1, 2 * TT], F32))
        p_mm = [Es(nc.psum_tensor(_u(f"rw_pm{i}"), [128, 512], F32)) for i in range(2)]
        p_ba = Es(nc.psum_tensor(_u("rw_pba"), [128, 512], F32))
        p_bb2 = Es(nc.psum_tensor(_u("rw_pbb"), [128, 512], F32))
        p_sx = Es(nc.psum_tensor(_u("rw_psx"), [1, TT], F32))
        p_sq = Es(nc.psum_tensor(_u("rw_psq"), [1, TT], F32))

        hz = bb.op("dve", lambda E_: E_.memset(xn[0][:, :, 0:1], 0.0))

        xv = x_in.rearrange("(c p) t -> p c t", p=128)
        ov = x_out.rearrange("(c p) t -> p c t", p=128)
        W = {"x": [None, None], "nx": None, "xn": [[hz], [hz]],
             "mm": [None, None], "rl": None, "dlt": None, "bc": None,
             "dt": None, "xr": None, "xk": None}
        apply_toks, kmix_toks, rmix_toks = {}, {}, {}
        bc_toks, ld_toks = {}, {}
        halo_toks = {0: hz}
        stats_job = None

        def e_ld(tt):
            sl = tt % 2
            ld_toks[tt] = bb.dma("sp", xt[sl][:], xv[:, :, ts(tt, TT)],
                                 f"rw_x{sl}", deps=[W["x"][sl]])

        def e_bc(tt):
            bc_toks[tt] = emit_bcast2(bb, ar, p_ba, p_bb2, s_row, m_row,
                                      ts(tt, TT), [rows_done, W["bc"]],
                                      sbb, mbb, TT)

        def e_apply(tt, cs):
            sl = tt % 2
            nsl = (tt + 1) % 2
            toks = apply_toks.setdefault(tt, [])
            for c in cs:
                i_ = ar.get("tscr_i", 0)
                t_ = ar["tscr"] if i_ == 0 else ar["tscr2"]
                wkey = "tscr_war" if i_ == 0 else "tscr_war2"
                ar["tscr_i"] = 1 - i_
                s1 = bb.op("pop", lambda E_, c=c, s=sl, t_=t_: E_.tensor_sub(
                    t_[:], xt[s][:, c, :], mbb[:]),
                    deps=[ld_toks[tt], ar.get(wkey)] + W["xn"][sl] +
                    bc_toks[tt])
                s2 = bb.op("dve", lambda E_, c=c, s=sl, t_=t_: E_.tensor_mul(
                    xn[s][:, c, 1:TT + 1], t_[:], sbb[:]), deps=[s1])
                ar[wkey] = s2
                toks.append(s2)
            if len(toks) == CH:
                W["bc"] = toks[-1]
                if tt + 1 < NTT:
                    halo_toks[tt + 1] = bb.op(
                        "dve", lambda E_, s=sl, nn=nsl: E_.tensor_copy(
                            xn[nn][:, :, 0:1], xn[s][:, :, TT:TT + 1]),
                        deps=toks + W["xn"][nsl])

        def e_kmix(tt, cs):
            sl = tt % 2
            toks = kmix_toks.setdefault(tt, [])
            for c in cs:
                d1 = bb.op("pop", lambda E_, c=c, s=sl: E_.tensor_sub(
                    dt_[:], xn[s][:, c, 1:TT + 1], xn[s][:, c, 0:TT]),
                    deps=[apply_toks[tt][c], halo_toks[tt], W["dt"]])
                k1 = bb.op("dve", lambda E_, c=c, s=sl: E_.scalar_tensor_tensor(
                    xk[:, c, :], dt_[:], muk[:, c:c + 1],
                    xn[s][:, c, 0:TT], ALU.mult, ALU.add),
                    deps=[d1, lk, W["xk"]])
                W["dt"] = k1
                toks.append(k1)

        def e_rmix(tt):
            sl = tt % 2
            toks = rmix_toks.setdefault(tt, [])
            for c in range(CH):
                d1 = bb.op("pop", lambda E_, c=c, s=sl: E_.tensor_sub(
                    dt_[:], xn[s][:, c, 1:TT + 1], xn[s][:, c, 0:TT]),
                    deps=[apply_toks[tt][c], halo_toks[tt], W["dt"]])
                r1 = bb.op("dve", lambda E_, c=c, s=sl: E_.scalar_tensor_tensor(
                    xr[:, c, :], dt_[:], mur[:, c:c + 1],
                    xn[s][:, c, 0:TT], ALU.mult, ALU.add),
                    deps=[d1, lr, W["xr"]])
                W["dt"] = r1
                toks.append(r1)
            W["xn"][sl] = [toks[-1]] + \
                ([halo_toks[tt + 1]] if tt + 1 in halo_toks else [])

        e_ld(0)
        e_bc(0)
        e_apply(0, range(CH))
        e_kmix(0, range(CH))
        for tt in range(NTT):
            sl = tt % 2
            e_rmix(tt)
            k_toks = []
            k_last = None
            for oc in range(CH):
                psl = oc % 2
                mm = None
                for kc in range(CH):
                    mm = bb.op("pe", lambda E_, kc=kc, oc=oc, p=psl: E_.matmul(
                        p_mm[p][:], wT["k"][:, kc, ts(oc, 128)], xk[:, kc, :],
                        start=(kc == 0), stop=(kc == CH - 1)),
                        deps=[kmix_toks[tt][kc], W["mm"][psl]] +
                        ([tr_last] if tt == 0 and oc == 0 else []))
                k_last = mm
                e1 = bb.op("act", lambda E_, p=psl: E_.activation(
                    rl[:], p_mm[p][:], AF.Relu), deps=[mm, W["rl"]])
                e2 = bb.op("act", lambda E_, oc=oc: E_.activation(
                    k2[:, oc, :], rl[:], AF.Square), deps=[e1])
                W["mm"][psl] = e2
                W["rl"] = e2
                k_toks.append(e2)
            W["xk"] = k_last
            if compute_stats_out and stats_job is not None:
                emit_stats_batch(bb, ar, *stats_job, src_hbm=ov)
                stats_job = None
            if tt + 1 < NTT:
                e_ld(tt + 1)
                e_bc(tt + 1)
            ep_toks = []
            war_rt = W["dlt"]
            r_last = None
            for oc in range(CH):
                psl = oc % 2
                mm = None
                for kc in range(CH):
                    mm = bb.op("pe", lambda E_, kc=kc, oc=oc, p=psl: E_.matmul(
                        p_mm[p][:], wT["r"][:, kc, ts(oc, 128)], xr[:, kc, :],
                        start=(kc == 0), stop=(kc == CH - 1)),
                        deps=[rmix_toks[tt][kc], W["mm"][psl]])
                r_last = mm
                sg = bb.op("act", lambda E_, p=psl: E_.activation(
                    rt[:], p_mm[p][:], AF.Sigmoid), deps=[mm, war_rt])
                W["mm"][psl] = sg
                mm = None
                for kc in range(CH):
                    mm = bb.op("pe", lambda E_, kc=kc, oc=oc, p=psl: E_.matmul(
                        p_mm[p][:], wT["v"][:, kc, ts(oc, 128)], k2[:, kc, :],
                        start=(kc == 0), stop=(kc == CH - 1)),
                        deps=[k_toks[kc], W["mm"][psl]])
                e1 = bb.op("dve", lambda E_, oc=oc, p=psl:
                           E_.scalar_tensor_tensor(
                    dlt[:], rt[:], 0.5, p_mm[p][:], ALU.mult,
                    ALU.mult), deps=[mm, sg, W["dlt"]])
                war_rt = e1
                e2 = bb.op("pop", lambda E_, oc=oc, s=sl: E_.tensor_add(
                    nx[:, oc, :], xt[s][:, oc, :], dlt[:]),
                    deps=[e1, W["nx"]])
                W["mm"][psl] = e1
                W["dlt"] = e2
                ep_toks.append(e2)
                if tt + 1 < NTT:
                    if oc < 4:
                        e_apply(tt + 1, [2 * oc, 2 * oc + 1])
                    else:
                        e_kmix(tt + 1, [2 * (oc - 4), 2 * (oc - 4) + 1])
            W["xr"] = r_last
            W["x"][sl] = ep_toks[-1]
            st = bb.dma("sp", ov[:, :, ts(tt, TT)], nx[:], "st_x",
                        deps=ep_toks)
            W["nx"] = st
            if compute_stats_out:
                stats_job = (None, TT, p_sx, p_sq, [st], srow, stats_out,
                             ts(tt, TT), "rw_sst")
        if compute_stats_out and stats_job is not None:
            emit_stats_batch(bb, ar, *stats_job, src_hbm=ov)
        bb.fence()



def emit_attn0(bb, ar, x_in, x_out, stats_in, stats_out, compute_stats_out):
    nc = bb.nc
    q_hbm, k_hbm = ar["q_hbm"], ar["k_hbm"]
    QT = 1024
    NQT = T // QT

    with ExitStack() as sc:
        Es = sc.enter_context
        vplus = Es(nc.sbuf_tensor(_u("a0_vplus"), [128, 32, NH, 66], BF16))
        m_row, s_row, rows_done = emit_rows_prologue(bb, ar, sc, stats_in,
                                                     want_rms_gamma=True)
        gcol = Es(nc.sbuf_tensor(_u("a0_gcol"), [128, CH], F32))
        lg = bb.dma("sp", gcol[:],
                    ar["t_L0_attn_gamma"].rearrange("(c p) -> p c", p=128),
                    "ld_rows", allow_slow_non_contiguous=True)
        lg2 = bb.op("dve", lambda E_: E_.tensor_scalar(
            gcol[:], gcol[:], 1.0, None, ALU.add), deps=[lg])
        vset = bb.op("dve", lambda E_: E_.memset(vplus[:], 1.0))
        negCall = Es(nc.sbuf_tensor(_u("a0_negCall"), [128, CH], F32))

        with ExitStack() as p1:
            Ep = p1.enter_context
            wq = Ep(nc.sbuf_tensor(_u("a0_wq"), [128, CH, 3 * D], BF16))
            lw = []
            wv_ = ar["t_L0_attn_w_qkv"].rearrange("(c p) n -> p c n", p=128)
            for kc in range(CH):
                lw.append(bb.dma("pop", wq[:, kc, :], wv_[:, kc, :], "a0_wld"))
            xt = [Ep(nc.sbuf_tensor(_u(f"a0_x{i}"), [128, CH, TT], F32))
                  for i in range(2)]
            zt = Ep(nc.sbuf_tensor(_u("a0_z"), [128, CH, TT], BF16))
            stg = [Ep(nc.sbuf_tensor(_u(f"a0_stg{i}"), [128, TT], BF16))
                   for i in range(2)]
            sbb = Ep(nc.sbuf_tensor(_u("a0_sb"), [128, TT], F32))
            mbb = Ep(nc.sbuf_tensor(_u("a0_mb"), [128, TT], F32))
            ar["tscr"] = Ep(nc.sbuf_tensor(_u("a0_tscr"), [128, TT], F32))
            ar["tscr2"] = Ep(nc.sbuf_tensor(_u("a0_tscr2"), [128, TT], F32))
            p_mm = [Ep(nc.psum_tensor(_u(f"a0_pm{i}"), [128, 512], F32))
                    for i in range(2)]
            p_v = Ep(nc.psum_tensor(_u("a0_pv"), [128, 1024], F32))
            p_ba = Ep(nc.psum_tensor(_u("a0_pba"), [128, 512], F32))
            p_bb2 = Ep(nc.psum_tensor(_u("a0_pbb"), [128, 512], F32))
            p_st = Ep(nc.psum_tensor(_u("a0_pst"), [128, 512], F32))
            sqsc = [Ep(nc.sbuf_tensor(_u(f"a0_sqsc{i}"), [128, TT], BF16))
                    for i in range(2)]
            qkmax = Ep(nc.sbuf_tensor(_u("a0_qkmax"), [1, 64], F32))
            kbarf = Ep(nc.sbuf_tensor(_u("a0_kbarf"), [128, CH], F32))
            kbred = Ep(nc.sbuf_tensor(_u("a0_kbred"), [128, 1], F32))
            kb2c = Ep(nc.sbuf_tensor(_u("a0_kb2"), [128, CH], BF16))
            qkz = bb.op("dve", lambda E_: E_.memset(qkmax[:], 0.0))
            kbz = bb.op("dve", lambda E_: E_.memset(kbarf[:], 0.0))

            xv = x_in.rearrange("(c p) t -> p c t", p=128)
            war_x = [None, None]
            war_z = None
            war_bc = None
            war_mm = [None, None]
            war_stg = [[], []]
            war_pv = None

            def e_qkstat(oc, psl, sq, ev):
                msq = bb.op("pe", lambda E_, s=psl: E_.matmul(
                    p_st[0:1, :], ar["onesc"][:], sqsc[s][:],
                    start=True, stop=True),
                    deps=[sq, ar.get("a0_pst_war")])
                rmx = bb.op("dve", lambda E_, oc=oc: E_.tensor_reduce(
                    qkmax[0:1, 32 + oc % 2:33 + oc % 2], p_st[0:1, :],
                    AX.X, ALU.max), deps=[msq, qkz])
                upd = bb.op("dve", lambda E_, oc=oc: E_.tensor_tensor(
                    qkmax[0:1, oc:oc + 1], qkmax[0:1, oc:oc + 1],
                    qkmax[0:1, 32 + oc % 2:33 + oc % 2], ALU.max),
                    deps=[rmx])
                ar[f"a0_sq_war{psl}"] = msq
                ar["a0_pst_war"] = rmx
                out = upd
                if oc >= 8:
                    kr = bb.op("dve", lambda E_, s=psl: E_.tensor_reduce(
                        kbred[:], stg[s][:], AX.X, ALU.add),
                        deps=[ev, ar.get("a0_kbr_war")])
                    ka = bb.op("dve", lambda E_, oc=oc: E_.tensor_add(
                        kbarf[:, oc - 8:oc - 7], kbarf[:, oc - 8:oc - 7],
                        kbred[:]), deps=[kr, kbz])
                    ar["a0_kbr_war"] = ka
                    war_stg[psl].append(kr)
                    out = ka
                return out

            ld_p1 = {0: bb.dma("sp", xt[0][:], xv[:, :, ts(0, TT)], "a0_x0")}
            for tt in range(NTT):
                sl = tt % 2
                ld = ld_p1[tt]
                bc = emit_bcast2(bb, ar, p_ba, p_bb2, s_row, m_row, ts(tt, TT),
                                 [rows_done, war_bc], sbb, mbb, TT)
                zt_toks = []
                for c in range(CH):
                    zt_toks.append(emit_apply_norm(
                        bb, ar, xt[sl][:, c, :], zt[:, c, :], mbb, sbb,
                        gcol[:, c:c + 1], deps=[ld, war_z, lg2] + bc))
                war_bc = zt_toks[-1]
                war_x[sl] = zt_toks[-1]
                if tt + 1 < NTT:
                    nsl = (tt + 1) % 2
                    ld_p1[tt + 1] = bb.dma(
                        "sp", xt[nsl][:], xv[:, :, ts(tt + 1, TT)],
                        f"a0_x{nsl}", deps=[war_x[nsl]])
                pend_st = None
                for oc in range(16):
                    psl = oc % 2
                    mm = None
                    for kc in range(CH):
                        mm = bb.op("pe", lambda E_, kc=kc, oc=oc, p=psl:
                                   E_.matmul(
                            p_mm[p][:], wq[:, kc, ts(oc, 128)], zt[:, kc, :],
                            start=(kc == 0), stop=(kc == CH - 1)),
                            deps=[zt_toks[kc], lw[kc], war_mm[psl]])
                    if pend_st is not None:
                        stat_last = e_qkstat(*pend_st)
                        pend_st = None
                    scale = 0.125 if oc < 8 else 1.0
                    ev = bb.op("act", lambda E_, p=psl, s=psl, sc_=scale:
                               E_.activation(stg[s][:], p_mm[p][:], AF.Copy,
                                             scale=sc_),
                               deps=[mm] + war_stg[psl])
                    war_mm[psl] = ev
                    dst = q_hbm if oc < 8 else k_hbm
                    st = bb.dma("sp", dst[oc % 8, :, ts(tt, TT)], stg[psl][:],
                                "a0_qkst", deps=[ev])
                    sq = bb.op("dve", lambda E_, s=psl: E_.tensor_mul(
                        sqsc[s][:], stg[s][:], stg[s][:]),
                        deps=[ev, ar.get(f"a0_sq_war{psl}")])
                    war_stg[psl] = [st, sq]
                    pend_st = (oc, psl, sq, ev)
                stat_last = e_qkstat(*pend_st)
                for tsub in range(TT // 128):
                    ktg = tt * 4 + tsub
                    mm = None
                    for kc in range(CH):
                        for j in range(2):
                            mm = bb.op("pe", lambda E_, kc=kc, j=j, t0=tsub:
                                       E_.matmul(
                                p_v[:, ts(j, 512)], zt[:, kc, ts(t0, 128)],
                                wq[:, kc, 2048 + j * 512:2048 + (j + 1) * 512],
                                start=(kc == 0), stop=(kc == CH - 1)),
                                deps=[zt_toks[kc], war_pv])
                    ev = bb.op("act", lambda E_, k=ktg: E_.copy(
                        vplus[:, k, :, 0:64],
                        p_v[:].rearrange("p (h d) -> p h d", h=NH)),
                        deps=[mm, vset])
                    war_pv = ev
                    war_z = ev
            kq = bb.op("dve", lambda E_: E_.tensor_mul(
                kb2c[:], kbarf[:], kbarf[:]),
                deps=[ar.get("a0_kbr_war"), kbz])
            kbn_ev = []
            for hh in range(2):
                mmk = bb.op("pe", lambda E_, hh=hh: E_.matmul(
                    p_st[0:1, 0:CH], ar["onesc"][64 * hh:64 * hh + 64, :],
                    kb2c[64 * hh:64 * hh + 64, :], start=True, stop=True),
                    deps=[kq, ar.get("a0_pst_war"), stat_last] + kbn_ev)
                evk = bb.op("dve", lambda E_, hh=hh: E_.tensor_copy(
                    qkmax[0:1, 40 + 8 * hh:48 + 8 * hh], p_st[0:1, 0:CH]),
                    deps=[mmk])
                kbn_ev.append(evk)
            cdone = []
            war_nc = None
            for p in range(CH):
                t1 = bb.op("dve", lambda E_, p=p: E_.tensor_tensor(
                    qkmax[0:1, 56:57], qkmax[0:1, 2 * p:2 * p + 1],
                    qkmax[0:1, 2 * p + 1:2 * p + 2], ALU.max),
                    deps=[stat_last] + cdone + kbn_ev)
                t2 = bb.op("dve", lambda E_, p=p: E_.tensor_tensor(
                    qkmax[0:1, 57:58], qkmax[0:1, 8 + 2 * p:9 + 2 * p],
                    qkmax[0:1, 9 + 2 * p:10 + 2 * p], ALU.max), deps=[t1])
                t3 = bb.op("dve", lambda E_, p=p: E_.tensor_tensor(
                    qkmax[0:1, 58:59], qkmax[0:1, 40 + p:41 + p],
                    qkmax[0:1, 48 + p:49 + p], ALU.max), deps=[t2])
                l1 = bb.op("act", lambda E_: E_.activation(
                    qkmax[0:1, 59:60], qkmax[0:1, 56:57], AF.Ln,
                    bias=ar["cvals"][0:1, 1:2]), deps=[t3])
                l2 = bb.op("act", lambda E_: E_.activation(
                    qkmax[0:1, 60:61], qkmax[0:1, 57:58], AF.Ln,
                    bias=ar["cvals"][0:1, 1:2]), deps=[l1])
                l3 = bb.op("act", lambda E_: E_.activation(
                    qkmax[0:1, 61:62], qkmax[0:1, 58:59], AF.Ln,
                    bias=ar["cvals"][0:1, 1:2]), deps=[l2])
                a1 = bb.op("dve", lambda E_: E_.tensor_add(
                    qkmax[0:1, 62:63], qkmax[0:1, 59:60], qkmax[0:1, 60:61]),
                    deps=[l2, l3])
                a2 = bb.op("act", lambda E_: E_.activation(
                    qkmax[0:1, 62:63], qkmax[0:1, 62:63], AF.Exp, scale=0.5),
                    deps=[a1])
                b1 = bb.op("dve", lambda E_: E_.tensor_add(
                    qkmax[0:1, 63:64], qkmax[0:1, 59:60], qkmax[0:1, 61:62]),
                    deps=[a2, l3])
                b2 = bb.op("act", lambda E_: E_.activation(
                    qkmax[0:1, 63:64], qkmax[0:1, 63:64], AF.Exp, scale=0.5),
                    deps=[b1])
                b3 = bb.op("dve", lambda E_: E_.tensor_scalar(
                    qkmax[0:1, 63:64], qkmax[0:1, 63:64], -1.0 / T, None,
                    ALU.mult), deps=[b2])
                c1 = bb.op("dve", lambda E_: E_.tensor_scalar(
                    qkmax[0:1, 63:64], qkmax[0:1, 63:64], 60.0, None,
                    ALU.add), deps=[b3])
                c2 = bb.op("dve", lambda E_: E_.tensor_tensor(
                    qkmax[0:1, 63:64], qkmax[0:1, 62:63], qkmax[0:1, 63:64],
                    ALU.min), deps=[c1])
                c3 = bb.op("dve", lambda E_: E_.tensor_scalar(
                    qkmax[0:1, 62:63], qkmax[0:1, 62:63], -70.0, None,
                    ALU.add), deps=[c2])
                c4 = bb.op("dve", lambda E_: E_.tensor_tensor(
                    qkmax[0:1, 63:64], qkmax[0:1, 63:64], qkmax[0:1, 62:63],
                    ALU.max), deps=[c3])
                c5 = bb.op("dve", lambda E_: E_.tensor_scalar(
                    qkmax[0:1, 63:64], qkmax[0:1, 63:64], -1.0, None,
                    ALU.mult), deps=[c4])
                bcm = bb.op("pe", lambda E_, p=p: E_.matmul(
                    p_st[:, 0:1], ar["onesr"][:], qkmax[0:1, 63:64],
                    start=True, stop=True), deps=[c5, war_nc, kbn_ev[-1]])
                ncv = bb.op("dve", lambda E_, p=p: E_.tensor_copy(
                    negCall[:, p:p + 1], p_st[:, 0:1]), deps=[bcm])
                war_nc = ncv
                cdone.append(ncv)
            ar["a0_c_done"] = cdone
            bb.fence()

        QW = 512
        NQW = T // QW
        NSLOT = 3
        qbuf = Es(nc.sbuf_tensor(_u("a0_qb"), [128, CH, QW], BF16))
        kbuf = [Es(nc.sbuf_tensor(_u(f"a0_kb{i}"), [128, T], BF16))
                for i in range(2)]
        ptil = [Es(nc.sbuf_tensor(_u(f"a0_pt{i}"), [128, 1024], BF16))
                for i in range(NSLOT)]
        attnT = Es(nc.sbuf_tensor(_u("a0_attnT"), [128, CH, QW], BF16))
        numsA = Es(nc.sbuf_tensor(_u("a0_numsA"), [64, QW], F32))
        numsB = Es(nc.sbuf_tensor(_u("a0_numsB"), [64, QW], F32))
        rdenA = Es(nc.sbuf_tensor(_u("a0_rdenA"), [1, QW], F32))
        rdenB = Es(nc.sbuf_tensor(_u("a0_rdenB"), [1, QW], F32))
        wo = Es(nc.sbuf_tensor(_u("a0_wo"), [128, CH, D], BF16))
        xsl = [Es(nc.sbuf_tensor(_u(f"a0_xs{i}"), [128, QW], F32))
               for i in range(2)]
        nxs = Es(nc.sbuf_tensor(_u("a0_nxs"), [128, QW], F32))
        ar["nxb"] = Es(nc.sbuf_tensor(_u("a0_nxb"), [128, QW], BF16))
        ar["nxq"] = Es(nc.sbuf_tensor(_u("a0_nxq"), [128, QW], BF16))
        srow = Es(nc.sbuf_tensor(_u("a0_srow"), [1, 2 * QW], F32))
        p_S = [Es(nc.psum_tensor(_u(f"a0_pS{i}"), [128, 1024], F32))
               for i in range(NSLOT)]
        p_oA = Es(nc.psum_tensor(_u("a0_poA"), [65, 512], F32))
        p_oB = Es(nc.psum_tensor(_u("a0_poB"), [65, 512], F32))

        wov = ar["t_L0_attn_w_out"].rearrange("(c p) n -> p c n", p=128)
        lwo = [bb.dma("pop", wo[:, ic, :], wov[:, ic, :], "a0_wld")
               for ic in range(CH)]

        war_q = None
        war_k = [None, None]
        war_attnT = None
        war_xr = [None, None]
        war_nxs = None
        war_oA = None
        war_oB = None
        exp_toks = [None] * NSLOT
        av_toks = [None] * NSLOT
        qv = q_hbm.rearrange("c p t -> p c t")
        xrv = x_in.rearrange("(c p) t -> p c t", p=128)
        ov = x_out.rearrange("(c p) t -> p c t", p=128)
        cdone = ar["a0_c_done"]

        NP = NH // 2
        lk_tok = [None] * NP
        for qt in range(NQW):
            lq = bb.dma("sp", qbuf[:], qv[:, :, ts(qt, QW)], "a0_q0",
                        deps=[war_q])
            if qt == 0:
                lk_tok[0] = bb.dma("sp", kbuf[0][:], k_hbm[0, :, :], "a0_k0",
                                   deps=[war_k[0]])
            for p in range(NP):
                psl = p % 2
                qhA = qbuf[0:64, p, :]
                qhB = qbuf[64:128, p, :]
                lk = lk_tok[p]
                ncc = cdone[p]
                if p + 1 < NP:
                    npsl = (p + 1) % 2
                    lk_tok[p + 1] = bb.dma(
                        "sp", kbuf[npsl][:], k_hbm[p + 1, :, :],
                        f"a0_k{npsl}", deps=[war_k[npsl]])
                elif qt + 1 < NQW:
                    lk_tok[0] = bb.dma("sp", kbuf[0][:], k_hbm[0, :, :],
                                       "a0_k0", deps=[war_k[0]])
                pend = []
                last_avA = last_avB = None
                for kt in range(34):
                    if kt < 32:
                        Ssl = kt % NSLOT
                        mmA = bb.op("pe", lambda E_, kt=kt, s=Ssl, ks=psl,
                                    qh=qhA: E_.matmul(
                            p_S[s][:, 0:512], kbuf[ks][0:64, ts(kt, 128)],
                            qh[:], start=True, stop=True,
                            tile_position=(0, 0)),
                            deps=[lk, lq, exp_toks[Ssl]])
                        mmB = bb.op("pe", lambda E_, kt=kt, s=Ssl, ks=psl,
                                    qh=qhB: E_.matmul(
                            p_S[s][:, 512:1024], kbuf[ks][64:128, ts(kt, 128)],
                            qh[:], start=True, stop=True,
                            tile_position=(64, 0)),
                            deps=[lk])
                        ex = bb.op("act", lambda E_, s=Ssl, p=p: E_.activation(
                            ptil[s][:], p_S[s][:], AF.Exp,
                            bias=negCall[:, p:p + 1]),
                            deps=[mmA, mmB, ncc, av_toks[Ssl]])
                        exp_toks[Ssl] = ex
                        pend.append((kt, Ssl, ex))
                    if pend and pend[0][0] <= kt - 2:
                        pkt, pSsl, pex = pend.pop(0)
                        avA = bb.op("pe", lambda E_, kt=pkt, s=pSsl, p=p:
                                    E_.matmul(
                            p_oA[:], vplus[:, kt, 2 * p, 0:65],
                            ptil[s][:, 0:512],
                            start=(kt == 0), stop=(kt == 31)),
                            deps=[pex, war_oA])
                        avB = bb.op("pe", lambda E_, kt=pkt, s=pSsl, p=p:
                                    E_.matmul(
                            p_oB[:], vplus[:, kt, 2 * p + 1, 0:65],
                            ptil[s][:, 512:1024],
                            start=(kt == 0), stop=(kt == 31)),
                            deps=[pex, war_oB])
                        av_toks[pSsl] = avB
                        last_avA, last_avB = avA, avB
                rcA = bb.op("dve", lambda E_: E_.reciprocal(
                    rdenA[:], p_oA[64:65, :]), deps=[last_avA])
                rcB = bb.op("dve", lambda E_: E_.reciprocal(
                    rdenB[:], p_oB[64:65, :]), deps=[last_avB])
                rbA = bb.op("pe", lambda E_: E_.matmul(
                    p_S[0][0:64, 0:512], ar["onesr"][:, 0:64], rdenA[:],
                    start=True, stop=True),
                    deps=[rcA, exp_toks[0], av_toks[0]])
                rbB = bb.op("pe", lambda E_: E_.matmul(
                    p_S[1][0:64, 0:512], ar["onesr"][:, 0:64], rdenB[:],
                    start=True, stop=True),
                    deps=[rcB, exp_toks[1], av_toks[1]])
                nevA = bb.op("act", lambda E_: E_.copy(numsA[:], p_oA[0:64, :]),
                             deps=[last_avA, rcA])
                nevB = bb.op("act", lambda E_: E_.copy(numsB[:], p_oB[0:64, :]),
                             deps=[last_avB, rcB])
                dA = bb.op("dve", lambda E_, p=p: E_.tensor_mul(
                    attnT[0:64, p, :], numsA[:], p_S[0][0:64, 0:512]),
                    deps=[nevA, rbA, war_attnT])
                dB = bb.op("dve", lambda E_, p=p: E_.tensor_mul(
                    attnT[64:128, p, :], numsB[:], p_S[1][0:64, 0:512]),
                    deps=[nevB, rbB])
                exp_toks[0] = dA
                exp_toks[1] = dB
                war_oA, war_oB = dA, dB
                war_k[psl] = dB
            head_done = dB
            stat_first = True
            wo_last = [exp_toks[2], exp_toks[2]]
            for oc in range(CH):
                xls = oc % 2
                wsl = oc % 2
                lx = bb.dma("sp", xsl[xls][:], xrv[:, oc, ts(qt, QW)],
                            f"a0_xr{xls}", deps=[war_xr[xls]])
                mm = None
                for ic in range(CH):
                    mm = bb.op("pe", lambda E_, ic=ic, oc=oc, w=wsl: E_.matmul(
                        p_S[2][:, ts(w, 512)], wo[:, ic, ts(oc, 128)],
                        attnT[:, ic, :],
                        start=(ic == 0), stop=(ic == CH - 1)),
                        deps=[head_done, lwo[ic], wo_last[wsl]])
                ad2 = bb.op("dve", lambda E_, x=xls, w=wsl: E_.tensor_add(
                    nxs[:], xsl[x][:], p_S[2][:, ts(w, 512)]),
                    deps=[mm, lx, war_nxs])
                wo_last[wsl] = ad2
                war_xr[xls] = ad2
                st = bb.dma("sp", ov[:, oc, ts(qt, QW)], nxs[:], "st_x",
                            deps=[ad2])
                war_nxs = st
                if compute_stats_out:
                    stok = emit_out_stats(
                        bb, ar, nxs[:], QW, p_oA[0:1, :], p_oB[0:1, :],
                        stat_first, oc == CH - 1, [ad2, war_oA, war_oB],
                        srow, stats_out, ts(qt, QW), "a0_sst")
                    stat_first = False
                    if oc == CH - 1:
                        war_oA = stok
                        war_oB = stok
            war_attnT = ad2
            exp_toks[2] = ad2
            war_q = ad2
        bb.fence()



def emit_attn1(bb, ar, x_in, x_out, stats_in, stats_out, compute_stats_out):
    nc = bb.nc
    dbg = os.environ.get("DBG5", "")

    with ExitStack() as sc:
        Es = sc.enter_context
        qres = Es(nc.sbuf_tensor(_u("a1_qres"), [128, CH, T], BF16))
        kvk = Es(nc.sbuf_tensor(_u("a1_kvk"), [64, NH, 66], BF16))
        projp = Es(nc.sbuf_tensor(_u("a1_proj"), [128, 64], BF16))
        lp1 = bb.dma("pop", projp[0:64, :], ar["t_L1_attn_proj"], "a1_wld")
        lp = bb.dma("pop", projp[64:128, :], ar["t_L1_attn_proj"], "a1_wld",
                    deps=[lp1])
        m_row, s_row, rows_done = emit_rows_prologue(bb, ar, sc, stats_in)

        with ExitStack() as pa:
            Ep = pa.enter_context
            wq_ = Ep(nc.sbuf_tensor(_u("a1_wq"), [128, CH, D], BF16))
            wk_ = Ep(nc.sbuf_tensor(_u("a1_wk"), [128, CH, D], BF16))
            wv2 = Ep(nc.sbuf_tensor(_u("a1_wv"), [128, CH, D], BF16))
            lws = []
            for wname, wt in (("Wq", wq_), ("Wk", wk_), ("Wv", wv2)):
                wv_ = ar["t_L1_attn_" + wname].rearrange("(c p) n -> p c n", p=128)
                for kc in range(CH):
                    lws.append(bb.dma("pop", wt[:, kc, :], wv_[:, kc, :],
                                      "a1_wld"))
            xt = [Ep(nc.sbuf_tensor(_u("a1_x0"), [128, CH, TT], F32))]
            xt.append(xt[0])
            zt = Ep(nc.sbuf_tensor(_u("a1_z"), [128, CH, TT], BF16))
            ktile = Ep(nc.sbuf_tensor(_u("a1_kt"), [128, CH, TT], BF16))
            vtile = Ep(nc.sbuf_tensor(_u("a1_vt"), [128, 4, NH, 66], BF16))
            kfs = Ep(nc.sbuf_tensor(_u("a1_kfs"), [128, 512], F32))
            kfb = Ep(nc.sbuf_tensor(_u("a1_kfb"), [128, 512], BF16))
            sbb = Ep(nc.sbuf_tensor(_u("a1_sb"), [128, TT], F32))
            mbb = Ep(nc.sbuf_tensor(_u("a1_mb"), [128, TT], F32))
            ar["tscr"] = Ep(nc.sbuf_tensor(_u("a1_tscr"), [128, TT], F32))
            ar["tscr2"] = Ep(nc.sbuf_tensor(_u("a1_tscr2"), [128, TT], F32))
            p_mm2 = [Ep(nc.psum_tensor(_u(f"a1_pmm{i}"), [128, 512], F32))
                     for i in range(2)]
            p_v = Ep(nc.psum_tensor(_u("a1_pv"), [128, 1024], F32))
            p_kf = Ep(nc.psum_tensor(_u("a1_pkf"), [128, 512], F32))
            p_kv = Ep(nc.psum_tensor(_u("a1_pkv"), [64, 512], F32))
            kv_acc = Ep(nc.sbuf_tensor(_u("a1_kvacc"), [64, NH, 66], F32))

            vset = bb.op("dve", lambda E_: E_.memset(vtile[:], 1.0))
            kz = bb.op("dve", lambda E_: E_.memset(kv_acc[:], 0.0))
            xv = x_in.rearrange("(c p) t -> p c t", p=128)
            war_x = [None, None]
            war_z = None
            war_bc = None
            war_mm = None
            war_pv = None
            war_kf = None
            war_kfs = None
            war_kt = None
            kv_last = [None] * NH
            for tt in range(NTT):
                sl = 0
                ld = bb.dma("sp", xt[sl][:], xv[:, :, ts(tt, TT)], "a1_x0",
                            deps=[war_x[sl]])
                bc = emit_bcast2(bb, ar, p_v[:, 0:512], p_v[:, 512:1024],
                                 s_row, m_row, ts(tt, TT),
                                 [rows_done, war_bc, war_pv], sbb, mbb, TT)
                zt_toks = []
                for c in range(CH):
                    zt_toks.append(emit_apply_norm(
                        bb, ar, xt[sl][:, c, :], zt[:, c, :], mbb, sbb, None,
                        deps=[ld, war_z] + bc))
                war_x[sl] = zt_toks[-1]
                war_mm2 = [war_mm, war_mm]
                for di, (dst, wt) in enumerate(((qres, wq_), (ktile, wk_))):
                    for oc in range(CH):
                        pi = (di * CH + oc) % 2
                        mm = None
                        for kc in range(CH):
                            mm = bb.op("pe", lambda E_, kc=kc, oc=oc, w=wt,
                                       pi=pi: E_.matmul(
                                p_mm2[pi][:], w[:, kc, ts(oc, 128)],
                                zt[:, kc, :],
                                start=(kc == 0), stop=(kc == CH - 1)),
                                deps=[zt_toks[kc], war_mm2[pi], lws[-1]])
                        if dst is qres:
                            ev = bb.op("act", lambda E_, oc=oc, t0=tt, pi=pi:
                                       E_.copy(
                                qres[:, oc, ts(t0, TT)], p_mm2[pi][:]),
                                deps=[mm])
                        else:
                            ev = bb.op("act", lambda E_, oc=oc, pi=pi: E_.copy(
                                ktile[:, oc, :], p_mm2[pi][:]),
                                deps=[mm, war_kt])
                        war_mm2[pi] = ev
                war_mm = war_mm2[0] if war_mm2[1] is None else war_mm2[1]
                war_mm = ev
                v_toks = []
                for tsub in range(4):
                    mm = None
                    for kc in range(CH):
                        for j in range(2):
                            mm = bb.op("pe", lambda E_, kc=kc, j=j, t0=tsub:
                                       E_.matmul(
                                p_v[:, ts(j, 512)], zt[:, kc, ts(t0, 128)],
                                wv2[:, kc, ts(j, 512)],
                                start=(kc == 0), stop=(kc == CH - 1)),
                                deps=[zt_toks[kc], war_pv] + bc)
                    ev = bb.op("act", lambda E_, t0=tsub: E_.copy(
                        vtile[:, t0, :, 0:64],
                        p_v[:].rearrange("p (h d) -> p h d", h=NH)),
                        deps=[mm, vset, war_pv])
                    war_pv = ev
                    v_toks.append(ev)
                war_z = war_pv
                war_bc = v_toks[-1]
                pairs = [] if "nokf" in dbg else [
                    (h, tsub) for h in range(NH) for tsub in range(4)]
                for b0 in range(0, len(pairs), 4):
                    batch = pairs[b0:b0 + 4]
                    mk = None
                    for jj, (h, tsub) in enumerate(batch):
                        hb, hc = 64 * (h % 2), h // 2
                        mk = bb.op("pe", lambda E_, jj=jj, h=h, t0=tsub, hb=hb,
                                   hc=hc, nb=len(batch): E_.matmul(
                            p_kf[:, ts(jj, 64)],
                            ktile[hb:hb + 64, hc, ts(t0, 128)],
                            projp[hb:hb + 64, :],
                            start=(jj == 0), stop=(jj == nb - 1)),
                            deps=[war_mm, lp, war_kf])
                    if "norl" in dbg:
                        war_kf = mk
                        war_kfs = mk
                        war_kt = mk
                        continue
                    W = 64 * len(batch)
                    rl_ = bb.op("act", lambda E_, W=W: E_.activation(
                        kfs[:, 0:W], p_kf[:, 0:W], AF.Relu),
                        deps=[mk, war_kfs])
                    if "noep" in dbg:
                        war_kf = rl_
                        war_kfs = rl_
                        war_kt = rl_
                        continue
                    ep_ = bb.op("dve", lambda E_, W=W: E_.tensor_scalar(
                        kfb[:, 0:W], kfs[:, 0:W], 1e-6, None, ALU.add),
                        deps=[rl_, war_kfs])
                    war_kf = ep_
                    mv_toks = []
                    if "nomv" in dbg:
                        war_kfs = ep_
                        war_kt = war_kfs
                        continue
                    bh = batch[0][0]
                    for jj, (h, tsub) in enumerate(batch):
                        mv = bb.op("pe", lambda E_, jj=jj, h=h, t0=tsub,
                                   nb=len(batch): E_.matmul(
                            p_kv[:, 0:65],
                            kfb[:, ts(jj, 64)], vtile[:, t0, h, 0:65],
                            start=(jj == 0), stop=(jj == nb - 1)),
                            deps=[ep_, v_toks[tsub], kv_last[0]])
                        mv_toks.append(mv)
                    acc = bb.op("dve", lambda E_, h=bh: E_.tensor_add(
                        kv_acc[:, h, 0:65], kv_acc[:, h, 0:65],
                        p_kv[0:64, 0:65]), deps=[mv_toks[-1], kz])
                    kv_last[0] = acc
                    war_kfs = acc
                    war_pv = acc
                war_kt = war_kfs
            kv_ev = []
            if "nokf" not in dbg:
                ev = bb.op("dve", lambda E_: E_.tensor_copy(
                    kvk[:], kv_acc[:]), deps=[kv_last[0]])
                kv_ev.append(ev)
            if "dbg_kvk" in ar:
                kvf = Ep(nc.sbuf_tensor(_u("dbg_kvf"), [64, NH * 66], F32))
                ck = bb.op("dve", lambda E_: E_.tensor_copy(
                    kvf[:], kvk[:].rearrange("p a b -> p (a b)")), deps=kv_ev)
                bb.dma("sp", ar["dbg_kvk"], kvf[:], "dbg_s", deps=[ck])
            bb.fence()

        if "aonly" in dbg:
            bb.fence()
            return
        with ExitStack() as pb:
            Ep = pb.enter_context
            wo_ = Ep(nc.sbuf_tensor(_u("a1_wo"), [128, CH, D], BF16))
            lwo = []
            wv_ = ar["t_L1_attn_Wo"].rearrange("(c p) n -> p c n", p=128)
            for ic in range(CH):
                lwo.append(bb.dma("pop", wo_[:, ic, :], wv_[:, ic, :], "a1_wld"))
            bocol = Ep(nc.sbuf_tensor(_u("a1_bo"), [128, CH], F32))
            lbo = bb.dma("sp", bocol[:],
                         ar["t_L1_attn_bo"].rearrange("(c p) -> p c", p=128),
                         "ld_rows", allow_slow_non_contiguous=True)
            attnT = Ep(nc.sbuf_tensor(_u("a1_attnT"), [128, CH, TT], BF16))
            qfs2 = [Ep(nc.sbuf_tensor(_u(f"a1_qfs{i}"), [64, TT], F32))
                    for i in range(2)]
            qfb2 = [Ep(nc.sbuf_tensor(_u(f"a1_qfb{i}"), [64, TT], BF16))
                    for i in range(2)]
            nums = Ep(nc.sbuf_tensor(_u("a1_nums"), [64, TT], F32))
            rd1 = Ep(nc.sbuf_tensor(_u("a1_rd1"), [1, TT], F32))
            rd2 = Ep(nc.sbuf_tensor(_u("a1_rd2"), [1, TT], F32))
            dlt = Ep(nc.sbuf_tensor(_u("a1_dlt"), [128, TT], F32))
            xslb = [Ep(nc.sbuf_tensor(_u(f"a1_xs{i}"), [128, CH, TT], F32))
                    for i in range(2)]
            nxs = [Ep(nc.sbuf_tensor(_u(f"a1_nxs{i}"), [128, CH, TT], F32))
                   for i in range(2)]
            ar["nxb"] = Ep(nc.sbuf_tensor(_u("a1_nxb"), [128, TT], BF16))
            ar["nxq"] = Ep(nc.sbuf_tensor(_u("a1_nxq"), [128, TT], BF16))
            srow = Ep(nc.sbuf_tensor(_u("a1_srow"), [1, 2 * TT], F32))
            p_qf2 = [Ep(nc.psum_tensor(_u(f"a1_pqf{i}"), [64, TT], F32))
                     for i in range(2)]
            p_num2 = [Ep(nc.psum_tensor(_u(f"a1_pnum{i}"), [65, TT], F32))
                      for i in range(2)]
            p_rb = Ep(nc.psum_tensor(_u("a1_prb"), [64, TT], F32))
            p_wo = [Ep(nc.psum_tensor(_u(f"a1_pwo{i}"), [128, 512], F32))
                    for i in range(2)]
            p_st1 = Ep(nc.psum_tensor(_u("a1_pst1"), [128, TT], F32))
            p_sx = p_st1[0:1, :]
            p_sq = p_st1[32:33, :]

            xrv = x_in.rearrange("(c p) t -> p c t", p=128)
            ov = x_out.rearrange("(c p) t -> p c t", p=128)
            war_xs = [None, None]
            war_nxs = [None, None]
            war_qf = None
            war_qfs = None
            war_qfb = None
            war_num2 = [None, None]
            war_d1 = None
            war_rd2 = None
            war_nums = None
            war_rb = None
            war_attnT = None
            war_wo = [None, None]
            war_dlt = None
            def emit_qf(h, qt):
                nonlocal war_qf, war_qfs, war_qfb
                hb, hc, pi = 64 * (h % 2), h // 2, h % 2
                mq = bb.op("pe", lambda E_, hb=hb, hc=hc, q=qt, pi=pi:
                           E_.matmul(
                    p_qf2[pi][:], projp[hb:hb + 64, :],
                    qres[hb:hb + 64, hc, ts(q, TT)],
                    start=True, stop=True), deps=[lp, war_qf[pi]] + kv_ev)
                rl_ = bb.op("act", lambda E_, pi=pi: E_.activation(
                    qfs2[pi][:], p_qf2[pi][:], AF.Relu),
                    deps=[mq, war_qfs[pi]])
                war_qf[pi] = rl_
                ep_ = bb.op("dve", lambda E_, pi=pi: E_.tensor_scalar(
                    qfb2[pi][:], qfs2[pi][:], 1e-6, None, ALU.add),
                    deps=[rl_, war_qfb[pi]])
                war_qfs[pi] = ep_
                return ep_

            war_qf = [None, None]
            war_qfs = [None, None]
            war_qfb = [None, None]
            qf_tok = [None] * NH
            for qt in range(NTT):
                qf_tok[0] = emit_qf(0, qt)
                for h in range(NH):
                    hb, hc, pi = 64 * (h % 2), h // 2, h % 2
                    if h + 1 < NH:
                        qf_tok[h + 1] = emit_qf(h + 1, qt)
                    mn = bb.op("pe", lambda E_, h=h, pi=pi: E_.matmul(
                        p_num2[pi][:], kvk[:, h, 0:65], qfb2[pi][:],
                        start=True, stop=True),
                        deps=[qf_tok[h], war_num2[pi]] + kv_ev)
                    war_qfb[pi] = mn
                    d1 = bb.op("dve", lambda E_, pi=pi: E_.tensor_scalar(
                        rd1[:], p_num2[pi][64:65, :], 1e-6, None, ALU.add),
                        deps=[mn, war_d1])
                    war_d1 = d1
                    d2 = bb.op("dve", lambda E_: E_.reciprocal(rd2[:], rd1[:]),
                               deps=[d1, war_rd2])
                    rb = bb.op("pe", lambda E_: E_.matmul(
                        p_rb[:], ar["onesr"][:, 0:64], rd2[:],
                        start=True, stop=True), deps=[d2, war_rb])
                    war_rd2 = rb
                    nev = bb.op("act", lambda E_, pi=pi: E_.copy(
                        nums[:], p_num2[pi][0:64, :]),
                        deps=[mn, d1, war_nums])
                    war_num2[pi] = nev
                    dm = bb.op("dve", lambda E_, h=h, hb=hb, hc=hc: E_.tensor_mul(
                        attnT[hb:hb + 64, hc, :], nums[:], p_rb[:]),
                        deps=[nev, rb, war_attnT])
                    war_rb = dm
                    war_nums = dm
                head_done = war_rb
                xls = qt % 2
                lx = bb.dma("sp", xslb[xls][:], xrv[:, :, ts(qt, TT)],
                            f"a1_xr{xls}", deps=[war_xs[xls]])
                ep_toks = []
                for oc in range(CH):
                    psl = oc % 2
                    mm = None
                    for ic in range(CH):
                        mm = bb.op("pe", lambda E_, ic=ic, oc=oc, p=psl:
                                   E_.matmul(
                            p_wo[p][:], wo_[:, ic, ts(oc, 128)],
                            attnT[:, ic, :], start=(ic == 0),
                            stop=(ic == CH - 1)),
                            deps=[head_done, lwo[ic], war_wo[psl]])
                    e1 = bb.op("act", lambda E_, oc=oc, p=psl: E_.activation(
                        dlt[:], p_wo[p][:], AF.Identity, bias=bocol[:, oc:oc + 1]),
                        deps=[mm, lbo, war_dlt])
                    e2 = bb.op("pop", lambda E_, oc=oc, x=xls: E_.tensor_add(
                        nxs[x][:, oc, :], xslb[x][:, oc, :], dlt[:]),
                        deps=[e1, lx, war_nxs[xls]])
                    war_wo[psl] = e1
                    war_dlt = e2
                    ep_toks.append(e2)
                    if compute_stats_out:
                        emit_out_stats(bb, ar, nxs[xls][:, oc, :], TT,
                                       p_sx, p_sq, oc == 0, oc == CH - 1,
                                       [e2], srow, stats_out, ts(qt, TT),
                                       "a1_sst", shared=True)
                war_xs[xls] = ep_toks[-1]
                war_attnT = ep_toks[-1]
                st = bb.dma("sp", ov[:, :, ts(qt, TT)], nxs[xls][:], "st_x",
                            deps=ep_toks)
                war_nxs[xls] = st
            bb.fence()



SHAPES = {
    "L0_attn_gamma": [D], "L0_attn_w_qkv": [D, 3 * D], "L0_attn_w_out": [D, D],
    "L0_ffn_weight": [D, D], "L0_ffn_alpha": [1], "L0_ffn_bias": [D],
    "L1_attn_Wq": [D, D], "L1_attn_Wk": [D, D], "L1_attn_Wv": [D, D],
    "L1_attn_proj": [64, 64], "L1_attn_Wo": [D, D], "L1_attn_bo": [D],
    "L1_ffn_W_r": [D, D], "L1_ffn_W_k": [D, D], "L1_ffn_W_v": [D, D],
    "L1_ffn_mu_r": [D], "L1_ffn_mu_k": [D],
}

SUB_WEIGHTS = {
    1: ["L0_ffn_weight", "L0_ffn_alpha", "L0_ffn_bias"],
    2: ["L0_attn_gamma", "L0_attn_w_qkv", "L0_attn_w_out"],
    3: ["L0_ffn_weight", "L0_ffn_alpha", "L0_ffn_bias"],
    4: ["L1_ffn_W_r", "L1_ffn_W_k", "L1_ffn_W_v", "L1_ffn_mu_r", "L1_ffn_mu_k"],
    5: ["L1_attn_Wq", "L1_attn_Wk", "L1_attn_Wv", "L1_attn_proj",
        "L1_attn_Wo", "L1_attn_bo"],
    6: ["L1_ffn_W_r", "L1_ffn_W_k", "L1_ffn_W_v", "L1_ffn_mu_r", "L1_ffn_mu_k"],
}

EMITTERS = {1: emit_ffn0, 2: emit_attn0, 3: emit_ffn0,
            4: emit_rwkv, 5: emit_attn1, 6: emit_rwkv}


def build_group(subs):
    nc = bass.Bass("TRN2", target_bir_lowering=False)
    stack = ExitStack()
    bb = BB(nc, stack)
    ar = {"E": stack.enter_context}

    names = []
    for s in subs:
        for w in SUB_WEIGHTS[s]:
            if w not in names:
                names.append(w)
    for w in names:
        ar["t_" + w] = nc.dram_tensor(w, SHAPES[w], F32, kind="ExternalInput").ap()
    x_in = nc.dram_tensor("x_in", [D, T], F32, kind="ExternalInput").ap()
    x_out = nc.dram_tensor("x_out", [D, T], F32, kind="ExternalOutput").ap()
    xa = nc.dram_tensor("x_buf_a", [D, T], F32, kind="Internal").ap()
    xb = nc.dram_tensor("x_buf_b", [D, T], F32, kind="Internal").ap()
    st_a = nc.dram_tensor("stats_a", [2, T], F32, kind="Internal").ap()
    st_b = nc.dram_tensor("stats_b", [2, T], F32, kind="Internal").ap()
    if 1 in subs or 3 in subs:
        ar["weff_hbm"] = nc.dram_tensor("weff_hbm", [128, CH * D], BF16,
                                        kind="Internal").ap()
        ar["wmeta_hbm"] = nc.dram_tensor("wmeta_hbm", [128, 2], F32,
                                         kind="Internal").ap()
    if 2 in subs:
        ar["q_hbm"] = nc.dram_tensor("q_hbm", [CH, 128, T], BF16,
                                     kind="Internal").ap()
        ar["k_hbm"] = nc.dram_tensor("k_hbm", [CH, 128, T], BF16,
                                     kind="Internal").ap()

    with stack:
        emit_consts(bb, ar)
        n = len(subs)
        bufs = [x_in]
        for i in range(n - 1):
            bufs.append(xa if i % 2 == 0 else xb)
        bufs.append(x_out)
        stats = [st_a, st_b]
        emit_input_stats(bb, ar, x_in, st_a)
        for i, s in enumerate(subs):
            ar.pop("stat_war", None)
            ar.pop("stat_war2", None)
            ar.pop("stat_war3", None)
            if s == 3 and 1 in subs[:i]:
                ar["ffn0_reuse"] = True
            EMITTERS[s](bb, ar, bufs[i], bufs[i + 1], stats[i % 2],
                        stats[(i + 1) % 2], i < n - 1)
        final = [(s, v) for s, v in bb.cnt.items() if s == "st_x"]
        bb.finalize(final)
    return nc, names


_PROGRAM_CACHE = {}


def _get_program(subs):
    key = tuple(subs)
    if key not in _PROGRAM_CACHE:
        _PROGRAM_CACHE[key] = build_group(list(subs))
    return _PROGRAM_CACHE[key]


GROUPS = [[1, 2, 3, 4, 5, 6]]


def kernel(**inputs):
    x = np.ascontiguousarray(inputs["L0_attn_x"], dtype=np.float32)
    cur = [np.ascontiguousarray(x[i].T) for i in range(NCORES)]
    for subs in GROUPS:
        nc, names = _get_program(tuple(subs))
        in_maps = []
        for i in range(NCORES):
            m = {w: np.ascontiguousarray(inputs[w], dtype=np.float32)
                 for w in names}
            m["x_in"] = cur[i]
            in_maps.append(m)
        res = run_bass_kernel_spmd(nc, in_maps, core_ids=list(range(NCORES)))
        cur = [np.asarray(r["x_out"]) for r in res.results]
    out = np.stack([c.T for c in cur], axis=0)
    return np.ascontiguousarray(out, dtype=np.float32)


# version 15
# speedup vs baseline: 1.0908x; 1.0137x over previous
import os
import numpy as np
from contextlib import ExitStack

import concourse.bass as bass
import concourse.mybir as mybir
from concourse.bass_utils import run_bass_kernel_spmd

F32 = mybir.dt.float32
BF16 = mybir.dt.bfloat16
AF = mybir.ActivationFunctionType
ALU = mybir.AluOpType
AX = mybir.AxisListType

NCORES = 8
T = 4096
D = 1024
CH = 8
TT = 512
NTT = T // TT
NH = 16
DH = 64

ENGS = ("pe", "act", "dve", "pop", "sp")


class BB:

    def __init__(self, nc, stack):
        self.nc = nc
        self.stack = stack
        self.q = {e: [] for e in ENGS}
        self.sems = {}
        self.cnt = {}
        self.hi = {e: {} for e in ENGS}
        self.pend = {e: [] for e in ENGS}
        self.same_sync = bool(int(os.environ.get("BB_SAME_ENGINE_SYNC", "1")))
        for e in ENGS:
            self._mksem("g_" + e)

    def _mksem(self, name):
        if name not in self.sems:
            self.sems[name] = self.stack.enter_context(self.nc.semaphore(name))
            self.cnt[name] = 0
        return self.sems[name]

    def _waits(self, eng, deps):
        out = []
        for d in deps:
            if d is None:
                continue
            s, v = d
            if s == "g_" + eng and not self.same_sync:
                continue
            if self.hi[eng].get(s, 0) >= v:
                continue
            self.hi[eng][s] = v
            out.append((s, v))
        return out

    def op(self, eng, fn, deps=()):
        deps = list(deps) + self.pend[eng]
        self.pend[eng] = []
        waits = self._waits(eng, deps)
        s = "g_" + eng
        self.cnt[s] += 1
        tok = (s, self.cnt[s])
        self.q[eng].append((fn, waits, [(s, 1)]))
        return tok

    def dma(self, eng, out_ap, in_ap, sem, deps=(), **kw):
        deps = list(deps) + self.pend[eng]
        self.pend[eng] = []
        waits = self._waits(eng, deps)
        self._mksem(sem)
        self.cnt[sem] += 16
        tok = (sem, self.cnt[sem])

        def fn(E, out_ap=out_ap, in_ap=in_ap, kw=kw):
            return E.dma_start(out_ap, in_ap, **kw)

        self.q[eng].append((fn, waits, [(sem, 16)]))
        return tok

    def fence(self, toks=None):
        if toks is None:
            toks = [(s, v) for s, v in self.cnt.items() if v > 0]
        for e in ENGS:
            self.pend[e] = list({*self.pend[e], *toks})

    def finalize(self, final_toks):
        nc = self.nc
        waits = self._waits("sp", list(final_toks))
        if waits:
            self.q["sp"].append((None, waits, []))
        with nc.Block() as block:
            emap = dict(pe=block.tensor, act=block.scalar, dve=block.vector,
                        pop=block.gpsimd, sp=block.sync)
            for e in ENGS:
                items = self.q[e]
                if not items:
                    continue

                def body(E, items=items):
                    for fn, ws, incs in items:
                        for s, v in ws:
                            E.wait_ge(self.sems[s], v)
                        if fn is None:
                            continue
                        ins = fn(E)
                        for s, amt in incs:
                            ins = ins.then_inc(self.sems[s], amt)

                emap[e](body)



_NAME_N = [0]


def _u(n):
    _NAME_N[0] += 1
    return f"{n}_{_NAME_N[0]}"


def ts(i, n):
    return slice(i * n, (i + 1) * n)



def emit_consts(bb, ar):
    nc = bb.nc
    import ml_dtypes
    ident_np = np.eye(128, dtype=np.float32).astype(ml_dtypes.bfloat16)
    onesc_np = np.ones((128, 1), dtype=np.float32).astype(ml_dtypes.bfloat16)
    onesr_np = np.ones((1, 128), dtype=np.float32)
    cvals_np = np.array([[1e-5, 1e-30, 0.0, 0.0]], dtype=np.float32)
    t_cv = nc.inline_tensor(cvals_np, "c_cvals").ap()
    t_id = nc.inline_tensor(ident_np, "c_ident").ap()
    t_oc = nc.inline_tensor(onesc_np, "c_onesc").ap()
    t_or = nc.inline_tensor(onesr_np, "c_onesr").ap()
    E = ar["E"]
    ar["ident"] = E(nc.sbuf_tensor(_u("ident"), [128, 128], BF16))
    ar["onesc"] = E(nc.sbuf_tensor(_u("onesc"), [128, 1], BF16))
    ar["onesr"] = E(nc.sbuf_tensor(_u("onesr"), [1, 128], F32))
    ar["cvals"] = E(nc.sbuf_tensor(_u("cvals"), [1, 4], F32))
    ar["onescf"] = E(nc.sbuf_tensor(_u("onescf"), [128, 1], F32))
    t0 = bb.dma("sp", ar["cvals"][:], t_cv, "ld_const")
    tcf = nc.inline_tensor(np.ones((128, 1), dtype=np.float32), "c_onescf").ap()
    t4 = bb.dma("sp", ar["onescf"][:], tcf, "ld_const")
    t1 = bb.dma("sp", ar["ident"][:], t_id, "ld_const")
    t2 = bb.dma("sp", ar["onesc"][:], t_oc, "ld_const")
    t3 = bb.dma("sp", ar["onesr"][:], t_or, "ld_const")
    ar["const_tok"] = [t0, t1, t2, t3, t4]


def stat_rows_store(bb, ar, srow, stats_hbm, tok_slice, sem, deps, width):
    s1 = bb.dma("sp", stats_hbm[0:1, tok_slice], srow[:, 0:width], sem, deps=deps)
    s2 = bb.dma("sp", stats_hbm[1:2, tok_slice], srow[:, width:2 * width], sem,
                deps=deps)
    return s2


def emit_input_stats(bb, ar, x_hbm, stats_hbm):
    nc = bb.nc
    with ExitStack() as sc:
        Es = sc.enter_context
        xt = [Es(nc.sbuf_tensor(_u(f"ist_x{i}"), [128, CH, TT], F32)) for i in range(2)]
        xb = Es(nc.sbuf_tensor(_u("ist_xb"), [128, CH, TT], BF16))
        xq = Es(nc.sbuf_tensor(_u("ist_xq"), [128, CH, TT], BF16))
        srow = Es(nc.sbuf_tensor(_u("ist_srow"), [1, 2 * TT], F32))
        p_sx = Es(nc.psum_tensor(_u("ist_psx"), [1, TT], F32))
        p_sq = Es(nc.psum_tensor(_u("ist_psq"), [1, TT], F32))
        xv = x_hbm.rearrange("(c p) t -> p c t", p=128)
        war = [None, None]
        mm_last = None
        ev_last = None
        st_tok = None
        for tt in range(NTT):
            sl = tt % 2
            ld = bb.dma("sp", xt[sl][:], xv[:, :, ts(tt, TT)], f"ist_x{sl}",
                        deps=[war[sl]])
            c1 = bb.op("act", lambda E_, s=sl: E_.copy(xb[:], xt[s][:]),
                       deps=[ld, mm_last])
            c2 = bb.op("act", lambda E_, s=sl: E_.activation(
                xq[:], xt[s][:], AF.Square), deps=[ld, mm_last])
            war[sl] = c2
            mt = mq = None
            for c in range(CH):
                mt = bb.op("pe", lambda E_, c=c: E_.matmul(
                    p_sx[:], ar["onesc"][:], xb[:, c, :],
                    start=(c == 0), stop=(c == CH - 1)),
                    deps=[c1, ev_last] + (ar["const_tok"] if tt == 0 else []))
            for c in range(CH):
                mq = bb.op("pe", lambda E_, c=c: E_.matmul(
                    p_sq[:], ar["onesc"][:], xq[:, c, :],
                    start=(c == 0), stop=(c == CH - 1)), deps=[c2, ev_last])
            mm_last = mq
            e1 = bb.op("dve", lambda E_: E_.tensor_copy(srow[:, 0:TT], p_sx[:]),
                       deps=[mt, st_tok])
            e2 = bb.op("dve", lambda E_: E_.tensor_copy(srow[:, TT:2 * TT], p_sq[:]),
                       deps=[mq, st_tok])
            ev_last = e2
            st_tok = stat_rows_store(bb, ar, srow, stats_hbm, ts(tt, TT),
                                     "ist_st", [e1, e2], TT)
        bb.fence()


def emit_rows_prologue(bb, ar, sc, stats_in, want_rms_gamma=False):
    nc = bb.nc
    Es = sc.enter_context
    m_row = Es(nc.sbuf_tensor(_u("m_row"), [1, T], F32))
    s_row = Es(nc.sbuf_tensor(_u("scl_row"), [1, T], F32))
    with ExitStack() as tsc:
        Et = tsc.enter_context
        r0 = Et(nc.sbuf_tensor(_u("rtmp0"), [1, T], F32))
        r1 = Et(nc.sbuf_tensor(_u("rtmp1"), [1, T], F32))
        r2 = Et(nc.sbuf_tensor(_u("rtmp2"), [1, T], F32))
        l0 = bb.dma("sp", r0[:], stats_in[0:1, :], "ld_rows")
        l1 = bb.dma("sp", r1[:], stats_in[1:2, :], "ld_rows")
        t1 = bb.op("dve", lambda E_: E_.tensor_scalar(
            m_row[:], r0[:], 1.0 / D, None, ALU.mult), deps=[l0])
        t2 = bb.op("dve", lambda E_: E_.tensor_mul(r0[:], m_row[:], m_row[:]),
                   deps=[t1])
        t3 = bb.op("dve", lambda E_: E_.scalar_tensor_tensor(
            r1[:], r1[:], 1.0 / D, r0[:], ALU.mult, ALU.subtract), deps=[t2, l1])
        t4 = bb.op("act", lambda E_: E_.activation(
            r2[:], r1[:], AF.Ln, bias=ar["cvals"][0:1, 0:1]),
            deps=[t3] + ar["const_tok"])
        t5 = bb.op("act", lambda E_: E_.activation(r0[:], r2[:], AF.Exp, scale=-0.5),
                   deps=[t4])
        if not want_rms_gamma:
            done = bb.op("dve", lambda E_: E_.tensor_copy(s_row[:], r0[:]),
                         deps=[t5])
        else:
            u1 = bb.op("act", lambda E_: E_.activation(
                r2[:], r1[:], AF.Ln, bias=ar["cvals"][0:1, 1:2],
                scale=float(D)), deps=[t3, t4])
            u2 = bb.op("act", lambda E_: E_.activation(
                r1[:], r2[:], AF.Exp, scale=0.5), deps=[u1])
            u3 = bb.op("dve", lambda E_: E_.tensor_mul(r1[:], r1[:], r0[:]),
                       deps=[u2, t5])
            u4 = bb.op("dve", lambda E_: E_.tensor_scalar(
                r1[:], r1[:], 1e-12, None, ALU.max), deps=[u3])
            u5 = bb.op("dve", lambda E_: E_.reciprocal(r2[:], r1[:]), deps=[u4])
            done = bb.op("dve", lambda E_: E_.scalar_tensor_tensor(
                s_row[:], r0[:], 32.0, r2[:], ALU.mult, ALU.mult), deps=[u5])
        bb.fence()
    return m_row, s_row, done


def emit_bcast2(bb, ar, p_a, p_b_, row_a, row_b, tslice, deps, sb_a, sb_b, width):
    out = []
    for p_dst, row, sb in ((p_a, row_a, sb_a), (p_b_, row_b, sb_b)):
        m = None
        for j in range(width // 512):
            m = bb.op("pe", lambda E_, p=p_dst, r=row, j=j: E_.matmul(
                p[:, ts(j, 512)], ar["onesr"][:],
                r[0:1, tslice][:, ts(j, 512)], start=True, stop=True), deps=deps)
        e = bb.op("act", lambda E_, p=p_dst, sb=sb: E_.copy(sb[:], p[:]),
                  deps=[m])
        out.append(e)
    return out


def emit_out_stats(bb, ar, src_f32, width, p_sx, p_sq, first, last, deps,
                   srow, stats_hbm, tok_slice, st_sem, shared=False):
    nxb, nxq = ar["nxb"], ar["nxq"]
    w = ar.get("stat_war")
    c1 = bb.op("act", lambda E_: E_.copy(nxb[:, 0:width], src_f32),
               deps=list(deps) + [w])
    c2 = bb.op("act", lambda E_: E_.activation(nxq[:, 0:width], src_f32, AF.Square),
               deps=list(deps) + [w])
    mt = mq = None
    for j in range(width // 512):
        mt = bb.op("pe", lambda E_, j=j: E_.matmul(
            p_sx[:, ts(j, 512)], ar["onesc"][:], nxb[:, ts(j, 512)],
            start=first, stop=last), deps=[c1, ar.get("stat_war2")])
        mq = bb.op("pe", lambda E_, j=j: E_.matmul(
            p_sq[:, ts(j, 512)], ar["onesc"][:], nxq[:, ts(j, 512)],
            start=(first and not shared), stop=last,
            skip_group_check=shared), deps=[c2, ar.get("stat_war2")])
    ar["stat_war"] = mq
    ret = mq
    if last:
        e1 = bb.op("dve", lambda E_: E_.tensor_copy(srow[:, 0:width], p_sx[:]),
                   deps=[mt, ar.get("stat_war3")])
        e2 = bb.op("dve", lambda E_: E_.tensor_copy(
            srow[:, width:2 * width], p_sq[:]), deps=[mq, ar.get("stat_war3")])
        st = stat_rows_store(bb, ar, srow, stats_hbm, tok_slice, st_sem,
                             [e1, e2], width)
        ar["stat_war2"] = e2
        ar["stat_war3"] = st
        ret = e2
    return ret


def emit_stats_batch(bb, ar, src, twidth, p_sx, p_sq, deps,
                     srow, stats_hbm, tok_slice, st_sem, shared=False,
                     src_hbm=None):
    nxb, nxq = ar["nxb"], ar["nxq"]
    nbc = ar.get("nxb_ch", CH)
    nj = twidth // 512
    mt = mq = None
    for b0 in range(0, CH, nbc):
        cs = slice(b0, b0 + nbc)
        W = nbc * twidth
        if src_hbm is not None:
            sv = (src_hbm[:, b0, tok_slice] if nbc == 1
                  else src_hbm[:, cs, tok_slice])
            dv = (nxb[:, 0:W] if nbc == 1
                  else nxb[:, 0:W].rearrange("p (c t) -> p c t", c=nbc))
            c1 = bb.dma("pop", dv, sv, "sb_ld",
                        deps=list(deps) + [ar.get("stat_war")])
            c2 = bb.op("act", lambda E_, W=W: E_.activation(
                nxq[:, 0:W], nxb[:, 0:W], AF.Square),
                deps=[c1, ar.get("stat_war")])
        else:
            c1 = bb.op("act", lambda E_, cs=cs, W=W: E_.copy(
                nxb[:, 0:W], src[:, cs, :].rearrange("p c t -> p (c t)")),
                deps=list(deps) + [ar.get("stat_war")])
            c2 = bb.op("act", lambda E_, cs=cs, W=W: E_.activation(
                nxq[:, 0:W], src[:, cs, :].rearrange("p c t -> p (c t)"),
                AF.Square), deps=list(deps) + [ar.get("stat_war")])
        for ci in range(nbc):
            c = b0 + ci
            for j in range(nj):
                o0 = ci * twidth + j * 512
                mt = bb.op("pe", lambda E_, o0=o0, j=j, c=c: E_.matmul(
                    p_sx[:, ts(j, 512)], ar["onesc"][:],
                    nxb[:, o0:o0 + 512],
                    start=(c == 0), stop=(c == CH - 1)),
                    deps=[c1, ar.get("stat_war2")])
                mq = bb.op("pe", lambda E_, o0=o0, j=j, c=c: E_.matmul(
                    p_sq[:, ts(j, 512)], ar["onesc"][:],
                    nxq[:, o0:o0 + 512],
                    start=(c == 0 and not shared), stop=(c == CH - 1),
                    skip_group_check=shared),
                    deps=[c2, ar.get("stat_war2")])
        ar["stat_war"] = mq
    e1 = bb.op("dve", lambda E_: E_.tensor_copy(srow[:, 0:twidth], p_sx[:]),
               deps=[mt, ar.get("stat_war3")])
    e2 = bb.op("dve", lambda E_: E_.tensor_copy(
        srow[:, twidth:2 * twidth], p_sq[:]), deps=[mq, ar.get("stat_war3")])
    st = stat_rows_store(bb, ar, srow, stats_hbm, tok_slice, st_sem,
                         [e1, e2], twidth)
    ar["stat_war2"] = e2
    ar["stat_war3"] = st
    return e2


def emit_apply_norm(bb, ar, x_view, z_view, mb, sb, gcol_slice, deps):
    i = ar.get("tscr_i", 0)
    t = ar["tscr"] if i == 0 else ar["tscr2"]
    war = "tscr_war" if i == 0 else "tscr_war2"
    ar["tscr_i"] = 1 - i
    s1 = bb.op("pop", lambda E_: E_.tensor_sub(t[:], x_view, mb[:]),
               deps=list(deps) + [ar.get(war)])
    if gcol_slice is not None:
        s2 = bb.op("dve", lambda E_: E_.scalar_tensor_tensor(
            z_view, t[:], gcol_slice, sb[:], ALU.mult, ALU.mult), deps=[s1])
    else:
        s2 = bb.op("dve", lambda E_: E_.tensor_mul(z_view, t[:], sb[:]),
                   deps=[s1])
    ar[war] = s2
    return s2



def emit_ffn0(bb, ar, x_in, x_out, stats_in, stats_out, compute_stats_out):
    nc = bb.nc
    t_w = ar["t_L0_ffn_weight"]
    t_alpha = ar["t_L0_ffn_alpha"]
    t_bias = ar["t_L0_ffn_bias"]

    reuse = ar.pop("ffn0_reuse", False)
    with ExitStack() as sc:
        Es = sc.enter_context
        m_row, s_row, rows_done = emit_rows_prologue(bb, ar, sc, stats_in)
        wT = Es(nc.sbuf_tensor(_u("f0_wT"), [128, CH, D], BF16))
        acol = Es(nc.sbuf_tensor(_u("f0_acol"), [128, 1], F32))
        hbias = Es(nc.sbuf_tensor(_u("f0_hbias"), [128, CH], F32))
        if "dbg_wt" in ar:
            ar["dbg_dcol_t"] = Es(nc.sbuf_tensor(_u("dbg_dc"), [128, 4], F32))

        if reuse:
            lwt = bb.dma("sp", wT[:].rearrange("p c d -> p (c d)"),
                         ar["weff_hbm"], "f0_wre")
            lac = bb.dma("sp", acol[:], ar["wmeta_hbm"][:, 0:1], "f0_wre",
                         allow_slow_non_contiguous=True)
            tr_last = None
            b4 = lac
            bb.fence([lwt, lac])
        if not reuse:
            with ExitStack() as tsc:
                Et = tsc.enter_context
                wch = [Et(nc.sbuf_tensor(_u(f"f0_w{i}"), [128, D], F32)) for i in range(2)]
                absb = Et(nc.sbuf_tensor(_u("f0_absb"), [128, D], BF16))
                scr = Et(nc.sbuf_tensor(_u("f0_scr"), [128, D], F32))
                wtn = Et(nc.sbuf_tensor(_u("f0_wtn"), [128, D], BF16))
                wtn2 = Et(nc.sbuf_tensor(_u("f0_wtn2"), [128, D], BF16))
                drow = Et(nc.sbuf_tensor(_u("f0_drow"), [1, 4], F32))
                dcol = Et(nc.sbuf_tensor(_u("f0_dcol"), [128, 1], F32))
                a11 = Et(nc.sbuf_tensor(_u("f0_a11"), [1, 1], F32))
                p_sum = Et(nc.psum_tensor(_u("f0_psum"), [1, 512], F32))
                p_bc = Et(nc.psum_tensor(_u("f0_pbc"), [128, 1], F32))
                p_tr = Et(nc.psum_tensor(_u("f0_ptr"), [128, 128], BF16))
                wv = t_w.rearrange("(c p) d -> p c d", p=128)

                war = [None, None]
                mprev = None
                for oc in range(CH):
                    sl = oc % 2
                    ld = bb.dma("pop", wch[sl][:], wv[:, oc, :], f"f0_w{sl}",
                                deps=[war[sl]])
                    ab = bb.op("act", lambda E_, s=sl: E_.activation(
                        scr[:], wch[s][:], AF.Abs),
                        deps=[ld, mprev] + ar["const_tok"])
                    war[sl] = ab
                    for j in range(2):
                        mprev = bb.op("pe", lambda E_, j=j, oc=oc: E_.matmul(
                            p_sum[:], ar["onescf"][:], scr[:, ts(j, 512)],
                            start=(oc == 0 and j == 0),
                            stop=(oc == CH - 1 and j == 1)), deps=[ab])
                r1 = bb.op("dve", lambda E_: E_.tensor_reduce(
                    drow[0:1, 0:1], p_sum[:], AX.X, ALU.add), deps=[mprev])
                r2 = bb.op("dve", lambda E_: E_.tensor_scalar(
                    drow[0:1, 1:2], drow[0:1, 0:1], 0.7 / (D * D), None, ALU.mult),
                    deps=[r1])
                b1 = bb.op("pe", lambda E_: E_.matmul(
                    p_bc[:], ar["onesr"][:], drow[0:1, 1:2], start=True, stop=True),
                    deps=[r2])
                b2 = bb.op("dve", lambda E_: E_.tensor_copy(dcol[:], p_bc[:]),
                           deps=[b1])
                la = bb.dma("sp", a11[:], t_alpha.rearrange("(a o) -> a o", o=1),
                            "ld_rows")
                b3 = bb.op("pe", lambda E_: E_.matmul(
                    p_bc[:], ar["onesr"][:], a11[:], start=True, stop=True),
                    deps=[la, b2])
                b4 = bb.op("dve", lambda E_: E_.tensor_scalar(
                    acol[:], p_bc[:], 0.5, None, ALU.mult), deps=[b3])

                if "dbg_wt" in ar:
                    bb.op("dve", lambda E_: E_.tensor_copy(
                        ar["dbg_dcol_t"][:, 0:1], dcol[:]), deps=[b2])
                    bb.op("dve", lambda E_: E_.tensor_copy(
                        ar["dbg_dcol_t"][0:1, 1:4].rearrange("a b -> a b"),
                        drow[0:1, 0:3]), deps=[b2])
                war = [r1, r1]
                war_ptr = None
                tr_last = None
                prev_a2 = prev_a4 = prev_tr = None
                for oc in range(CH):
                    sl = oc % 2
                    ld = bb.dma("pop", wch[sl][:], wv[:, oc, :], f"f0_w{sl}",
                                deps=[war[sl]])
                    a1 = bb.op("act", lambda E_, s=sl: E_.activation(
                        scr[:], wch[s][:], AF.Abs), deps=[ld, b2, prev_a2])
                    a2 = bb.op("dve", lambda E_: E_.tensor_scalar(
                        absb[:], scr[:], dcol[:], None, ALU.is_gt),
                        deps=[a1, b2, prev_a4])
                    a3 = bb.op("act", lambda E_, s=sl: E_.sign(wtn[:], wch[s][:]),
                               deps=[a1, prev_tr])
                    war[sl] = a3
                    a4 = bb.op("dve", lambda E_: E_.tensor_mul(
                        wtn2[:], wtn[:], absb[:]), deps=[a3, a2, prev_tr])
                    prev_a2, prev_a4 = a2, a4
                    for dc in range(CH):
                        tr = bb.op("pe", lambda E_, dc=dc: E_.transpose(
                            p_tr[:], wtn2[:, ts(dc, 128)], ar["ident"][:]),
                            deps=[a4, war_ptr])
                        ev = bb.op("act", lambda E_, dc=dc, oc=oc: E_.copy(
                            wT[:, dc, ts(oc, 128)], p_tr[:]), deps=[tr])
                        war_ptr = ev
                        tr_last = ev
                    prev_tr = tr_last
                bb.fence()

            if "dbg_wt" in ar:
                wtf = Es(nc.sbuf_tensor(_u("dbg_wtf"), [128, CH * D], F32))
                ct = bb.op("dve", lambda E_: E_.tensor_copy(
                    wtf[:], wT[:].rearrange("p c d -> p (c d)")), deps=[tr_last])
                bb.dma("sp", ar["dbg_wt"], wtf[:], "dbg_s", deps=[ct])
                bb.dma("sp", ar["dbg_dcol"], ar["dbg_dcol_t"][:], "dbg_s",
                       deps=[tr_last])
            sw = bb.dma("sp", ar["weff_hbm"],
                        wT[:].rearrange("p c d -> p (c d)"), "f0_wsp",
                        deps=[tr_last])
            sa = bb.dma("sp", ar["wmeta_hbm"][:, 0:1], acol[:], "f0_wsp",
                        deps=[b4], allow_slow_non_contiguous=True)
        lb = bb.dma("sp", hbias[:], t_bias.rearrange("(c p) -> p c", p=128),
                    "ld_rows", allow_slow_non_contiguous=True)
        hb = bb.op("dve", lambda E_: E_.tensor_scalar(
            hbias[:], hbias[:], 0.5, None, ALU.mult), deps=[lb])

        xt = [Es(nc.sbuf_tensor(_u(f"f0_x{i}"), [128, CH, TT], F32)) for i in range(2)]
        zt2 = [Es(nc.sbuf_tensor(_u(f"f0_z{i}"), [128, CH, TT], BF16))
               for i in range(2)]
        nx = [Es(nc.sbuf_tensor(_u(f"f0_nx{i}"), [128, CH, TT], F32)) for i in range(2)]
        sbb = Es(nc.sbuf_tensor(_u("f0_sb"), [128, TT], F32))
        mbb = Es(nc.sbuf_tensor(_u("f0_mb"), [128, TT], F32))
        dlt2 = [Es(nc.sbuf_tensor(_u(f"f0_dlt{i}"), [128, TT], F32))
                for i in range(2)]
        ar["tscr"] = Es(nc.sbuf_tensor(_u("f0_tscr"), [128, TT], F32))
        ar["tscr2"] = Es(nc.sbuf_tensor(_u("f0_tscr2"), [128, TT], F32))
        ar["nxb"] = Es(nc.sbuf_tensor(_u("f0_nxb"), [128, CH * TT], BF16))
        ar["nxq"] = Es(nc.sbuf_tensor(_u("f0_nxq"), [128, CH * TT], BF16))
        ar["nxb_ch"] = CH
        srow = Es(nc.sbuf_tensor(_u("f0_srow"), [1, 2 * TT], F32))
        p_mm = [Es(nc.psum_tensor(_u(f"f0_pm{i}"), [128, 512], F32)) for i in range(2)]
        p_ba = Es(nc.psum_tensor(_u("f0_pba"), [128, 512], F32))
        p_bb2 = Es(nc.psum_tensor(_u("f0_pbb"), [128, 512], F32))
        p_sx = Es(nc.psum_tensor(_u("f0_psx"), [1, TT], F32))
        p_sq = Es(nc.psum_tensor(_u("f0_psq"), [1, TT], F32))

        xv = x_in.rearrange("(c p) t -> p c t", p=128)
        ov = x_out.rearrange("(c p) t -> p c t", p=128)
        war_x = [None, None]
        war_nx = [None, None]
        war_z = None
        war_bc = None
        war_mm = [None, None]
        war_dlt = None
        stats_job = None
        ld_f0 = {0: bb.dma("sp", xt[0][:], xv[:, :, ts(0, TT)], "f0_x0")}
        bc_toks = {0: emit_bcast2(bb, ar, p_ba, p_bb2, s_row, m_row,
                                  ts(0, TT), [rows_done, None], sbb, mbb, TT)}
        appl = {}
        war_z2 = [None, None]

        def e_f0_apply(tt1, cs):
            s1_ = tt1 % 2
            toks = appl.setdefault(tt1, [])
            for c in cs:
                toks.append(emit_apply_norm(
                    bb, ar, xt[s1_][:, c, :], zt2[s1_][:, c, :], mbb, sbb,
                    None, deps=[ld_f0[tt1], war_z2[s1_]] + bc_toks[tt1]))

        e_f0_apply(0, range(CH))
        for tt in range(NTT):
            sl = tt % 2
            nsl2 = (tt + 1) % 2
            if tt + 1 < NTT:
                ld_f0[tt + 1] = bb.dma(
                    "sp", xt[nsl2][:], xv[:, :, ts(tt + 1, TT)],
                    f"f0_x{nsl2}", deps=[war_x[nsl2]])
                bc_toks[tt + 1] = emit_bcast2(
                    bb, ar, p_ba, p_bb2, s_row, m_row, ts(tt + 1, TT),
                    [rows_done, appl[tt][-1]], sbb, mbb, TT)
            zt_toks = appl[tt]
            ep_toks = []
            last_mm = None
            for oc in range(CH):
                psl = oc % 2
                mm = None
                for kc in range(CH):
                    mm = bb.op("pe", lambda E_, kc=kc, oc=oc, p=psl, s=sl:
                               E_.matmul(
                        p_mm[p][:], wT[:, kc, ts(oc, 128)], zt2[s][:, kc, :],
                        start=(kc == 0), stop=(kc == CH - 1)),
                        deps=[zt_toks[kc], war_mm[psl]] +
                        ([tr_last] if tt == 0 and oc == 0 else []))
                last_mm = mm
                dl_ = dlt2[oc % 2]
                e1 = bb.op("act", lambda E_, oc=oc, p=psl, dl_=dl_: E_.activation(
                    dl_[:], p_mm[p][:], AF.Identity, bias=hbias[:, oc:oc + 1],
                    scale=acol[:]), deps=[mm, hb, b4, war_dlt])
                e2 = bb.op("dve", lambda E_, oc=oc, s=sl, dl_=dl_: E_.tensor_add(
                    nx[s][:, oc, :], xt[s][:, oc, :], dl_[:]),
                    deps=[e1, war_nx[sl], ar.get("stat_war")])
                war_mm[psl] = e1
                war_dlt = ep_toks[-1] if ep_toks else None
                ep_toks.append(e2)
                if tt + 1 < NTT:
                    e_f0_apply(tt + 1, [oc])
            war_x[sl] = ep_toks[-1]
            war_z2[sl] = last_mm
            st = bb.dma("sp", ov[:, :, ts(tt, TT)], nx[sl][:], "st_x",
                        deps=ep_toks)
            war_nx[sl] = st
            if compute_stats_out:
                if stats_job is not None:
                    emit_stats_batch(bb, ar, *stats_job)
                stats_job = (nx[sl][:], TT, p_sx, p_sq, [ep_toks[-1]],
                             srow, stats_out, ts(tt, TT), "f0_sst")
        if compute_stats_out and stats_job is not None:
            emit_stats_batch(bb, ar, *stats_job)
        bb.fence()



def emit_rwkv(bb, ar, x_in, x_out, stats_in, stats_out, compute_stats_out):
    nc = bb.nc
    t_ws = {"r": ar["t_L1_ffn_W_r"], "k": ar["t_L1_ffn_W_k"],
            "v": ar["t_L1_ffn_W_v"]}

    with ExitStack() as sc:
        Es = sc.enter_context
        m_row, s_row, rows_done = emit_rows_prologue(bb, ar, sc, stats_in)
        wT = {n: Es(nc.sbuf_tensor(_u(f"rw_wT{n}"), [128, CH, D], BF16))
              for n in ("r", "k", "v")}
        with ExitStack() as tsc:
            Et = tsc.enter_context
            wch = [Et(nc.sbuf_tensor(_u(f"rw_w{i}"), [128, D], BF16)) for i in range(2)]
            p_tr = Et(nc.psum_tensor(_u("rw_ptr"), [128, 128], BF16))
            war = [None, None]
            war_ptr = None
            tr_last = None
            for name in ("r", "k", "v"):
                wv_ = t_ws[name].rearrange("(c p) d -> p c d", p=128)
                for dc in range(CH):
                    sl = dc % 2
                    ld = bb.dma("pop", wch[sl][:], wv_[:, dc, :], f"rw_w{sl}",
                                deps=[war[sl]])
                    for cc in range(CH):
                        tr = bb.op("pe", lambda E_, cc=cc, s=sl: E_.transpose(
                            p_tr[:], wch[s][:, ts(cc, 128)], ar["ident"][:]),
                            deps=[ld, war_ptr] + ar["const_tok"])
                        ev = bb.op("act", lambda E_, cc=cc, dc=dc, n=name: E_.copy(
                            wT[n][:, cc, ts(dc, 128)], p_tr[:]), deps=[tr])
                        war_ptr = ev
                        tr_last = ev
                    war[sl] = tr_last
            bb.fence()

        mur = Es(nc.sbuf_tensor(_u("rw_mur"), [128, CH], F32))
        muk = Es(nc.sbuf_tensor(_u("rw_muk"), [128, CH], F32))
        lr = bb.dma("sp", mur[:], ar["t_L1_ffn_mu_r"].rearrange("(c p) -> p c", p=128),
                    "ld_rows", allow_slow_non_contiguous=True)
        lk = bb.dma("sp", muk[:], ar["t_L1_ffn_mu_k"].rearrange("(c p) -> p c", p=128),
                    "ld_rows", allow_slow_non_contiguous=True)

        xt = [Es(nc.sbuf_tensor(_u(f"rw_x{i}"), [128, CH, TT], F32)) for i in range(2)]
        xn = [Es(nc.sbuf_tensor(_u(f"rw_xn{i}"), [128, CH, TT + 1], F32))
              for i in range(2)]
        dt_ = Es(nc.sbuf_tensor(_u("rw_d"), [128, TT], F32))
        xr = Es(nc.sbuf_tensor(_u("rw_xr"), [128, CH, TT], BF16))
        xk = Es(nc.sbuf_tensor(_u("rw_xk"), [128, CH, TT], BF16))

        k2 = Es(nc.sbuf_tensor(_u("rw_k2"), [128, CH, TT], BF16))

        rt = Es(nc.sbuf_tensor(_u("rw_r"), [128, TT], F32))
        rl = Es(nc.sbuf_tensor(_u("rw_rl"), [128, TT], F32))
        dlt = Es(nc.sbuf_tensor(_u("rw_dlt"), [128, TT], F32))
        nx = Es(nc.sbuf_tensor(_u("rw_nx"), [128, CH, TT], F32))
        sbb = Es(nc.sbuf_tensor(_u("rw_sb"), [128, TT], F32))
        mbb = Es(nc.sbuf_tensor(_u("rw_mb"), [128, TT], F32))

        ar["tscr"] = Es(nc.sbuf_tensor(_u("rw_tscr"), [128, TT], F32))
        ar["tscr2"] = Es(nc.sbuf_tensor(_u("rw_tscr2"), [128, TT], F32))
        ar["nxb"] = Es(nc.sbuf_tensor(_u("rw_nxb"), [128, TT], BF16))
        ar["nxq"] = Es(nc.sbuf_tensor(_u("rw_nxq"), [128, TT], BF16))
        ar["nxb_ch"] = 1
        srow = Es(nc.sbuf_tensor(_u("rw_srow"), [1, 2 * TT], F32))
        p_mm = [Es(nc.psum_tensor(_u(f"rw_pm{i}"), [128, 512], F32)) for i in range(2)]
        p_ba = Es(nc.psum_tensor(_u("rw_pba"), [128, 512], F32))
        p_bb2 = Es(nc.psum_tensor(_u("rw_pbb"), [128, 512], F32))
        p_sx = Es(nc.psum_tensor(_u("rw_psx"), [1, TT], F32))
        p_sq = Es(nc.psum_tensor(_u("rw_psq"), [1, TT], F32))

        hz = bb.op("dve", lambda E_: E_.memset(xn[0][:, :, 0:1], 0.0))

        xv = x_in.rearrange("(c p) t -> p c t", p=128)
        ov = x_out.rearrange("(c p) t -> p c t", p=128)
        W = {"x": [None, None], "nx": None, "xn": [[hz], [hz]],
             "mm": [None, None], "rl": None, "dlt": None, "bc": None,
             "dt": None, "xr": None, "xk": None}
        apply_toks, kmix_toks, rmix_toks = {}, {}, {}
        bc_toks, ld_toks = {}, {}
        halo_toks = {0: hz}
        stats_job = None

        def e_ld(tt):
            sl = tt % 2
            ld_toks[tt] = bb.dma("sp", xt[sl][:], xv[:, :, ts(tt, TT)],
                                 f"rw_x{sl}", deps=[W["x"][sl]])

        def e_bc(tt):
            bc_toks[tt] = emit_bcast2(bb, ar, p_ba, p_bb2, s_row, m_row,
                                      ts(tt, TT), [rows_done, W["bc"]],
                                      sbb, mbb, TT)

        def e_apply(tt, cs):
            sl = tt % 2
            nsl = (tt + 1) % 2
            toks = apply_toks.setdefault(tt, [])
            for c in cs:
                i_ = ar.get("tscr_i", 0)
                t_ = ar["tscr"] if i_ == 0 else ar["tscr2"]
                wkey = "tscr_war" if i_ == 0 else "tscr_war2"
                ar["tscr_i"] = 1 - i_
                s1 = bb.op("pop", lambda E_, c=c, s=sl, t_=t_: E_.tensor_sub(
                    t_[:], xt[s][:, c, :], mbb[:]),
                    deps=[ld_toks[tt], ar.get(wkey)] + W["xn"][sl] +
                    bc_toks[tt])
                s2 = bb.op("dve", lambda E_, c=c, s=sl, t_=t_: E_.tensor_mul(
                    xn[s][:, c, 1:TT + 1], t_[:], sbb[:]), deps=[s1])
                ar[wkey] = s2
                toks.append(s2)
            if len(toks) == CH:
                W["bc"] = toks[-1]
                if tt + 1 < NTT:
                    halo_toks[tt + 1] = bb.op(
                        "dve", lambda E_, s=sl, nn=nsl: E_.tensor_copy(
                            xn[nn][:, :, 0:1], xn[s][:, :, TT:TT + 1]),
                        deps=toks + W["xn"][nsl])

        def e_kmix(tt, cs):
            sl = tt % 2
            toks = kmix_toks.setdefault(tt, [])
            for c in cs:
                d1 = bb.op("pop", lambda E_, c=c, s=sl: E_.tensor_sub(
                    dt_[:], xn[s][:, c, 1:TT + 1], xn[s][:, c, 0:TT]),
                    deps=[apply_toks[tt][c], halo_toks[tt], W["dt"]])
                k1 = bb.op("dve", lambda E_, c=c, s=sl: E_.scalar_tensor_tensor(
                    xk[:, c, :], dt_[:], muk[:, c:c + 1],
                    xn[s][:, c, 0:TT], ALU.mult, ALU.add),
                    deps=[d1, lk, W["xk"]])
                W["dt"] = k1
                toks.append(k1)

        def e_rmix(tt):
            sl = tt % 2
            toks = rmix_toks.setdefault(tt, [])
            for c in range(CH):
                d1 = bb.op("pop", lambda E_, c=c, s=sl: E_.tensor_sub(
                    dt_[:], xn[s][:, c, 1:TT + 1], xn[s][:, c, 0:TT]),
                    deps=[apply_toks[tt][c], halo_toks[tt], W["dt"]])
                r1 = bb.op("dve", lambda E_, c=c, s=sl: E_.scalar_tensor_tensor(
                    xr[:, c, :], dt_[:], mur[:, c:c + 1],
                    xn[s][:, c, 0:TT], ALU.mult, ALU.add),
                    deps=[d1, lr, W["xr"]])
                W["dt"] = r1
                toks.append(r1)
            W["xn"][sl] = [toks[-1]] + \
                ([halo_toks[tt + 1]] if tt + 1 in halo_toks else [])

        e_ld(0)
        e_bc(0)
        e_apply(0, range(CH))
        e_kmix(0, range(CH))
        for tt in range(NTT):
            sl = tt % 2
            e_rmix(tt)
            k_toks = []
            k_last = None
            for oc in range(CH):
                psl = oc % 2
                mm = None
                for kc in range(CH):
                    mm = bb.op("pe", lambda E_, kc=kc, oc=oc, p=psl: E_.matmul(
                        p_mm[p][:], wT["k"][:, kc, ts(oc, 128)], xk[:, kc, :],
                        start=(kc == 0), stop=(kc == CH - 1)),
                        deps=[kmix_toks[tt][kc], W["mm"][psl]] +
                        ([tr_last] if tt == 0 and oc == 0 else []))
                k_last = mm
                e1 = bb.op("act", lambda E_, p=psl: E_.activation(
                    rl[:], p_mm[p][:], AF.Relu), deps=[mm, W["rl"]])
                e2 = bb.op("act", lambda E_, oc=oc: E_.activation(
                    k2[:, oc, :], rl[:], AF.Square), deps=[e1])
                W["mm"][psl] = e2
                W["rl"] = e2
                k_toks.append(e2)
            W["xk"] = k_last
            if compute_stats_out and stats_job is not None:
                emit_stats_batch(bb, ar, *stats_job, src_hbm=ov)
                stats_job = None
            if tt + 1 < NTT:
                e_ld(tt + 1)
                e_bc(tt + 1)
            ep_toks = []
            war_rt = W["dlt"]
            r_last = None
            for oc in range(CH):
                psl = oc % 2
                mm = None
                for kc in range(CH):
                    mm = bb.op("pe", lambda E_, kc=kc, oc=oc, p=psl: E_.matmul(
                        p_mm[p][:], wT["r"][:, kc, ts(oc, 128)], xr[:, kc, :],
                        start=(kc == 0), stop=(kc == CH - 1)),
                        deps=[rmix_toks[tt][kc], W["mm"][psl]])
                r_last = mm
                sg = bb.op("act", lambda E_, p=psl: E_.activation(
                    rt[:], p_mm[p][:], AF.Sigmoid), deps=[mm, war_rt])
                W["mm"][psl] = sg
                mm = None
                for kc in range(CH):
                    mm = bb.op("pe", lambda E_, kc=kc, oc=oc, p=psl: E_.matmul(
                        p_mm[p][:], wT["v"][:, kc, ts(oc, 128)], k2[:, kc, :],
                        start=(kc == 0), stop=(kc == CH - 1)),
                        deps=[k_toks[kc], W["mm"][psl]])
                e1 = bb.op("dve", lambda E_, oc=oc, p=psl:
                           E_.scalar_tensor_tensor(
                    dlt[:], rt[:], 0.5, p_mm[p][:], ALU.mult,
                    ALU.mult), deps=[mm, sg, W["dlt"]])
                war_rt = e1
                e2 = bb.op("pop", lambda E_, oc=oc, s=sl: E_.tensor_add(
                    nx[:, oc, :], xt[s][:, oc, :], dlt[:]),
                    deps=[e1, W["nx"]])
                W["mm"][psl] = e1
                W["dlt"] = e2
                ep_toks.append(e2)
                if tt + 1 < NTT:
                    if oc < 4:
                        e_apply(tt + 1, [2 * oc, 2 * oc + 1])
                    else:
                        e_kmix(tt + 1, [2 * (oc - 4), 2 * (oc - 4) + 1])
            W["xr"] = r_last
            W["x"][sl] = ep_toks[-1]
            st = bb.dma("sp", ov[:, :, ts(tt, TT)], nx[:], "st_x",
                        deps=ep_toks)
            W["nx"] = st
            if compute_stats_out:
                stats_job = (None, TT, p_sx, p_sq, [st], srow, stats_out,
                             ts(tt, TT), "rw_sst")
        if compute_stats_out and stats_job is not None:
            emit_stats_batch(bb, ar, *stats_job, src_hbm=ov)
        bb.fence()



def emit_attn0(bb, ar, x_in, x_out, stats_in, stats_out, compute_stats_out):
    nc = bb.nc
    q_hbm, k_hbm = ar["q_hbm"], ar["k_hbm"]
    QT = 1024
    NQT = T // QT

    with ExitStack() as sc:
        Es = sc.enter_context
        vplus = Es(nc.sbuf_tensor(_u("a0_vplus"), [128, 32, NH, 66], BF16))
        m_row, s_row, rows_done = emit_rows_prologue(bb, ar, sc, stats_in,
                                                     want_rms_gamma=True)
        gcol = Es(nc.sbuf_tensor(_u("a0_gcol"), [128, CH], F32))
        lg = bb.dma("sp", gcol[:],
                    ar["t_L0_attn_gamma"].rearrange("(c p) -> p c", p=128),
                    "ld_rows", allow_slow_non_contiguous=True)
        lg2 = bb.op("dve", lambda E_: E_.tensor_scalar(
            gcol[:], gcol[:], 1.0, None, ALU.add), deps=[lg])
        vset = bb.op("dve", lambda E_: E_.memset(vplus[:], 1.0))
        negCall = Es(nc.sbuf_tensor(_u("a0_negCall"), [128, CH], F32))

        with ExitStack() as p1:
            Ep = p1.enter_context
            wq = Ep(nc.sbuf_tensor(_u("a0_wq"), [128, CH, 3 * D], BF16))
            lw = []
            wv_ = ar["t_L0_attn_w_qkv"].rearrange("(c p) n -> p c n", p=128)
            for kc in range(CH):
                lw.append(bb.dma("pop", wq[:, kc, :], wv_[:, kc, :], "a0_wld"))
            xt = [Ep(nc.sbuf_tensor(_u(f"a0_x{i}"), [128, CH, TT], F32))
                  for i in range(2)]
            zt = Ep(nc.sbuf_tensor(_u("a0_z"), [128, CH, TT], BF16))
            stg = [Ep(nc.sbuf_tensor(_u(f"a0_stg{i}"), [128, TT], BF16))
                   for i in range(2)]
            sbb = Ep(nc.sbuf_tensor(_u("a0_sb"), [128, TT], F32))
            mbb = Ep(nc.sbuf_tensor(_u("a0_mb"), [128, TT], F32))
            ar["tscr"] = Ep(nc.sbuf_tensor(_u("a0_tscr"), [128, TT], F32))
            ar["tscr2"] = Ep(nc.sbuf_tensor(_u("a0_tscr2"), [128, TT], F32))
            p_mm = [Ep(nc.psum_tensor(_u(f"a0_pm{i}"), [128, 512], F32))
                    for i in range(2)]
            p_v = Ep(nc.psum_tensor(_u("a0_pv"), [128, 1024], F32))
            p_ba = Ep(nc.psum_tensor(_u("a0_pba"), [128, 512], F32))
            p_bb2 = Ep(nc.psum_tensor(_u("a0_pbb"), [128, 512], F32))
            p_st = Ep(nc.psum_tensor(_u("a0_pst"), [128, 512], F32))
            sqsc = [Ep(nc.sbuf_tensor(_u(f"a0_sqsc{i}"), [128, TT], BF16))
                    for i in range(2)]
            qkmax = Ep(nc.sbuf_tensor(_u("a0_qkmax"), [1, 64], F32))
            kbarf = Ep(nc.sbuf_tensor(_u("a0_kbarf"), [128, CH], F32))
            kbred = Ep(nc.sbuf_tensor(_u("a0_kbred"), [128, 1], F32))
            kb2c = Ep(nc.sbuf_tensor(_u("a0_kb2"), [128, CH], BF16))
            qkz = bb.op("dve", lambda E_: E_.memset(qkmax[:], 0.0))
            kbz = bb.op("dve", lambda E_: E_.memset(kbarf[:], 0.0))

            xv = x_in.rearrange("(c p) t -> p c t", p=128)
            war_x = [None, None]
            war_z = None
            war_bc = None
            war_mm = [None, None]
            war_stg = [[], []]
            war_pv = None

            def e_qkstat(oc, psl, sq, ev):
                msq = bb.op("pe", lambda E_, s=psl: E_.matmul(
                    p_st[0:1, :], ar["onesc"][:], sqsc[s][:],
                    start=True, stop=True),
                    deps=[sq, ar.get("a0_pst_war")])
                rmx = bb.op("dve", lambda E_, oc=oc: E_.tensor_reduce(
                    qkmax[0:1, 32 + oc % 2:33 + oc % 2], p_st[0:1, :],
                    AX.X, ALU.max), deps=[msq, qkz])
                upd = bb.op("dve", lambda E_, oc=oc: E_.tensor_tensor(
                    qkmax[0:1, oc:oc + 1], qkmax[0:1, oc:oc + 1],
                    qkmax[0:1, 32 + oc % 2:33 + oc % 2], ALU.max),
                    deps=[rmx])
                ar[f"a0_sq_war{psl}"] = msq
                ar["a0_pst_war"] = rmx
                out = upd
                if oc >= 8:
                    kr = bb.op("dve", lambda E_, s=psl: E_.tensor_reduce(
                        kbred[:], stg[s][:], AX.X, ALU.add),
                        deps=[ev, ar.get("a0_kbr_war")])
                    ka = bb.op("dve", lambda E_, oc=oc: E_.tensor_add(
                        kbarf[:, oc - 8:oc - 7], kbarf[:, oc - 8:oc - 7],
                        kbred[:]), deps=[kr, kbz])
                    ar["a0_kbr_war"] = ka
                    war_stg[psl].append(kr)
                    out = ka
                return out

            ld_p1 = {0: bb.dma("sp", xt[0][:], xv[:, :, ts(0, TT)], "a0_x0")}
            for tt in range(NTT):
                sl = tt % 2
                ld = ld_p1[tt]
                bc = emit_bcast2(bb, ar, p_ba, p_bb2, s_row, m_row, ts(tt, TT),
                                 [rows_done, war_bc], sbb, mbb, TT)
                zt_toks = []
                for c in range(CH):
                    zt_toks.append(emit_apply_norm(
                        bb, ar, xt[sl][:, c, :], zt[:, c, :], mbb, sbb,
                        gcol[:, c:c + 1], deps=[ld, war_z, lg2] + bc))
                war_bc = zt_toks[-1]
                war_x[sl] = zt_toks[-1]
                if tt + 1 < NTT:
                    nsl = (tt + 1) % 2
                    ld_p1[tt + 1] = bb.dma(
                        "sp", xt[nsl][:], xv[:, :, ts(tt + 1, TT)],
                        f"a0_x{nsl}", deps=[war_x[nsl]])
                pend_st = None
                for oc in range(16):
                    psl = oc % 2
                    mm = None
                    for kc in range(CH):
                        mm = bb.op("pe", lambda E_, kc=kc, oc=oc, p=psl:
                                   E_.matmul(
                            p_mm[p][:], wq[:, kc, ts(oc, 128)], zt[:, kc, :],
                            start=(kc == 0), stop=(kc == CH - 1)),
                            deps=[zt_toks[kc], lw[kc], war_mm[psl]])
                    if pend_st is not None:
                        stat_last = e_qkstat(*pend_st)
                        pend_st = None
                    scale = 0.125 if oc < 8 else 1.0
                    ev = bb.op("act", lambda E_, p=psl, s=psl, sc_=scale:
                               E_.activation(stg[s][:], p_mm[p][:], AF.Copy,
                                             scale=sc_),
                               deps=[mm] + war_stg[psl])
                    war_mm[psl] = ev
                    dst = q_hbm if oc < 8 else k_hbm
                    st = bb.dma("sp", dst[oc % 8, :, ts(tt, TT)], stg[psl][:],
                                "a0_qkst", deps=[ev])
                    sq = bb.op("dve", lambda E_, s=psl: E_.tensor_mul(
                        sqsc[s][:], stg[s][:], stg[s][:]),
                        deps=[ev, ar.get(f"a0_sq_war{psl}")])
                    war_stg[psl] = [st, sq]
                    pend_st = (oc, psl, sq, ev)
                stat_last = e_qkstat(*pend_st)
                for tsub in range(TT // 128):
                    ktg = tt * 4 + tsub
                    mm = None
                    for kc in range(CH):
                        for j in range(2):
                            mm = bb.op("pe", lambda E_, kc=kc, j=j, t0=tsub:
                                       E_.matmul(
                                p_v[:, ts(j, 512)], zt[:, kc, ts(t0, 128)],
                                wq[:, kc, 2048 + j * 512:2048 + (j + 1) * 512],
                                start=(kc == 0), stop=(kc == CH - 1)),
                                deps=[zt_toks[kc], war_pv])
                    ev = bb.op("act", lambda E_, k=ktg: E_.copy(
                        vplus[:, k, :, 0:64],
                        p_v[:].rearrange("p (h d) -> p h d", h=NH)),
                        deps=[mm, vset])
                    war_pv = ev
                    war_z = ev
            kq = bb.op("dve", lambda E_: E_.tensor_mul(
                kb2c[:], kbarf[:], kbarf[:]),
                deps=[ar.get("a0_kbr_war"), kbz])
            kbn_ev = []
            for hh in range(2):
                mmk = bb.op("pe", lambda E_, hh=hh: E_.matmul(
                    p_st[0:1, 0:CH], ar["onesc"][64 * hh:64 * hh + 64, :],
                    kb2c[64 * hh:64 * hh + 64, :], start=True, stop=True),
                    deps=[kq, ar.get("a0_pst_war"), stat_last] + kbn_ev)
                evk = bb.op("dve", lambda E_, hh=hh: E_.tensor_copy(
                    qkmax[0:1, 40 + 8 * hh:48 + 8 * hh], p_st[0:1, 0:CH]),
                    deps=[mmk])
                kbn_ev.append(evk)
            cdone = []
            war_nc = None
            for p in range(CH):
                t1 = bb.op("dve", lambda E_, p=p: E_.tensor_tensor(
                    qkmax[0:1, 56:57], qkmax[0:1, 2 * p:2 * p + 1],
                    qkmax[0:1, 2 * p + 1:2 * p + 2], ALU.max),
                    deps=[stat_last] + cdone + kbn_ev)
                t2 = bb.op("dve", lambda E_, p=p: E_.tensor_tensor(
                    qkmax[0:1, 57:58], qkmax[0:1, 8 + 2 * p:9 + 2 * p],
                    qkmax[0:1, 9 + 2 * p:10 + 2 * p], ALU.max), deps=[t1])
                t3 = bb.op("dve", lambda E_, p=p: E_.tensor_tensor(
                    qkmax[0:1, 58:59], qkmax[0:1, 40 + p:41 + p],
                    qkmax[0:1, 48 + p:49 + p], ALU.max), deps=[t2])
                l1 = bb.op("act", lambda E_: E_.activation(
                    qkmax[0:1, 59:60], qkmax[0:1, 56:57], AF.Ln,
                    bias=ar["cvals"][0:1, 1:2]), deps=[t3])
                l2 = bb.op("act", lambda E_: E_.activation(
                    qkmax[0:1, 60:61], qkmax[0:1, 57:58], AF.Ln,
                    bias=ar["cvals"][0:1, 1:2]), deps=[l1])
                l3 = bb.op("act", lambda E_: E_.activation(
                    qkmax[0:1, 61:62], qkmax[0:1, 58:59], AF.Ln,
                    bias=ar["cvals"][0:1, 1:2]), deps=[l2])
                a1 = bb.op("dve", lambda E_: E_.tensor_add(
                    qkmax[0:1, 62:63], qkmax[0:1, 59:60], qkmax[0:1, 60:61]),
                    deps=[l2, l3])
                a2 = bb.op("act", lambda E_: E_.activation(
                    qkmax[0:1, 62:63], qkmax[0:1, 62:63], AF.Exp, scale=0.5),
                    deps=[a1])
                b1 = bb.op("dve", lambda E_: E_.tensor_add(
                    qkmax[0:1, 63:64], qkmax[0:1, 59:60], qkmax[0:1, 61:62]),
                    deps=[a2, l3])
                b2 = bb.op("act", lambda E_: E_.activation(
                    qkmax[0:1, 63:64], qkmax[0:1, 63:64], AF.Exp, scale=0.5),
                    deps=[b1])
                b3 = bb.op("dve", lambda E_: E_.tensor_scalar(
                    qkmax[0:1, 63:64], qkmax[0:1, 63:64], -1.0 / T, None,
                    ALU.mult), deps=[b2])
                c1 = bb.op("dve", lambda E_: E_.tensor_scalar(
                    qkmax[0:1, 63:64], qkmax[0:1, 63:64], 60.0, None,
                    ALU.add), deps=[b3])
                c2 = bb.op("dve", lambda E_: E_.tensor_tensor(
                    qkmax[0:1, 63:64], qkmax[0:1, 62:63], qkmax[0:1, 63:64],
                    ALU.min), deps=[c1])
                c3 = bb.op("dve", lambda E_: E_.tensor_scalar(
                    qkmax[0:1, 62:63], qkmax[0:1, 62:63], -70.0, None,
                    ALU.add), deps=[c2])
                c4 = bb.op("dve", lambda E_: E_.tensor_tensor(
                    qkmax[0:1, 63:64], qkmax[0:1, 63:64], qkmax[0:1, 62:63],
                    ALU.max), deps=[c3])
                c5 = bb.op("dve", lambda E_: E_.tensor_scalar(
                    qkmax[0:1, 63:64], qkmax[0:1, 63:64], -1.0, None,
                    ALU.mult), deps=[c4])
                bcm = bb.op("pe", lambda E_, p=p: E_.matmul(
                    p_st[:, 0:1], ar["onesr"][:], qkmax[0:1, 63:64],
                    start=True, stop=True), deps=[c5, war_nc, kbn_ev[-1]])
                ncv = bb.op("dve", lambda E_, p=p: E_.tensor_copy(
                    negCall[:, p:p + 1], p_st[:, 0:1]), deps=[bcm])
                war_nc = ncv
                cdone.append(ncv)
            ar["a0_c_done"] = cdone
            bb.fence()

        QW = 512
        NQW = T // QW
        NSLOT = 3
        qbuf = Es(nc.sbuf_tensor(_u("a0_qb"), [128, CH, QW], BF16))
        kbuf = [Es(nc.sbuf_tensor(_u(f"a0_kb{i}"), [128, T], BF16))
                for i in range(2)]
        ptil = [Es(nc.sbuf_tensor(_u(f"a0_pt{i}"), [128, 1024], BF16))
                for i in range(NSLOT)]
        attnT = Es(nc.sbuf_tensor(_u("a0_attnT"), [128, CH, QW], BF16))
        numsA = Es(nc.sbuf_tensor(_u("a0_numsA"), [64, QW], F32))
        numsB = Es(nc.sbuf_tensor(_u("a0_numsB"), [64, QW], F32))
        rdenA = Es(nc.sbuf_tensor(_u("a0_rdenA"), [1, QW], F32))
        rdenB = Es(nc.sbuf_tensor(_u("a0_rdenB"), [1, QW], F32))
        wo = Es(nc.sbuf_tensor(_u("a0_wo"), [128, CH, D], BF16))
        xsl = [Es(nc.sbuf_tensor(_u(f"a0_xs{i}"), [128, QW], F32))
               for i in range(2)]
        nxs = Es(nc.sbuf_tensor(_u("a0_nxs"), [128, QW], F32))
        ar["nxb"] = Es(nc.sbuf_tensor(_u("a0_nxb"), [128, QW], BF16))
        ar["nxq"] = Es(nc.sbuf_tensor(_u("a0_nxq"), [128, QW], BF16))
        srow = Es(nc.sbuf_tensor(_u("a0_srow"), [1, 2 * QW], F32))
        p_S = [Es(nc.psum_tensor(_u(f"a0_pS{i}"), [128, 1024], F32))
               for i in range(NSLOT)]
        p_oA = Es(nc.psum_tensor(_u("a0_poA"), [65, 512], F32))
        p_oB = Es(nc.psum_tensor(_u("a0_poB"), [65, 512], F32))

        wov = ar["t_L0_attn_w_out"].rearrange("(c p) n -> p c n", p=128)
        lwo = [bb.dma("pop", wo[:, ic, :], wov[:, ic, :], "a0_wld")
               for ic in range(CH)]

        war_q = None
        war_k = [None, None]
        war_attnT = None
        war_xr = [None, None]
        war_nxs = None
        war_oA = None
        war_oB = None
        exp_toks = [None] * NSLOT
        av_toks = [None] * NSLOT
        qv = q_hbm.rearrange("c p t -> p c t")
        xrv = x_in.rearrange("(c p) t -> p c t", p=128)
        ov = x_out.rearrange("(c p) t -> p c t", p=128)
        cdone = ar["a0_c_done"]

        NP = NH // 2
        lk_tok = [None] * NP
        for qt in range(NQW):
            lq = bb.dma("sp", qbuf[:], qv[:, :, ts(qt, QW)], "a0_q0",
                        deps=[war_q])
            if qt == 0:
                lk_tok[0] = bb.dma("sp", kbuf[0][:], k_hbm[0, :, :], "a0_k0",
                                   deps=[war_k[0]])
            for p in range(NP):
                psl = p % 2
                qhA = qbuf[0:64, p, :]
                qhB = qbuf[64:128, p, :]
                lk = lk_tok[p]
                ncc = cdone[p]
                if p + 1 < NP:
                    npsl = (p + 1) % 2
                    lk_tok[p + 1] = bb.dma(
                        "sp", kbuf[npsl][:], k_hbm[p + 1, :, :],
                        f"a0_k{npsl}", deps=[war_k[npsl]])
                elif qt + 1 < NQW:
                    lk_tok[0] = bb.dma("sp", kbuf[0][:], k_hbm[0, :, :],
                                       "a0_k0", deps=[war_k[0]])
                pend = []
                last_avA = last_avB = None
                for kt in range(34):
                    if kt < 32:
                        Ssl = kt % NSLOT
                        mmA = bb.op("pe", lambda E_, kt=kt, s=Ssl, ks=psl,
                                    qh=qhA: E_.matmul(
                            p_S[s][:, 0:512], kbuf[ks][0:64, ts(kt, 128)],
                            qh[:], start=True, stop=True,
                            tile_position=(0, 0)),
                            deps=[lk, lq, exp_toks[Ssl]])
                        mmB = bb.op("pe", lambda E_, kt=kt, s=Ssl, ks=psl,
                                    qh=qhB: E_.matmul(
                            p_S[s][:, 512:1024], kbuf[ks][64:128, ts(kt, 128)],
                            qh[:], start=True, stop=True,
                            tile_position=(64, 0)),
                            deps=[lk])
                        ex = bb.op("act", lambda E_, s=Ssl, p=p: E_.activation(
                            ptil[s][:], p_S[s][:], AF.Exp,
                            bias=negCall[:, p:p + 1]),
                            deps=[mmA, mmB, ncc, av_toks[Ssl]])
                        exp_toks[Ssl] = ex
                        pend.append((kt, Ssl, ex))
                    if pend and pend[0][0] <= kt - 2:
                        pkt, pSsl, pex = pend.pop(0)
                        avA = bb.op("pe", lambda E_, kt=pkt, s=pSsl, p=p:
                                    E_.matmul(
                            p_oA[:], vplus[:, kt, 2 * p, 0:65],
                            ptil[s][:, 0:512],
                            start=(kt == 0), stop=(kt == 31)),
                            deps=[pex, war_oA])
                        avB = bb.op("pe", lambda E_, kt=pkt, s=pSsl, p=p:
                                    E_.matmul(
                            p_oB[:], vplus[:, kt, 2 * p + 1, 0:65],
                            ptil[s][:, 512:1024],
                            start=(kt == 0), stop=(kt == 31)),
                            deps=[pex, war_oB])
                        av_toks[pSsl] = avB
                        last_avA, last_avB = avA, avB
                rcA = bb.op("dve", lambda E_: E_.reciprocal(
                    rdenA[:], p_oA[64:65, :]), deps=[last_avA])
                rcB = bb.op("dve", lambda E_: E_.reciprocal(
                    rdenB[:], p_oB[64:65, :]), deps=[last_avB])
                rbA = bb.op("pe", lambda E_: E_.matmul(
                    p_S[0][0:64, 0:512], ar["onesr"][:, 0:64], rdenA[:],
                    start=True, stop=True),
                    deps=[rcA, exp_toks[0], av_toks[0]])
                rbB = bb.op("pe", lambda E_: E_.matmul(
                    p_S[1][0:64, 0:512], ar["onesr"][:, 0:64], rdenB[:],
                    start=True, stop=True),
                    deps=[rcB, exp_toks[1], av_toks[1]])
                nevA = bb.op("act", lambda E_: E_.copy(numsA[:], p_oA[0:64, :]),
                             deps=[last_avA, rcA])
                nevB = bb.op("act", lambda E_: E_.copy(numsB[:], p_oB[0:64, :]),
                             deps=[last_avB, rcB])
                dA = bb.op("dve", lambda E_, p=p: E_.tensor_mul(
                    attnT[0:64, p, :], numsA[:], p_S[0][0:64, 0:512]),
                    deps=[nevA, rbA, war_attnT])
                dB = bb.op("dve", lambda E_, p=p: E_.tensor_mul(
                    attnT[64:128, p, :], numsB[:], p_S[1][0:64, 0:512]),
                    deps=[nevB, rbB])
                exp_toks[0] = dA
                exp_toks[1] = dB
                war_oA, war_oB = dA, dB
                war_k[psl] = dB
            head_done = dB
            stat_first = True
            wo_last = [exp_toks[2], exp_toks[2]]
            for oc in range(CH):
                xls = oc % 2
                wsl = oc % 2
                lx = bb.dma("sp", xsl[xls][:], xrv[:, oc, ts(qt, QW)],
                            f"a0_xr{xls}", deps=[war_xr[xls]])
                mm = None
                for ic in range(CH):
                    mm = bb.op("pe", lambda E_, ic=ic, oc=oc, w=wsl: E_.matmul(
                        p_S[2][:, ts(w, 512)], wo[:, ic, ts(oc, 128)],
                        attnT[:, ic, :],
                        start=(ic == 0), stop=(ic == CH - 1)),
                        deps=[head_done, lwo[ic], wo_last[wsl]])
                ad2 = bb.op("dve", lambda E_, x=xls, w=wsl: E_.tensor_add(
                    nxs[:], xsl[x][:], p_S[2][:, ts(w, 512)]),
                    deps=[mm, lx, war_nxs])
                wo_last[wsl] = ad2
                war_xr[xls] = ad2
                st = bb.dma("sp", ov[:, oc, ts(qt, QW)], nxs[:], "st_x",
                            deps=[ad2])
                war_nxs = st
                if compute_stats_out:
                    stok = emit_out_stats(
                        bb, ar, nxs[:], QW, p_oA[0:1, :], p_oB[0:1, :],
                        stat_first, oc == CH - 1, [ad2, war_oA, war_oB],
                        srow, stats_out, ts(qt, QW), "a0_sst")
                    stat_first = False
                    if oc == CH - 1:
                        war_oA = stok
                        war_oB = stok
            war_attnT = ad2
            exp_toks[2] = ad2
            war_q = ad2
        bb.fence()



def emit_attn1(bb, ar, x_in, x_out, stats_in, stats_out, compute_stats_out):
    nc = bb.nc
    dbg = os.environ.get("DBG5", "")

    with ExitStack() as sc:
        Es = sc.enter_context
        qres = Es(nc.sbuf_tensor(_u("a1_qres"), [128, CH, T], BF16))
        kvk = Es(nc.sbuf_tensor(_u("a1_kvk"), [64, NH, 66], BF16))
        projp = Es(nc.sbuf_tensor(_u("a1_proj"), [128, 64], BF16))
        lp1 = bb.dma("pop", projp[0:64, :], ar["t_L1_attn_proj"], "a1_wld")
        lp = bb.dma("pop", projp[64:128, :], ar["t_L1_attn_proj"], "a1_wld",
                    deps=[lp1])
        m_row, s_row, rows_done = emit_rows_prologue(bb, ar, sc, stats_in)

        with ExitStack() as pa:
            Ep = pa.enter_context
            wq_ = Ep(nc.sbuf_tensor(_u("a1_wq"), [128, CH, D], BF16))
            wk_ = Ep(nc.sbuf_tensor(_u("a1_wk"), [128, CH, D], BF16))
            wv2 = Ep(nc.sbuf_tensor(_u("a1_wv"), [128, CH, D], BF16))
            lws = []
            for wname, wt in (("Wq", wq_), ("Wk", wk_), ("Wv", wv2)):
                wv_ = ar["t_L1_attn_" + wname].rearrange("(c p) n -> p c n", p=128)
                for kc in range(CH):
                    lws.append(bb.dma("pop", wt[:, kc, :], wv_[:, kc, :],
                                      "a1_wld"))
            xt = [Ep(nc.sbuf_tensor(_u("a1_x0"), [128, CH, TT], F32))]
            xt.append(xt[0])
            zt = Ep(nc.sbuf_tensor(_u("a1_z"), [128, CH, TT], BF16))
            ktile = Ep(nc.sbuf_tensor(_u("a1_kt"), [128, CH, TT], BF16))
            vtile = Ep(nc.sbuf_tensor(_u("a1_vt"), [128, 4, NH, 66], BF16))
            kfs = Ep(nc.sbuf_tensor(_u("a1_kfs"), [128, 512], F32))
            kfb = Ep(nc.sbuf_tensor(_u("a1_kfb"), [128, 512], BF16))
            sbb = Ep(nc.sbuf_tensor(_u("a1_sb"), [128, TT], F32))
            mbb = Ep(nc.sbuf_tensor(_u("a1_mb"), [128, TT], F32))
            ar["tscr"] = Ep(nc.sbuf_tensor(_u("a1_tscr"), [128, TT], F32))
            ar["tscr2"] = Ep(nc.sbuf_tensor(_u("a1_tscr2"), [128, TT], F32))
            p_mm2 = [Ep(nc.psum_tensor(_u(f"a1_pmm{i}"), [128, 512], F32))
                     for i in range(2)]
            p_v = Ep(nc.psum_tensor(_u("a1_pv"), [128, 1024], F32))
            p_kf = Ep(nc.psum_tensor(_u("a1_pkf"), [128, 512], F32))
            p_kv = Ep(nc.psum_tensor(_u("a1_pkv"), [64, 512], F32))
            kv_acc = Ep(nc.sbuf_tensor(_u("a1_kvacc"), [64, NH, 66], F32))

            vset = bb.op("dve", lambda E_: E_.memset(vtile[:], 1.0))
            kz = bb.op("dve", lambda E_: E_.memset(kv_acc[:], 0.0))
            xv = x_in.rearrange("(c p) t -> p c t", p=128)
            war_x = [None, None]
            war_z = None
            war_bc = None
            war_mm = None
            war_pv = None
            war_kf = None
            war_kfs = None
            war_kt = None
            kv_last = [None] * NH
            for tt in range(NTT):
                sl = 0
                ld = bb.dma("sp", xt[sl][:], xv[:, :, ts(tt, TT)], "a1_x0",
                            deps=[war_x[sl]])
                bc = emit_bcast2(bb, ar, p_v[:, 0:512], p_v[:, 512:1024],
                                 s_row, m_row, ts(tt, TT),
                                 [rows_done, war_bc, war_pv], sbb, mbb, TT)
                zt_toks = []
                for c in range(CH):
                    zt_toks.append(emit_apply_norm(
                        bb, ar, xt[sl][:, c, :], zt[:, c, :], mbb, sbb, None,
                        deps=[ld, war_z] + bc))
                war_x[sl] = zt_toks[-1]
                war_mm2 = [war_mm, war_mm]
                for di, (dst, wt) in enumerate(((qres, wq_), (ktile, wk_))):
                    for oc in range(CH):
                        pi = (di * CH + oc) % 2
                        mm = None
                        for kc in range(CH):
                            mm = bb.op("pe", lambda E_, kc=kc, oc=oc, w=wt,
                                       pi=pi: E_.matmul(
                                p_mm2[pi][:], w[:, kc, ts(oc, 128)],
                                zt[:, kc, :],
                                start=(kc == 0), stop=(kc == CH - 1)),
                                deps=[zt_toks[kc], war_mm2[pi], lws[-1]])
                        if dst is qres:
                            ev = bb.op("act", lambda E_, oc=oc, t0=tt, pi=pi:
                                       E_.copy(
                                qres[:, oc, ts(t0, TT)], p_mm2[pi][:]),
                                deps=[mm])
                        else:
                            ev = bb.op("act", lambda E_, oc=oc, pi=pi: E_.copy(
                                ktile[:, oc, :], p_mm2[pi][:]),
                                deps=[mm, war_kt])
                        war_mm2[pi] = ev
                war_mm = war_mm2[0] if war_mm2[1] is None else war_mm2[1]
                war_mm = ev
                v_toks = []
                for tsub in range(4):
                    mm = None
                    for kc in range(CH):
                        for j in range(2):
                            mm = bb.op("pe", lambda E_, kc=kc, j=j, t0=tsub:
                                       E_.matmul(
                                p_v[:, ts(j, 512)], zt[:, kc, ts(t0, 128)],
                                wv2[:, kc, ts(j, 512)],
                                start=(kc == 0), stop=(kc == CH - 1)),
                                deps=[zt_toks[kc], war_pv] + bc)
                    ev = bb.op("act", lambda E_, t0=tsub: E_.copy(
                        vtile[:, t0, :, 0:64],
                        p_v[:].rearrange("p (h d) -> p h d", h=NH)),
                        deps=[mm, vset, war_pv])
                    war_pv = ev
                    v_toks.append(ev)
                war_z = war_pv
                war_bc = v_toks[-1]
                pairs = [] if "nokf" in dbg else [
                    (h, tsub) for h in range(NH) for tsub in range(4)]
                for b0 in range(0, len(pairs), 4):
                    batch = pairs[b0:b0 + 4]
                    mk = None
                    for jj, (h, tsub) in enumerate(batch):
                        hb, hc = 64 * (h % 2), h // 2
                        mk = bb.op("pe", lambda E_, jj=jj, h=h, t0=tsub, hb=hb,
                                   hc=hc, nb=len(batch): E_.matmul(
                            p_kf[:, ts(jj, 64)],
                            ktile[hb:hb + 64, hc, ts(t0, 128)],
                            projp[hb:hb + 64, :],
                            start=(jj == 0), stop=(jj == nb - 1)),
                            deps=[war_mm, lp, war_kf])
                    if "norl" in dbg:
                        war_kf = mk
                        war_kfs = mk
                        war_kt = mk
                        continue
                    W = 64 * len(batch)
                    rl_ = bb.op("act", lambda E_, W=W: E_.activation(
                        kfs[:, 0:W], p_kf[:, 0:W], AF.Relu),
                        deps=[mk, war_kfs])
                    if "noep" in dbg:
                        war_kf = rl_
                        war_kfs = rl_
                        war_kt = rl_
                        continue
                    ep_ = bb.op("dve", lambda E_, W=W: E_.tensor_scalar(
                        kfb[:, 0:W], kfs[:, 0:W], 1e-6, None, ALU.add),
                        deps=[rl_, war_kfs])
                    war_kf = ep_
                    mv_toks = []
                    if "nomv" in dbg:
                        war_kfs = ep_
                        war_kt = war_kfs
                        continue
                    bh = batch[0][0]
                    for jj, (h, tsub) in enumerate(batch):
                        mv = bb.op("pe", lambda E_, jj=jj, h=h, t0=tsub,
                                   nb=len(batch): E_.matmul(
                            p_kv[:, 0:65],
                            kfb[:, ts(jj, 64)], vtile[:, t0, h, 0:65],
                            start=(jj == 0), stop=(jj == nb - 1)),
                            deps=[ep_, v_toks[tsub], kv_last[0]])
                        mv_toks.append(mv)
                    acc = bb.op("dve", lambda E_, h=bh: E_.tensor_add(
                        kv_acc[:, h, 0:65], kv_acc[:, h, 0:65],
                        p_kv[0:64, 0:65]), deps=[mv_toks[-1], kz])
                    kv_last[0] = acc
                    war_kfs = acc
                    war_pv = acc
                war_kt = war_kfs
            kv_ev = []
            if "nokf" not in dbg:
                ev = bb.op("dve", lambda E_: E_.tensor_copy(
                    kvk[:], kv_acc[:]), deps=[kv_last[0]])
                kv_ev.append(ev)
            if "dbg_kvk" in ar:
                kvf = Ep(nc.sbuf_tensor(_u("dbg_kvf"), [64, NH * 66], F32))
                ck = bb.op("dve", lambda E_: E_.tensor_copy(
                    kvf[:], kvk[:].rearrange("p a b -> p (a b)")), deps=kv_ev)
                bb.dma("sp", ar["dbg_kvk"], kvf[:], "dbg_s", deps=[ck])
            bb.fence()

        if "aonly" in dbg:
            bb.fence()
            return
        with ExitStack() as pb:
            Ep = pb.enter_context
            wo_ = Ep(nc.sbuf_tensor(_u("a1_wo"), [128, CH, D], BF16))
            lwo = []
            wv_ = ar["t_L1_attn_Wo"].rearrange("(c p) n -> p c n", p=128)
            for ic in range(CH):
                lwo.append(bb.dma("pop", wo_[:, ic, :], wv_[:, ic, :], "a1_wld"))
            bocol = Ep(nc.sbuf_tensor(_u("a1_bo"), [128, CH], F32))
            lbo = bb.dma("sp", bocol[:],
                         ar["t_L1_attn_bo"].rearrange("(c p) -> p c", p=128),
                         "ld_rows", allow_slow_non_contiguous=True)
            attnT = Ep(nc.sbuf_tensor(_u("a1_attnT"), [128, CH, TT], BF16))
            qfs2 = [Ep(nc.sbuf_tensor(_u(f"a1_qfs{i}"), [64, TT], F32))
                    for i in range(2)]
            qfb2 = [Ep(nc.sbuf_tensor(_u(f"a1_qfb{i}"), [64, TT], BF16))
                    for i in range(2)]
            nums = Ep(nc.sbuf_tensor(_u("a1_nums"), [64, TT], F32))
            rd1 = Ep(nc.sbuf_tensor(_u("a1_rd1"), [1, TT], F32))
            rd2 = Ep(nc.sbuf_tensor(_u("a1_rd2"), [1, TT], F32))
            dlt = Ep(nc.sbuf_tensor(_u("a1_dlt"), [128, TT], F32))
            xslb = [Ep(nc.sbuf_tensor(_u(f"a1_xs{i}"), [128, CH, TT], F32))
                    for i in range(2)]
            nxs = [Ep(nc.sbuf_tensor(_u(f"a1_nxs{i}"), [128, CH, TT], F32))
                   for i in range(2)]
            ar["nxb"] = Ep(nc.sbuf_tensor(_u("a1_nxb"), [128, TT], BF16))
            ar["nxq"] = Ep(nc.sbuf_tensor(_u("a1_nxq"), [128, TT], BF16))
            srow = Ep(nc.sbuf_tensor(_u("a1_srow"), [1, 2 * TT], F32))
            p_qf2 = [Ep(nc.psum_tensor(_u(f"a1_pqf{i}"), [64, TT], F32))
                     for i in range(2)]
            p_num2 = [Ep(nc.psum_tensor(_u(f"a1_pnum{i}"), [65, TT], F32))
                      for i in range(2)]
            p_rb = Ep(nc.psum_tensor(_u("a1_prb"), [64, TT], F32))
            p_wo = [Ep(nc.psum_tensor(_u(f"a1_pwo{i}"), [128, 512], F32))
                    for i in range(2)]
            p_st1 = Ep(nc.psum_tensor(_u("a1_pst1"), [128, TT], F32))
            p_sx = p_st1[0:1, :]
            p_sq = p_st1[32:33, :]

            xrv = x_in.rearrange("(c p) t -> p c t", p=128)
            ov = x_out.rearrange("(c p) t -> p c t", p=128)
            war_xs = [None, None]
            war_nxs = [None, None]
            war_qf = None
            war_qfs = None
            war_qfb = None
            war_num2 = [None, None]
            war_d1 = None
            war_rd2 = None
            war_nums = None
            war_rb = None
            war_attnT = None
            war_wo = [None, None]
            war_dlt = None
            def emit_qf(h, qt):
                nonlocal war_qf, war_qfs, war_qfb
                hb, hc, pi = 64 * (h % 2), h // 2, h % 2
                mq = bb.op("pe", lambda E_, hb=hb, hc=hc, q=qt, pi=pi:
                           E_.matmul(
                    p_qf2[pi][:], projp[hb:hb + 64, :],
                    qres[hb:hb + 64, hc, ts(q, TT)],
                    start=True, stop=True), deps=[lp, war_qf[pi]] + kv_ev)
                rl_ = bb.op("act", lambda E_, pi=pi: E_.activation(
                    qfs2[pi][:], p_qf2[pi][:], AF.Relu),
                    deps=[mq, war_qfs[pi]])
                war_qf[pi] = rl_
                ep_ = bb.op("dve", lambda E_, pi=pi: E_.tensor_scalar(
                    qfb2[pi][:], qfs2[pi][:], 1e-6, None, ALU.add),
                    deps=[rl_, war_qfb[pi]])
                war_qfs[pi] = ep_
                return ep_

            war_qf = [None, None]
            war_qfs = [None, None]
            war_qfb = [None, None]
            qf_tok = [None] * NH
            for qt in range(NTT):
                qf_tok[0] = emit_qf(0, qt)
                for h in range(NH):
                    hb, hc, pi = 64 * (h % 2), h // 2, h % 2
                    if h + 1 < NH:
                        qf_tok[h + 1] = emit_qf(h + 1, qt)
                    mn = bb.op("pe", lambda E_, h=h, pi=pi: E_.matmul(
                        p_num2[pi][:], kvk[:, h, 0:65], qfb2[pi][:],
                        start=True, stop=True),
                        deps=[qf_tok[h], war_num2[pi]] + kv_ev)
                    war_qfb[pi] = mn
                    d1 = bb.op("dve", lambda E_, pi=pi: E_.tensor_scalar(
                        rd1[:], p_num2[pi][64:65, :], 1e-6, None, ALU.add),
                        deps=[mn, war_d1])
                    war_d1 = d1
                    d2 = bb.op("dve", lambda E_: E_.reciprocal(rd2[:], rd1[:]),
                               deps=[d1, war_rd2])
                    rb = bb.op("pe", lambda E_: E_.matmul(
                        p_rb[:], ar["onesr"][:, 0:64], rd2[:],
                        start=True, stop=True), deps=[d2, war_rb])
                    war_rd2 = rb
                    nev = bb.op("act", lambda E_, pi=pi: E_.copy(
                        nums[:], p_num2[pi][0:64, :]),
                        deps=[mn, d1, war_nums])
                    war_num2[pi] = nev
                    dm = bb.op("dve", lambda E_, h=h, hb=hb, hc=hc: E_.tensor_mul(
                        attnT[hb:hb + 64, hc, :], nums[:], p_rb[:]),
                        deps=[nev, rb, war_attnT])
                    war_rb = dm
                    war_nums = dm
                head_done = war_rb
                xls = qt % 2
                lx = bb.dma("sp", xslb[xls][:], xrv[:, :, ts(qt, TT)],
                            f"a1_xr{xls}", deps=[war_xs[xls]])
                ep_toks = []
                for oc in range(CH):
                    psl = oc % 2
                    mm = None
                    for ic in range(CH):
                        mm = bb.op("pe", lambda E_, ic=ic, oc=oc, p=psl:
                                   E_.matmul(
                            p_wo[p][:], wo_[:, ic, ts(oc, 128)],
                            attnT[:, ic, :], start=(ic == 0),
                            stop=(ic == CH - 1)),
                            deps=[head_done, lwo[ic], war_wo[psl]])
                    e1 = bb.op("act", lambda E_, oc=oc, p=psl: E_.activation(
                        dlt[:], p_wo[p][:], AF.Identity, bias=bocol[:, oc:oc + 1]),
                        deps=[mm, lbo, war_dlt])
                    e2 = bb.op("pop", lambda E_, oc=oc, x=xls: E_.tensor_add(
                        nxs[x][:, oc, :], xslb[x][:, oc, :], dlt[:]),
                        deps=[e1, lx, war_nxs[xls]])
                    war_wo[psl] = e1
                    war_dlt = e2
                    ep_toks.append(e2)
                    if compute_stats_out:
                        emit_out_stats(bb, ar, nxs[xls][:, oc, :], TT,
                                       p_sx, p_sq, oc == 0, oc == CH - 1,
                                       [e2], srow, stats_out, ts(qt, TT),
                                       "a1_sst", shared=True)
                war_xs[xls] = ep_toks[-1]
                war_attnT = ep_toks[-1]
                st = bb.dma("sp", ov[:, :, ts(qt, TT)], nxs[xls][:], "st_x",
                            deps=ep_toks)
                war_nxs[xls] = st
            bb.fence()



SHAPES = {
    "L0_attn_gamma": [D], "L0_attn_w_qkv": [D, 3 * D], "L0_attn_w_out": [D, D],
    "L0_ffn_weight": [D, D], "L0_ffn_alpha": [1], "L0_ffn_bias": [D],
    "L1_attn_Wq": [D, D], "L1_attn_Wk": [D, D], "L1_attn_Wv": [D, D],
    "L1_attn_proj": [64, 64], "L1_attn_Wo": [D, D], "L1_attn_bo": [D],
    "L1_ffn_W_r": [D, D], "L1_ffn_W_k": [D, D], "L1_ffn_W_v": [D, D],
    "L1_ffn_mu_r": [D], "L1_ffn_mu_k": [D],
}

SUB_WEIGHTS = {
    1: ["L0_ffn_weight", "L0_ffn_alpha", "L0_ffn_bias"],
    2: ["L0_attn_gamma", "L0_attn_w_qkv", "L0_attn_w_out"],
    3: ["L0_ffn_weight", "L0_ffn_alpha", "L0_ffn_bias"],
    4: ["L1_ffn_W_r", "L1_ffn_W_k", "L1_ffn_W_v", "L1_ffn_mu_r", "L1_ffn_mu_k"],
    5: ["L1_attn_Wq", "L1_attn_Wk", "L1_attn_Wv", "L1_attn_proj",
        "L1_attn_Wo", "L1_attn_bo"],
    6: ["L1_ffn_W_r", "L1_ffn_W_k", "L1_ffn_W_v", "L1_ffn_mu_r", "L1_ffn_mu_k"],
}

EMITTERS = {1: emit_ffn0, 2: emit_attn0, 3: emit_ffn0,
            4: emit_rwkv, 5: emit_attn1, 6: emit_rwkv}


def build_group(subs):
    nc = bass.Bass("TRN2", target_bir_lowering=False)
    stack = ExitStack()
    bb = BB(nc, stack)
    ar = {"E": stack.enter_context}

    names = []
    for s in subs:
        for w in SUB_WEIGHTS[s]:
            if w not in names:
                names.append(w)
    for w in names:
        ar["t_" + w] = nc.dram_tensor(w, SHAPES[w], F32, kind="ExternalInput").ap()
    x_in = nc.dram_tensor("x_in", [D, T], F32, kind="ExternalInput").ap()
    x_out = nc.dram_tensor("x_out", [D, T], F32, kind="ExternalOutput").ap()
    xa = nc.dram_tensor("x_buf_a", [D, T], F32, kind="Internal").ap()
    xb = nc.dram_tensor("x_buf_b", [D, T], F32, kind="Internal").ap()
    st_a = nc.dram_tensor("stats_a", [2, T], F32, kind="Internal").ap()
    st_b = nc.dram_tensor("stats_b", [2, T], F32, kind="Internal").ap()
    if 1 in subs or 3 in subs:
        ar["weff_hbm"] = nc.dram_tensor("weff_hbm", [128, CH * D], BF16,
                                        kind="Internal").ap()
        ar["wmeta_hbm"] = nc.dram_tensor("wmeta_hbm", [128, 2], F32,
                                         kind="Internal").ap()
    if 2 in subs:
        ar["q_hbm"] = nc.dram_tensor("q_hbm", [CH, 128, T], BF16,
                                     kind="Internal").ap()
        ar["k_hbm"] = nc.dram_tensor("k_hbm", [CH, 128, T], BF16,
                                     kind="Internal").ap()

    with stack:
        emit_consts(bb, ar)
        n = len(subs)
        bufs = [x_in]
        for i in range(n - 1):
            bufs.append(xa if i % 2 == 0 else xb)
        bufs.append(x_out)
        stats = [st_a, st_b]
        emit_input_stats(bb, ar, x_in, st_a)
        for i, s in enumerate(subs):
            ar.pop("stat_war", None)
            ar.pop("stat_war2", None)
            ar.pop("stat_war3", None)
            if s == 3 and 1 in subs[:i]:
                ar["ffn0_reuse"] = True
            EMITTERS[s](bb, ar, bufs[i], bufs[i + 1], stats[i % 2],
                        stats[(i + 1) % 2], i < n - 1)
        final = [(s, v) for s, v in bb.cnt.items() if s == "st_x"]
        bb.finalize(final)
    return nc, names


_PROGRAM_CACHE = {}


def _get_program(subs):
    key = tuple(subs)
    if key not in _PROGRAM_CACHE:
        _PROGRAM_CACHE[key] = build_group(list(subs))
    return _PROGRAM_CACHE[key]


GROUPS = [[1, 2, 3, 4, 5, 6]]


def kernel(**inputs):
    x = np.ascontiguousarray(inputs["L0_attn_x"], dtype=np.float32)
    cur = [np.ascontiguousarray(x[i].T) for i in range(NCORES)]
    for subs in GROUPS:
        nc, names = _get_program(tuple(subs))
        in_maps = []
        for i in range(NCORES):
            m = {w: np.ascontiguousarray(inputs[w], dtype=np.float32)
                 for w in names}
            m["x_in"] = cur[i]
            in_maps.append(m)
        res = run_bass_kernel_spmd(nc, in_maps, core_ids=list(range(NCORES)))
        cur = [np.asarray(r["x_out"]) for r in res.results]
    out = np.stack([c.T for c in cur], axis=0)
    return np.ascontiguousarray(out, dtype=np.float32)
